# version 13
# baseline (speedup 1.0000x reference)
import numpy as np
import ml_dtypes
import concourse.bass as bass
import concourse.bacc as bacc_mod
import concourse.mybir as mybir
from concourse.bass_utils import run_bass_kernel_spmd

F32 = mybir.dt.float32
BF16 = mybir.dt.bfloat16
I32 = mybir.dt.int32
AF = mybir.ActivationFunctionType
ALU = mybir.AluOpType
AX = mybir.AxisListType
BFNP = ml_dtypes.bfloat16

NCORES = 8
D = 1024
NCH = 8
TOKC = 4096
SEQ = 2048
NB = 2
HID0 = 3072
MCH0 = 24
NHEAD = 16
HDIM = 64
TOPK = 614
EXP4 = 4096
NEXP = 16
LN_EPS = 1e-5
ENGINES = ("sync", "gpsimd", "tensor", "vector", "scalar")


def AFF(*terms, const=0):
    c = const
    pairs = []
    for v, k in terms:
        if isinstance(v, (int, np.integer)):
            c += int(v) * k
        else:
            pairs.append((v, k))
    return (c, pairs)


class Prog:

    def __init__(self, nc, es):
        self.nc = nc
        self.es = es
        self.sems = {}
        for name in ("dw", "dwA", "dwB", "dx", "ds", "pe", "dv", "ac", "pb"):
            self.sems[name] = es.enter_context(nc.semaphore(name))
        self.cnt = {k: 0 for k in self.sems}
        self.cur = None
        self.eng = None
        self.cur_last = None
        self.dry = False
        self.loops = []
        self.wregs = {}
        self.symrr = {}
        self.phases = []

    def _note(self, incs):
        for s, n in incs:
            self.cnt[s] += n

    def ins(self, engine, f, incs=()):
        self._note(incs)
        if self.cur == engine and not self.dry:
            instr = f(self.eng)
            for s, n in incs:
                instr = instr.then_inc(self.sems[s], n)
            self.cur_last = instr
            return instr
        return None

    def _wregs(self):
        if self.cur not in self.wregs:
            r1 = self.es.enter_context(self.eng.register(f"wr1_{self.cur}"))
            r2 = self.es.enter_context(self.eng.register(f"wr2_{self.cur}"))
            self.wregs[self.cur] = (r1, r2)
        return self.wregs[self.cur]

    def wait(self, engine, sem, val, loop=False):
        if self.cur != engine or self.dry:
            return
        if loop and self.loops:
            levels = []
            const = val
            for snap, deltas, it, peel in self.loops:
                d = deltas.get(sem, 0)
                if d:
                    levels.append((it, d))
                    const -= peel * d
            if levels:
                w1, w2 = self._wregs()
                eng = self.eng
                eng.reg_mul(w1, levels[0][0], levels[0][1])
                for itl, dl in levels[1:]:
                    eng.reg_mul(w2, itl, dl)
                    eng.reg_add(w1, w1, w2)
                if const:
                    eng.reg_add(w1, w1, const)
                eng.wait_ge(self.sems[sem], w1)
                return
        self.eng.wait_ge(self.sems[sem], val)

    def emit_loop(self, n, body_fn, peel=1):
        for i in range(min(peel, n)):
            body_fn(i)
        if n <= peel:
            return
        snap = dict(self.cnt)
        was_dry = self.dry
        self.dry = True
        body_fn(peel)
        self.dry = was_dry
        deltas = {s: self.cnt[s] - snap[s] for s in self.cnt
                  if self.cnt[s] != snap[s]}
        self.cnt = dict(snap)
        if self.dry:
            self.cnt = {s: snap[s] + deltas.get(s, 0) * (n - peel)
                        for s in self.cnt}
            return
        depth = len(self.loops)
        key = (self.cur, depth)
        if key not in self.wregs:
            self.wregs[key] = self.es.enter_context(
                self.eng.register(f"li{depth}_{self.cur}"))
        reg = self.wregs[key]
        nc_ = self.nc
        lid = nc_.next_id()
        ls, le = f"ml{lid}s_{self.cur}", f"ml{lid}e_{self.cur}"
        self.eng.reg_mov(reg, peel)
        self.eng.br(ls)
        eng_set = [self.eng.engine]
        with nc_.body(ls, valid_engines=eng_set):
            self.loops.append((snap, deltas, reg, peel))
            body_fn(reg)
            self.loops.pop()
            self.eng.reg_add(reg, reg, 1)
            self.eng.br_lt(reg, n, on_true=ls, on_false=le)
        nc_.switch_bb(le)
        self.cnt = {s: snap[s] + deltas.get(s, 0) * (n - peel)
                    for s in self.cnt}

    def symap(self, engine, base2d, aff, width, rows=None):
        const, pairs = aff
        if rows is not None:
            base2d = base2d[rows[0]:rows[1], :]
        if not pairs:
            return base2d[:, const:const + width]
        if self.cur != engine or self.dry:
            return None
        key = (engine, "symap")
        pool = self.wregs.setdefault(key, [])
        if not pool:
            for i in range(4):
                pool.append(self.es.enter_context(
                    self.eng.register(f"so{i}_{engine}")))
            self.wregs[key] = pool
        rr = self.symrr.get(engine, 0)
        reg = pool[rr % 3]
        tmp = pool[3]
        self.symrr[engine] = rr + 1
        eng = self.eng
        eng.reg_mul(reg, pairs[0][0], pairs[0][1])
        for v, k in pairs[1:]:
            eng.reg_mul(tmp, v, k)
            eng.reg_add(reg, reg, tmp)
        total_c = const + base2d.offset
        if total_c:
            eng.reg_add(reg, reg, total_c)
        pat = [list(base2d.ap[0]), [1, width]]
        return bass.AP(base2d.tensor, reg, pat)

    def symap3(self, engine, base_ap, aff, pat):
        const, pairs = aff
        if not pairs:
            return bass.AP(base_ap.tensor, base_ap.offset + const, pat)
        if self.cur != engine or self.dry:
            return None
        key = (engine, "symap")
        pool = self.wregs.setdefault(key, [])
        if not pool:
            for i in range(4):
                pool.append(self.es.enter_context(
                    self.eng.register(f"so{i}_{engine}")))
            self.wregs[key] = pool
        rr = self.symrr.get(engine, 0)
        reg = pool[rr % 3]
        tmp = pool[3]
        self.symrr[engine] = rr + 1
        eng = self.eng
        eng.reg_mul(reg, pairs[0][0], pairs[0][1])
        for v, k in pairs[1:]:
            eng.reg_mul(tmp, v, k)
            eng.reg_add(reg, reg, tmp)
        total_c = const + base_ap.offset
        if total_c:
            eng.reg_add(reg, reg, total_c)
        return bass.AP(base_ap.tensor, reg, pat)

    ESEM = {"vector": "dv", "scalar": "ac", "tensor": "pe", "gpsimd": "dv"}

    def selfsync(self, engine):
        sem = self.ESEM[engine]
        self.wait(engine, sem, self.cnt[sem], loop=bool(self.loops))

    def barrier(self):
        base = self.cnt["pb"]
        for e in ENGINES:
            self._note((("pb", 1),))
            if self.cur == e:
                if e in self.ESEM:
                    self.wait(e, self.ESEM[e], self.cnt[self.ESEM[e]])
                self.eng.nop().then_inc(self.sems["pb"], 1)
                self.cur_last = None
        for e in ENGINES:
            self.wait(e, "pb", base + len(ENGINES))

    def run(self):
        nc = self.nc
        snap = dict(self.cnt)
        results = []

        def run_engine(name, proxy):
            self.cnt = dict(snap)
            self.cur = name
            self.eng = proxy
            self.cur_last = None
            self.dry = False
            self.loops = []
            for body in self.phases:
                body()
            results.append(dict(self.cnt))

        with nc.Block() as block:
            @block.sync
            def _(e):
                run_engine("sync", e)

            @block.gpsimd
            def _(e):
                run_engine("gpsimd", e)

            @block.tensor
            def _(e):
                run_engine("tensor", e)

            @block.vector
            def _(e):
                run_engine("vector", e)

            @block.scalar
            def _(e):
                run_engine("scalar", e)
        for r in results[1:]:
            assert r == results[0], "nondeterministic phase bodies"


class Arena:

    def __init__(self, nc, es, name, nbytes):
        self.t = es.enter_context(nc.sbuf_tensor(name, [128, nbytes], mybir.dt.uint8))
        self.nbytes = nbytes
        self.off = 0

    def reset(self, off=0):
        self.off = off

    def alloc(self, cols, dtype):
        size = mybir.dt.size(dtype)
        self.off = (self.off + 31) // 32 * 32
        start = self.off
        self.off += cols * size
        assert self.off <= self.nbytes, f"arena overflow: {self.off} > {self.nbytes}"
        full = self.t[:].bitcast(dtype)
        return full[:, start // size: start // size + cols]


def _chunked_vec(v):
    n = v.shape[-1]
    return np.ascontiguousarray(v.reshape(n // 128, 128).T)


def _rotary_table():
    rot, max_freq = 32, 10.0
    dim_exp = rot // 2
    freqs = 2.0 ** np.linspace(0.0, np.log2(max_freq / 2.0), dim_exp)
    pos = 1.0 + np.arange(SEQ, dtype=np.float64)
    rad = pos[:, None] * np.concatenate([freqs, freqs]) * np.pi
    sinr = np.sin(rad)
    cosr = np.cos(rad)
    tab = np.where(np.arange(rot) % 2 == 0, sinr - cosr, sinr + cosr)
    return np.ascontiguousarray(tab.T.astype(np.float32))


def phase_xcopy(p, dst, src, sem="dx", queue="gpsimd"):
    def body():
        for c in range(NCH):
            p.ins(queue, lambda e, c=c: e.dma_start(
                out=dst[c * 128:(c + 1) * 128, :],
                in_=src[c * 128:(c + 1) * 128, :]), incs=((sem, 16),))
        p.wait(queue, sem, p.cnt[sem])
        p.barrier()
    return body


def phase_wcast(p, pairs):
    def body():
        for dst, src in pairs:
            rows = dst.shape[0]
            step = 512
            for r0 in range(0, rows, step):
                r1 = min(rows, r0 + step)
                p.ins("gpsimd", lambda e, r0=r0, r1=r1, dst=dst, src=src: e.dma_start(
                    out=dst[r0:r1, :], in_=src[r0:r1, :]), incs=(("dx", 16),))
        p.wait("gpsimd", "dx", p.cnt["dx"])
        p.barrier()
    return body


class LNBlock:
    def __init__(self, p, ar, T):
        self.p = p
        self.T = T
        nh = T // 512
        self.nh = nh
        self.x2t = ar.alloc(2 * 512, F32)
        self.rows = ar.alloc(4 * nh * 512, F32)
        self.bcin = ar.alloc(2 * nh * 512, F32)


def emit_ln(p, ln, consts, xblk_c, xn_c, ps, lnscale_bias, post=None):
    T, nh = ln.T, ln.nh
    ones1, ones2 = consts["ones1"], consts["ones2"]
    eps_c, neghalf_c = consts["eps"], consts["neghalf"]
    rows = ln.rows
    bcin = ln.bcin

    for j in range(nh):
        for c in range(NCH):
            p.ins("tensor", lambda e, c=c, j=j: e.matmul(
                ps[4][0:1, :], ones1, xblk_c(c)[:, j * 512:(j + 1) * 512],
                start=(c == 0), stop=(c == NCH - 1)),
                incs=(("pe", 1),) if c == NCH - 1 else ())
        p.wait("vector", "pe", p.cnt["pe"])
        p.ins("vector", lambda e, j=j: e.tensor_copy(
            rows[0:1, j * 512:(j + 1) * 512], ps[4][0:1, :]), incs=(("dv", 1),))
        p.wait("tensor", "dv", p.cnt["dv"])
        pe_sq0 = p.cnt["pe"]
        for c in range(NCH):
            if c >= 2:
                p.wait("scalar", "pe", pe_sq0 + c - 1)
            else:
                p.wait("scalar", "pe", pe_sq0)
            ac_before = p.cnt["ac"]
            p.ins("scalar", lambda e, c=c, j=j: e.activation(
                ln.x2t[:, (c % 2) * 512:(c % 2) * 512 + 512],
                xblk_c(c)[:, j * 512:(j + 1) * 512], AF.Square),
                incs=(("ac", 1),))
            p.wait("tensor", "ac", ac_before + 1)
            p.ins("tensor", lambda e, c=c: e.matmul(
                ps[4][0:1, :], ones1, ln.x2t[:, (c % 2) * 512:(c % 2) * 512 + 512],
                start=(c == 0), stop=(c == NCH - 1)),
                incs=(("pe", 1),))
        p.wait("vector", "pe", p.cnt["pe"])
        p.ins("vector", lambda e, j=j: e.tensor_copy(
            rows[0:1, (nh + j) * 512:(nh + j + 1) * 512], ps[4][0:1, :]), incs=(("dv", 1),))
        p.wait("tensor", "dv", p.cnt["dv"])

    mean = rows[0:1, 2 * nh * 512: 3 * nh * 512]
    tmpr = rows[0:1, 3 * nh * 512: 4 * nh * 512]
    sums = rows[0:1, 0: nh * 512]
    sumsq = rows[0:1, nh * 512: 2 * nh * 512]
    rstd = bcin[0:1, 0: nh * 512]
    mr = bcin[0:1, nh * 512: 2 * nh * 512]
    p.selfsync("vector")
    p.ins("vector", lambda e: e.tensor_scalar_mul(mean, sums, 1.0 / D), incs=(("dv", 1),))
    p.ins("vector", lambda e: e.tensor_scalar_mul(tmpr, sumsq, 1.0 / D), incs=(("dv", 1),))
    p.selfsync("vector")
    p.ins("vector", lambda e: e.tensor_tensor(sums, mean, mean, ALU.mult), incs=(("dv", 1),))
    p.selfsync("vector")
    p.ins("vector", lambda e: e.tensor_tensor(tmpr, tmpr, sums, ALU.subtract),
          incs=(("dv", 1),))
    p.wait("scalar", "dv", p.cnt["dv"])
    p.ins("scalar", lambda e: e.activation(tmpr, tmpr, AF.Ln, bias=eps_c[0:1, 0:1]),
          incs=(("ac", 1),))
    p.selfsync("scalar")
    p.ins("scalar", lambda e: e.activation(rstd, tmpr, AF.Exp,
                                           scale=neghalf_c[0:1, 0:1],
                                           bias=lnscale_bias[0:1, 0:1]),
          incs=(("ac", 1),))
    p.wait("vector", "ac", p.cnt["ac"])
    p.ins("vector", lambda e: e.tensor_tensor(mr, mean, rstd, ALU.mult),
          incs=(("dv", 1),))
    p.wait("tensor", "dv", p.cnt["dv"])

    for j in range(nh):
        p.ins("tensor", lambda e, j=j: e.matmul(
            ps[j][:, :], ones2, bcin[0:2, j * 512:(j + 1) * 512],
            start=True, stop=True))
    for j in range(nh):
        p.ins("tensor", lambda e, j=j: e.matmul(
            ps[2 + j][:, :], ones2, bcin[0:2, (nh + j) * 512:(nh + j + 1) * 512],
            start=True, stop=True), incs=(("pe", 1),) if j == nh - 1 else ())
    p.wait("vector", "pe", p.cnt["pe"])

    for c in range(NCH):
        for j in range(nh):
            p.selfsync("vector")
            p.ins("vector", lambda e, c=c, j=j: e.tensor_tensor(
                ln.x2t[:, 0:512], xblk_c(c)[:, j * 512:(j + 1) * 512],
                ps[j][:, :], ALU.mult), incs=(("dv", 1),))
            p.selfsync("vector")
            if post is not None:
                p.wait("vector", "ac", p.cnt["ac"])
            p.ins("vector", lambda e, c=c, j=j: e.tensor_tensor(
                xn_c(c, j), ln.x2t[:, 0:512],
                ps[2 + j][:, :], ALU.subtract),
                incs=(("dv", 1),))
            if post is not None:
                p.wait("scalar", "dv", p.cnt["dv"])
                post(c, j)
    p.wait("tensor", "dv", p.cnt["dv"])
    if post is not None:
        p.wait("tensor", "ac", p.cnt["ac"])


def phase_l0ffn(p, ar_mk, consts, xsc, wg_bf, wu_bf, wd_bf, ps):
    MT = 1024
    NMT = TOKC // MT
    nh = MT // 512

    def body():
        ar = ar_mk()
        xblk = ar.alloc(NCH * MT, F32)
        xn = ar.alloc(NCH * MT, BF16)
        ln = LNBlock(p, ar, MT)
        wgm = ar.alloc(2 * D, BF16)
        wum = ar.alloc(2 * D, BF16)
        wdo = ar.alloc(2 * MCH0 * 128, BF16)
        h_t = ar.alloc(MCH0 * MT, BF16)
        sg = ar.alloc(2 * MT, F32)

        def xblk_c(c):
            return xblk[:, c * MT:(c + 1) * MT]

        def xn_c(c, j):
            return xn[:, c * MT + j * 512: c * MT + (j + 1) * 512]

        p.ins("vector", lambda e: e.memset(ln.bcin[0:2, :], 0.0), incs=(("dv", 1),))

        def wcol(wbf, m):
            return wbf[m * 128:(m + 1) * 128, :]

        for mt in range(NMT):
            t0 = mt * MT
            dx0 = p.cnt["dx"]
            for c in range(NCH):
                p.ins("gpsimd", lambda e, c=c: e.dma_start(
                    out=xblk_c(c), in_=xsc[c * 128:(c + 1) * 128, t0:t0 + MT]),
                    incs=(("dx", 16),))
            for e in ("tensor", "vector", "scalar"):
                p.wait(e, "dx", dx0 + 16 * NCH)

            emit_ln(p, ln, consts, xblk_c, xn_c, ps, consts["ln127"])

            dv_gu0 = p.cnt["dv"]
            for m in range(MCH0):
                if m >= 2:
                    p.wait("sync", "dv", dv_gu0 + (m - 1) * nh)
                slot = "dwA" if m % 2 == 0 else "dwB"
                p.ins("sync", lambda e, m=m: e.dma_start(
                    out=wgm[:, (m % 2) * D:(m % 2) * D + D],
                    in_=wcol(wg_bf, m)), incs=((slot, 16),))
                p.ins("sync", lambda e, m=m: e.dma_start(
                    out=wum[:, (m % 2) * D:(m % 2) * D + D],
                    in_=wcol(wu_bf, m)), incs=((slot, 16),))
                bsel = (m % 2) * 4
                gb = [ps[bsel + 0], ps[bsel + 1]]
                ub = [ps[bsel + 2], ps[bsel + 3]]
                p.wait("tensor", slot, p.cnt[slot])
                if m >= 2:
                    p.wait("tensor", "dv", dv_gu0 + (m - 1) * nh)
                for k in range(NCH):
                    p.ins("tensor", lambda e, k=k, m=m: e.ldweights(
                        wgm[:, (m % 2) * D + k * 128:(m % 2) * D + k * 128 + 128]))
                    for j in range(nh):
                        p.ins("tensor", lambda e, k=k, m=m, j=j: e.matmul(
                            gb[j][:, :],
                            wgm[:, (m % 2) * D + k * 128:(m % 2) * D + k * 128 + 128],
                            xn_c(k, j),
                            start=(k == 0), stop=(k == NCH - 1)))
                    p.ins("tensor", lambda e, k=k, m=m: e.ldweights(
                        wum[:, (m % 2) * D + k * 128:(m % 2) * D + k * 128 + 128]))
                    for j in range(nh):
                        p.ins("tensor", lambda e, k=k, m=m, j=j: e.matmul(
                            ub[j][:, :],
                            wum[:, (m % 2) * D + k * 128:(m % 2) * D + k * 128 + 128],
                            xn_c(k, j),
                            start=(k == 0), stop=(k == NCH - 1)),
                            incs=(("pe", 1),) if (k == NCH - 1 and j == nh - 1) else ())
                p.wait("scalar", "pe", p.cnt["pe"])
                if m >= 2:
                    p.wait("scalar", "dv", dv_gu0 + (m - 1) * nh)
                for j in range(nh):
                    p.ins("scalar", lambda e, m=m, j=j: e.activation(
                        sg[:, (m % 2) * MT + j * 512:(m % 2) * MT + (j + 1) * 512],
                        gb[j][:, :], AF.Silu),
                        incs=(("ac", 1),))
                p.wait("vector", "ac", p.cnt["ac"])
                for j in range(nh):
                    p.ins("vector", lambda e, m=m, j=j: e.tensor_tensor(
                        h_t[:, m * MT + j * 512: m * MT + (j + 1) * 512],
                        sg[:, (m % 2) * MT + j * 512:(m % 2) * MT + (j + 1) * 512],
                        ub[j][:, :], ALU.mult),
                        incs=(("dv", 1),))
            p.wait("tensor", "dv", p.cnt["dv"])

            dv_do0 = p.cnt["dv"]
            for o in range(NCH):
                if o >= 2:
                    p.wait("sync", "dv", dv_do0 + (o - 1) * nh)
                else:
                    p.wait("sync", "dv", dv_do0)
                slot = "dwA" if o % 2 == 0 else "dwB"
                p.ins("sync", lambda e, o=o: e.dma_start(
                    out=wdo[:, (o % 2) * MCH0 * 128:((o % 2) + 1) * MCH0 * 128],
                    in_=wcol(wd_bf, o)), incs=((slot, 16),))
                bsel = (o % 2) * 4
                db = [ps[bsel + 0], ps[bsel + 1]]
                p.wait("tensor", slot, p.cnt[slot])
                if o >= 2:
                    p.wait("tensor", "dv", dv_do0 + (o - 1) * nh)
                for k in range(MCH0):
                    p.ins("tensor", lambda e, k=k, o=o: e.ldweights(
                        wdo[:, (o % 2) * MCH0 * 128 + k * 128:(o % 2) * MCH0 * 128 + k * 128 + 128]))
                    for j in range(nh):
                        p.ins("tensor", lambda e, k=k, o=o, j=j: e.matmul(
                            db[j][:, :],
                            wdo[:, (o % 2) * MCH0 * 128 + k * 128:(o % 2) * MCH0 * 128 + k * 128 + 128],
                            h_t[:, k * MT + j * 512: k * MT + (j + 1) * 512],
                            start=(k == 0), stop=(k == MCH0 - 1)),
                            incs=(("pe", 1),) if (k == MCH0 - 1 and j == nh - 1) else ())
                p.wait("vector", "pe", p.cnt["pe"])
                for j in range(nh):
                    p.ins("vector", lambda e, o=o, j=j: e.scalar_tensor_tensor(
                        xblk_c(o)[:, j * 512:(j + 1) * 512],
                        db[j][:, :], 0.5 / 127.0,
                        xblk_c(o)[:, j * 512:(j + 1) * 512],
                        op0=ALU.mult, op1=ALU.add),
                        incs=(("dv", 1),))
            p.wait("gpsimd", "dv", p.cnt["dv"])
            for c in range(NCH):
                p.ins("gpsimd", lambda e, c=c: e.dma_start(
                    out=xsc[c * 128:(c + 1) * 128, t0:t0 + MT], in_=xblk_c(c)),
                    incs=(("dx", 16),))
            p.wait("gpsimd", "dx", p.cnt["dx"])
            p.barrier()
    return body



QB = 256
NE_SEL = 5
SEL_LAM = (1.0, 1.0, 0.7, 0.5, 0.35, 0.25, 0.2, 0.15)
SEL_Q70 = 0.5244
SEL_DINV = 1.0 / (SEQ * 0.3477)


def phase_attention(p, ar_mk, consts, cfg, ps):
    sel = cfg["sel"]
    stream = cfg["stream"]
    ident = consts["ident"]
    identf = consts["identf"]
    ones1 = consts["ones1"]

    def body():
        ar = ar_mk()
        xn = ar.alloc(NCH * SEQ, BF16)
        ksb = ar.alloc(NCH * SEQ, BF16)
        vt = ar.alloc(16 * 1024, BF16)
        wq_s = ar.alloc(NCH * D, BF16)
        wkv_s = ar.alloc(NCH * D, BF16)
        qsb = ar.alloc(NCH * QB, BF16)
        att_sb = ar.alloc(NCH * QB, BF16)
        bq_t = ar.alloc(NCH, F32)
        bk_t = ar.alloc(NCH, F32)
        bv_t = ar.alloc(NCH, F32)
        bo_t = ar.alloc(NCH, F32) if cfg.get("bo") is not None else None
        lng_t = ar.alloc(NCH, F32) if cfg.get("lng") is not None else None
        lnb_t = ar.alloc(NCH, F32) if cfg.get("lnb") is not None else None
        qtab_t = ar.alloc(SEQ, BF16) if cfg.get("qtab") is not None else None
        ktab_t = ar.alloc(SEQ, BF16) if cfg.get("ktab") is not None else None
        bq_eff = ar.alloc(NCH, F32)
        xblk = ar.alloc(NCH * 512, F32)
        ln = LNBlock(p, ar, 512)
        lntmp = ar.alloc(512, F32)
        s_f = wkv_s.bitcast(F32)[:, 0:2 * SEQ]
        s_bfT = ar.alloc(2 * SEQ, BF16) if sel else None
        s_bf2 = wkv_s.bitcast(BF16)
        e_f = xblk[:, 0:SEQ]
        trash_bf = xblk.bitcast(BF16)[:, 2 * SEQ:3 * SEQ]
        trash2_bf = xblk.bitcast(BF16)[:, 3 * SEQ:4 * SEQ]
        p_t = ar.alloc(2 * SEQ, BF16)
        p_t2 = (p_t, s_bfT if sel else xblk.bitcast(BF16)[:, 2 * SEQ:4 * SEQ])
        pt_t = ar.alloc(2 * QB, BF16)
        wo_s = ar.alloc(2 * D, BF16)
        zb_sb = ar.alloc(QB, F32)
        t4_sb = ar.alloc(128, F32)
        selst = ar.alloc(80, F32)
        zcol = ar.alloc(8, F32)
        vtmp = ar.alloc(2 * 512, BF16)
        ktmp = xblk[:, 3 * 1024:4 * 1024]
        xr = ar.alloc(NCH * QB, F32)


        dw0 = p.cnt["dw"]
        nload = 0
        for nm, t in (("bq", bq_t), ("bk", bk_t), ("bv", bv_t), ("bo", bo_t),
                      ("lng", lng_t), ("lnb", lnb_t), ("qtab", qtab_t),
                      ("ktab", ktab_t)):
            if t is not None:
                p.ins("sync", lambda e, nm=nm, t=t: e.dma_start(out=t, in_=cfg[nm]),
                      incs=(("dw", 16),))
                nload += 1
        for e in ("vector", "scalar", "tensor"):
            p.wait(e, "dw", dw0 + 16 * nload)
        if cfg.get("qscale") is not None:
            p.ins("vector", lambda e: e.memset(consts["qsc"], float(cfg["qscale"])))
            p.ins("vector", lambda e: e.tensor_scalar_mul(
                bq_eff, bq_t, float(cfg["qscale"])), incs=(("dv", 1),))
        else:
            p.ins("vector", lambda e: e.tensor_copy(bq_eff, bq_t), incs=(("dv", 1),))
        p.ins("vector", lambda e: e.memset(ln.bcin[0:2, :], 0.0), incs=(("dv", 1),))
        p.wait("scalar", "dv", p.cnt["dv"])

        for b in range(NB):
            tb = b * SEQ

            for blk in range(4):
                t0 = tb + blk * 512
                dxa = p.cnt["dx"]
                for c in range(NCH):
                    p.ins("gpsimd", lambda e, c=c, t0=t0: e.dma_start(
                        out=xblk[:, c * 512:(c + 1) * 512],
                        in_=stream[c * 128:(c + 1) * 128, t0:t0 + 512]),
                        incs=(("dx", 16),))
                for e in ("tensor", "vector", "scalar"):
                    p.wait(e, "dx", dxa + 16 * NCH)

                def xblk_c(c):
                    return xblk[:, c * 512:(c + 1) * 512]

                if lng_t is None:
                    def xn_dst(c, j, blk=blk):
                        return xn[:, c * SEQ + blk * 512: c * SEQ + blk * 512 + 512]
                    emit_ln(p, ln, consts, xblk_c, xn_dst, ps, consts["zero"])
                else:
                    def xn_dst(c, j):
                        return lntmp[:, 0:512]
                    def post(c, j, blk=blk):
                        p.ins("scalar", lambda e, c=c: e.activation(
                            xn[:, c * SEQ + blk * 512: c * SEQ + blk * 512 + 512],
                            lntmp[:, 0:512], AF.Identity,
                            scale=lng_t[:, c:c + 1], bias=lnb_t[:, c:c + 1]),
                            incs=(("ac", 1),))
                    emit_ln(p, ln, consts, xblk_c, xn_dst, ps, consts["zero"],
                            post=post)
                p.wait("gpsimd", "dv", p.cnt["dv"])

            for wname, btile, isv in (("wv", bv_t, True), ("wk", bk_t, False)):
                p.wait("sync", "dw", p.cnt["dw"])
                dwp = p.cnt["dw"]
                if not isv:
                    p.wait("sync", "pe", p.cnt["pe"])
                p.wait("tensor", "ac", p.cnt["ac"])
                p.wait("tensor", "dv", p.cnt["dv"])
                for m in range(NCH):
                    p.ins("sync", lambda e, m=m, wname=wname: e.dma_start(
                        out=wkv_s[:, m * D:(m + 1) * D],
                        in_=cfg[wname][m * 128:(m + 1) * 128, :]),
                        incs=(("dw", 16),))
                p.wait("tensor", "dw", dwp + 16 * NCH)
                evd = []
                trd = []
                for m in range(NCH):
                    for tq in range(4):
                        si = m * 4 + tq
                        bank = ps[si % 2]
                        if si >= 2:
                            p.wait("tensor", evd[si - 2][0], evd[si - 2][1],
                                   loop=False)
                        for k in range(NCH):
                            if k == 0:
                                p.ins("tensor", lambda e, m=m: e.ldweights(
                                    wkv_s[:, m * D: m * D + 128]))
                            p.ins("tensor", lambda e, m=m, k=k, tq=tq, bank=bank:
                                  e.matmul(
                                      bank[:, :],
                                      wkv_s[:, m * D + k * 128: m * D + k * 128 + 128],
                                      xn[:, k * SEQ + tq * 512: k * SEQ + (tq + 1) * 512],
                                      start=(k == 0), stop=(k == NCH - 1)),
                                  incs=(("pe", 1),) if k == NCH - 1 else ())
                        pe_mm = p.cnt["pe"]
                        p.wait("scalar", "pe", pe_mm)
                        if isv:
                            if si >= 2:
                                p.wait("scalar", "pe", trd[si - 2][0])
                            p.ins("scalar", lambda e, m=m, bank=bank, si=si:
                                  e.activation(
                                      vtmp[:, (si % 2) * 512:(si % 2) * 512 + 512],
                                      bank[:, :], AF.Identity, bias=btile[:, m:m + 1]),
                                  incs=(("ac", 1),))
                            evd.append(("ac", p.cnt["ac"]))
                            p.wait("tensor", "ac", p.cnt["ac"])
                            if si >= 1:
                                p.wait("tensor", "dv", trd[si - 1][1])
                            psT = ps[2][:, :].bitcast(BF16)
                            for i in range(4):
                                p.ins("tensor", lambda e, i=i, si=si: e.transpose(
                                    psT[:, i * 128:(i + 1) * 128],
                                    vtmp[:, (si % 2) * 512 + i * 128:(si % 2) * 512 + (i + 1) * 128],
                                    ident),
                                    incs=(("pe", 1),) if i == 3 else ())
                            trd.append((p.cnt["pe"], None))
                            p.wait("vector", "pe", p.cnt["pe"])
                            for i in range(4):
                                kt = tq * 4 + i
                                p.ins("vector", lambda e, i=i, kt=kt, m=m: e.tensor_copy(
                                    vt[:, kt * 1024 + m * 128: kt * 1024 + (m + 1) * 128],
                                    psT[:, i * 128:(i + 1) * 128]),
                                    incs=(("dv", 1),) if i == 3 else ())
                            trd[si] = (trd[si][0], p.cnt["dv"])
                        else:
                            if ktab_t is None:
                                p.ins("scalar", lambda e, m=m, tq=tq, bank=bank:
                                      e.activation(
                                          ksb[:, m * SEQ + tq * 512: m * SEQ + (tq + 1) * 512],
                                          bank[:, :], AF.Identity, bias=btile[:, m:m + 1]),
                                      incs=(("ac", 1),))
                                evd.append(("ac", p.cnt["ac"]))
                            else:
                                if si >= 2:
                                    p.wait("scalar", "dv", evd[si - 2][1])
                                p.ins("scalar", lambda e, m=m, si=si, bank=bank:
                                      e.activation(
                                          ktmp[:, (si % 2) * 512:(si % 2) * 512 + 512],
                                          bank[:, :], AF.Identity, bias=btile[:, m:m + 1]),
                                      incs=(("ac", 1),))
                                p.wait("vector", "ac", p.cnt["ac"])
                                p.ins("vector", lambda e, m=m, tq=tq, si=si: e.tensor_tensor(
                                    ksb[:, m * SEQ + tq * 512: m * SEQ + (tq + 1) * 512],
                                    ktmp[:, (si % 2) * 512:(si % 2) * 512 + 512],
                                    ktab_t[:, tq * 512:(tq + 1) * 512], ALU.mult),
                                    incs=(("dv", 1),))
                                evd.append(("dv", p.cnt["dv"]))
            p.barrier()

            p.wait("sync", "dw", p.cnt["dw"])
            dwq = p.cnt["dw"]
            for m in range(NCH):
                p.ins("sync", lambda e, m=m: e.dma_start(
                    out=wq_s[:, m * D:(m + 1) * D],
                    in_=cfg["wq"][m * 128:(m + 1) * 128, :]), incs=(("dw", 16),))
            p.wait("tensor", "dw", p.cnt["dw"])

            def qb_body(qb):
                p.selfsync("vector")
                evq = []
                for m in range(NCH):
                    bank = ps[m % 2]
                    if m >= 2:
                        p.wait("tensor", evq[m - 2][0], evq[m - 2][1], loop=True)
                    else:
                        p.wait("tensor", "dv", p.cnt["dv"], loop=True)
                    for k in range(NCH):
                        if k == 0:
                            p.ins("tensor", lambda e, m=m: e.ldweights(
                                wq_s[:, m * D: m * D + 128]))
                        rhs = p.symap("tensor", xn,
                                      AFF((qb, QB), const=k * SEQ), QB)
                        p.ins("tensor", lambda e, m=m, k=k, bank=bank, rhs=rhs:
                              e.matmul(
                                  bank[:, 0:QB],
                                  wq_s[:, m * D + k * 128: m * D + k * 128 + 128],
                                  rhs,
                                  start=(k == 0), stop=(k == NCH - 1)),
                              incs=(("pe", 1),) if k == NCH - 1 else ())
                    p.wait("scalar", "pe", p.cnt["pe"], loop=True)
                    if cfg.get("qscale") is not None:
                        p.ins("scalar", lambda e, m=m, bank=bank: e.activation(
                            qsb[:, m * QB:(m + 1) * QB], bank[:, 0:QB], AF.Identity,
                            scale=consts["qsc"][0:128, 0:1],
                            bias=bq_eff[:, m:m + 1]), incs=(("ac", 1),))
                        evq.append(("ac", p.cnt["ac"]))
                    else:
                        p.wait("scalar", "dv", p.cnt["dv"], loop=True)
                        p.ins("scalar", lambda e, m=m, bank=bank: e.activation(
                            lntmp[:, 0:QB], bank[:, 0:QB], AF.Identity,
                            bias=bq_eff[:, m:m + 1]), incs=(("ac", 1),))
                        p.wait("vector", "ac", p.cnt["ac"], loop=True)
                        qt_ap = p.symap("vector", qtab_t, AFF((qb, QB)), QB)
                        p.ins("vector", lambda e, m=m, qt_ap=qt_ap: e.tensor_tensor(
                            qsb[:, m * QB:(m + 1) * QB], lntmp[:, 0:QB],
                            qt_ap, ALU.mult),
                            incs=(("dv", 1),))
                        evq.append(("dv", p.cnt["dv"]))
                p.wait("gpsimd", "dx", p.cnt["dx"], loop=True)
                p.wait("gpsimd", "dv", p.cnt["dv"], loop=True)
                dxr0 = p.cnt["dx"]
                for c in range(NCH):
                    src_ap = p.symap("gpsimd", stream[c * 128:(c + 1) * 128, :],
                                     AFF((qb, QB), const=tb), QB)
                    p.ins("gpsimd", lambda e, c=c, src_ap=src_ap: e.dma_start(
                        out=xr[:, c * QB:(c + 1) * QB], in_=src_ap),
                        incs=(("dx", 16),))

                p.mark_sf_free = None
                if sel:
                    _attn_sel_pipe(p, consts, cfg, ps, qsb, ksb, vt, s_bf2,
                                   e_f, trash_bf, trash2_bf, p_t2, pt_t,
                                   selst, zcol, ident, identf, t4_sb, zb_sb,
                                   att_sb)
                else:
                    sb3 = (s_bf2[:, 0:2 * SEQ], s_bf2[:, 2 * SEQ:4 * SEQ],
                           xblk.bitcast(BF16)[:, 0:2 * SEQ])
                    _attn_plain_pipe(p, consts, cfg, ps, qsb, ksb, vt, sb3,
                                     p_t2, t4_sb, zb_sb, att_sb)

                evo = []
                for o in range(NCH):
                    slot = "dwA" if o % 2 == 0 else "dwB"
                    if o >= 2:
                        p.wait("sync", evo[o - 2][0], evo[o - 2][1], loop=True)
                    else:
                        p.wait("sync", "dv", p.cnt["dv"], loop=True)
                    p.ins("sync", lambda e, o=o: e.dma_start(
                        out=wo_s[:, (o % 2) * D:(o % 2) * D + D],
                        in_=cfg["wo"][o * 128:(o + 1) * 128, :]),
                        incs=((slot, 16),))
                    bank = ps[o % 2]
                    p.wait("tensor", slot, p.cnt[slot], loop=True)
                    if o < 2:
                        p.wait("tensor", "dv", p.cnt["dv"], loop=True)
                    else:
                        p.wait("tensor", evo[o - 2][0], evo[o - 2][1], loop=True)
                    for k in range(NCH):
                        if k == 0:
                            p.ins("tensor", lambda e, o=o: e.ldweights(
                                wo_s[:, (o % 2) * D: (o % 2) * D + 128]))
                        p.ins("tensor", lambda e, o=o, k=k, bank=bank: e.matmul(
                            bank[:, 0:QB],
                            wo_s[:, (o % 2) * D + k * 128:(o % 2) * D + k * 128 + 128],
                            att_sb[:, k * QB:(k + 1) * QB],
                            start=(k == 0), stop=(k == NCH - 1)),
                            incs=(("pe", 1),) if k == NCH - 1 else ())
                    p.wait("vector", "pe", p.cnt["pe"], loop=True)
                    p.wait("vector", "dx", dxr0 + 16 * NCH, loop=True)
                    if bo_t is not None:
                        p.ins("vector", lambda e, o=o, bank=bank: e.scalar_tensor_tensor(
                            xr[:, o * QB:(o + 1) * QB], bank[:, 0:QB],
                            bo_t[:, o:o + 1], xr[:, o * QB:(o + 1) * QB],
                            op0=ALU.add, op1=ALU.add), incs=(("dv", 1),))
                    else:
                        p.ins("vector", lambda e, o=o, bank=bank: e.tensor_tensor(
                            xr[:, o * QB:(o + 1) * QB], bank[:, 0:QB],
                            xr[:, o * QB:(o + 1) * QB], ALU.add),
                            incs=(("dv", 1),))
                    evo.append(("dv", p.cnt["dv"]))
                p.wait("gpsimd", "dv", p.cnt["dv"], loop=True)
                for c in range(NCH):
                    dst_ap = p.symap("gpsimd", stream[c * 128:(c + 1) * 128, :],
                                     AFF((qb, QB), const=tb), QB)
                    p.ins("gpsimd", lambda e, c=c, dst_ap=dst_ap: e.dma_start(
                        out=dst_ap, in_=xr[:, c * QB:(c + 1) * QB]),
                        incs=(("dx", 16),))

            p.emit_loop(SEQ // QB, qb_body, peel=0)
            p.wait("gpsimd", "dx", p.cnt["dx"])
            p.barrier()
    return body


def _attn_inner(p, consts, cfg, ps, sel, hc, head, qsb, ksb, vt,
                s_f, s_bfT, e_f, trash_bf, trash2_bf, p_t, pt_t, selst, zcol,
                ident):
    KSEL = float(TOPK)
    ev = []
    pe_attv_prev = p.cnt["pe"]
    p.wait("tensor", "ac", p.cnt["ac"], loop=True)
    dv_prev_inst = p.mark_sf_free if (sel and p.mark_sf_free is not None) \
        else p.cnt["dv"]
    for qt in range(2):
        for kb in range(4):
            si = qt * 4 + kb
            bank = ps[si % 2]
            if si >= 2:
                p.wait("tensor", "ac", ev[si - 2], loop=True)
            lhs = qsb[head * 64:(head + 1) * 64,
                      hc * QB + qt * 128: hc * QB + qt * 128 + 128]
            rhs = ksb[head * 64:(head + 1) * 64,
                      hc * SEQ + kb * 512: hc * SEQ + (kb + 1) * 512]
            p.ins("tensor", lambda e, lhs=lhs, head=head: e.ldweights(
                lhs, tile_position=(head * 64, 0)))
            p.ins("tensor", lambda e, lhs=lhs, rhs=rhs, head=head, bank=bank:
                  e.matmul(bank[:, :], lhs, rhs, start=True, stop=True,
                           tile_position=(head * 64, 0)),
                  incs=(("pe", 1),))
            p.wait("scalar", "pe", p.cnt["pe"], loop=True)
            p.selfsync("scalar")
            if si == 0:
                p.wait("scalar", "dv", dv_prev_inst, loop=True)
            p.ins("scalar", lambda e, qt=qt, kb=kb, si=si, bank=bank: e.activation(
                s_f[:, qt * SEQ + kb * 512: qt * SEQ + (kb + 1) * 512],
                bank[:, :], AF.Identity,
                accum_out=selst[:, 24 + si:25 + si]), incs=(("ac", 1),))
            ev.append(p.cnt["ac"])
    p.wait("vector", "ac", p.cnt["ac"], loop=True)

    if sel:
        p.selfsync("vector")
        for qt in range(2):
            p.ins("vector", lambda e, qt=qt: e.tensor_copy(
                s_bfT[:, qt * SEQ:(qt + 1) * SEQ], s_f[:, qt * SEQ:(qt + 1) * SEQ]),
                incs=(("dv", 1),))
        p.mark_sf_free = p.cnt["dv"]
        p.selfsync("vector")
        for qt in range(2):
            p.ins("vector", lambda e, qt=qt: e.reduce_sum(
                selst[:, 16 + qt:17 + qt], selst[:, 24 + qt * 4:28 + qt * 4],
                axis=AX.X), incs=(("dv", 1),))
        p.selfsync("vector")
        p.ins("vector", lambda e: e.tensor_scalar_mul(
            selst[:, 16:18], selst[:, 16:18], 1.0 / SEQ), incs=(("dv", 1),))
        p.selfsync("vector")
        p.ins("vector", lambda e: e.tensor_scalar_mul(
            selst[:, 18:20], selst[:, 16:18], -1.0), incs=(("dv", 1),))
        p.wait("scalar", "dv", p.cnt["dv"], loop=True)
        for qt in range(2):
            p.selfsync("scalar")
            p.ins("scalar", lambda e, qt=qt: e.activation(
                trash2_bf[:, 0:SEQ], s_bfT[:, qt * SEQ:(qt + 1) * SEQ], AF.Abs,
                bias=selst[:, 18 + qt:19 + qt],
                accum_out=selst[:, 32 + qt:33 + qt]), incs=(("ac", 1),))
        p.wait("vector", "ac", p.cnt["ac"], loop=True)
        p.selfsync("vector")
        p.ins("vector", lambda e: e.tensor_scalar_mul(
            selst[:, 20:22], selst[:, 32:34], 1.2533141 / SEQ), incs=(("dv", 1),))
        p.selfsync("vector")
        p.ins("vector", lambda e: e.tensor_scalar_mul(
            selst[:, 22:24], selst[:, 20:22], SEL_DINV), incs=(("dv", 1),))
        p.ins("vector", lambda e: e.scalar_tensor_tensor(
            selst[:, 0:2], selst[:, 20:22], SEL_Q70, selst[:, 16:18],
            op0=ALU.mult, op1=ALU.add), incs=(("dv", 1),))
        for r in range(NE_SEL):
            p.selfsync("vector")
            p.ins("vector", lambda e: e.tensor_scalar_mul(
                selst[:, 38:39], selst[:, 1:2], -1.0), incs=(("dv", 1),))
            p.wait("scalar", "dv", p.cnt["dv"], loop=True)
            p.selfsync("scalar")
            p.ins("scalar", lambda e: e.activation(
                trash2_bf[:, 0:SEQ], s_bfT[:, SEQ:2 * SEQ], AF.Sign,
                bias=selst[:, 38:39],
                accum_out=selst[:, 3:4]), incs=(("ac", 1),))
            p.selfsync("vector")
            p.ins("vector", lambda e: e.tensor_scalar(
                trash_bf[:, 0:SEQ], s_bfT[:, 0:SEQ],
                selst[:, 0:1], 0.0, op0=ALU.is_ge, op1=ALU.add,
                accum_out=selst[:, 2:3]), incs=(("dv", 1),))
            p.selfsync("vector")
            p.ins("vector", lambda e: e.tensor_scalar(
                selst[:, 4:5], selst[:, 2:3], KSEL, None, op0=ALU.subtract),
                incs=(("dv", 1),))
            p.wait("vector", "ac", p.cnt["ac"], loop=True)
            p.ins("vector", lambda e: e.tensor_scalar(
                selst[:, 5:6], selst[:, 3:4], 0.5, SEQ / 2.0 - KSEL,
                op0=ALU.mult, op1=ALU.add), incs=(("dv", 1),))
            p.selfsync("vector")
            if r == 0:
                p.ins("vector", lambda e: e.tensor_copy(
                    selst[:, 6:8], selst[:, 0:2]), incs=(("dv", 1),))
                p.ins("vector", lambda e: e.tensor_tensor(
                    selst[:, 8:10], selst[:, 4:6], selst[:, 4:6], ALU.mult),
                    incs=(("dv", 1),))
            else:
                p.ins("vector", lambda e: e.tensor_tensor(
                    selst[:, 10:12], selst[:, 4:6], selst[:, 4:6], ALU.mult),
                    incs=(("dv", 1),))
                p.selfsync("vector")
                maskv = selst[:, 12:14].bitcast(mybir.dt.uint32)
                p.ins("vector", lambda e, maskv=maskv: e.tensor_tensor(
                    maskv, selst[:, 10:12], selst[:, 8:10], ALU.is_lt),
                    incs=(("dv", 1),))
                p.selfsync("vector")
                p.ins("vector", lambda e, maskv=maskv: e.copy_predicated(
                    selst[:, 6:8], maskv, selst[:, 0:2]),
                    incs=(("dv", 1),))
                p.selfsync("vector")
                p.ins("vector", lambda e, maskv=maskv: e.copy_predicated(
                    selst[:, 8:10], maskv, selst[:, 10:12]),
                    incs=(("dv", 1),))
            if r < NE_SEL - 1:
                p.selfsync("vector")
                p.ins("vector", lambda e: e.tensor_tensor(
                    selst[:, 4:6], selst[:, 4:6], selst[:, 22:24], ALU.mult),
                    incs=(("dv", 1),))
                p.selfsync("vector")
                p.ins("vector", lambda e, r=r: e.scalar_tensor_tensor(
                    selst[:, 0:2], selst[:, 4:6], SEL_LAM[r], selst[:, 0:2],
                    op0=ALU.mult, op1=ALU.add), incs=(("dv", 1),))
        p.selfsync("vector")
        p.ins("vector", lambda e: e.tensor_scalar_mul(
            selst[:, 14:16], selst[:, 6:8], -1.0), incs=(("dv", 1),))
        p.wait("scalar", "dv", p.cnt["dv"], loop=True)
        for qt in range(2):
            p.selfsync("scalar")
            p.ins("scalar", lambda e, qt=qt: e.activation(
                e_f[:, 0:SEQ], s_bfT[:, qt * SEQ:(qt + 1) * SEQ], AF.Exp,
                bias=selst[:, 14 + qt:15 + qt]), incs=(("ac", 1),))
            p.wait("vector", "ac", p.cnt["ac"], loop=True)
            p.wait("vector", "pe", pe_attv_prev, loop=True)
            p.ins("vector", lambda e, qt=qt, head=head: e.scalar_tensor_tensor(
                p_t[:, qt * SEQ:(qt + 1) * SEQ], s_bfT[:, qt * SEQ:(qt + 1) * SEQ],
                selst[:, 6 + qt:7 + qt], e_f[:, 0:SEQ],
                op0=ALU.is_ge, op1=ALU.mult,
                accum_out=zcol[:, head * 2 + qt:head * 2 + qt + 1]),
                incs=(("dv", 1),))
            p.wait("scalar", "dv", p.cnt["dv"], loop=True)
    else:
        p.selfsync("vector")
        for qt in range(2):
            p.ins("vector", lambda e, qt=qt: e.reduce_sum(
                selst[:, 16 + qt:17 + qt], selst[:, 24 + qt * 4:28 + qt * 4],
                axis=AX.X), incs=(("dv", 1),))
        p.selfsync("vector")
        p.ins("vector", lambda e: e.tensor_scalar_mul(
            selst[:, 14:16], selst[:, 16:18], -1.0 / SEQ), incs=(("dv", 1),))
        p.wait("scalar", "dv", p.cnt["dv"], loop=True)
        for qt in range(2):
            p.selfsync("scalar")
            p.wait("scalar", "pe", pe_attv_prev, loop=True)
            p.ins("scalar", lambda e, qt=qt, head=head: e.activation(
                p_t[:, qt * SEQ:(qt + 1) * SEQ], s_f[:, qt * SEQ:(qt + 1) * SEQ],
                AF.Exp, bias=selst[:, 14 + qt:15 + qt],
                accum_out=zcol[:, head * 2 + qt:head * 2 + qt + 1]),
                incs=(("ac", 1),))
        p.wait("vector", "ac", p.cnt["ac"], loop=True)
        p.ins("vector", lambda e: e.nop(), incs=(("dv", 1),))

    dv_p = p.cnt["dv"]
    p.wait("tensor", "dv", dv_p, loop=True)

    def emit_tr(kt):
        ktl = kt % 2
        psT = ps[2 + ktl][:, :].bitcast(BF16)
        for qt in range(2):
            p.ins("tensor", lambda e, qt=qt, kt=kt, psT=psT: e.transpose(
                psT[:, qt * 128:(qt + 1) * 128],
                p_t[:, qt * SEQ + kt * 128: qt * SEQ + (kt + 1) * 128], ident),
                incs=(("pe", 1),) if qt == 1 else ())
        trpe.append(p.cnt["pe"])

    trpe = []
    cpq = []
    emit_tr(0)
    for kt in range(16):
        ktl = kt % 2
        psT = ps[2 + ktl][:, :].bitcast(BF16)
        p.wait("vector", "pe", trpe[kt], loop=True)
        p.selfsync("vector")
        p.ins("vector", lambda e, ktl=ktl, psT=psT: e.tensor_copy(
            pt_t[:, ktl * QB:(ktl + 1) * QB], psT[:, 0:QB]),
            incs=(("dv", 1),))
        cpq.append(p.cnt["dv"])
        if kt < 15:
            emit_tr(kt + 1)
        p.wait("tensor", "dv", cpq[kt], loop=True)
        vtl = vt[:, kt * 1024 + hc * 128 + head * 64:
                 kt * 1024 + hc * 128 + head * 64 + 64]
        p.ins("tensor", lambda e, vtl=vtl, head=head: e.ldweights(
            vtl, tile_position=(0, head * 64)))
        p.ins("tensor", lambda e, vtl=vtl, ktl=ktl, head=head, kt=kt: e.matmul(
            ps[4][head * 64:(head + 1) * 64, 0:QB],
            vtl, pt_t[:, ktl * QB:(ktl + 1) * QB],
            start=(kt == 0), stop=(kt == 15),
            tile_position=(0, head * 64)),
            incs=(("pe", 1),))


def _attn_sel_pipe(p, consts, cfg, ps, qsb, ksb, vt, s_bf2, e_f, trash_bf,
                   trash2_bf, p_t2, pt_t, selst, zcol, ident, identf,
                   t4_sb, zb_sb, att_sb):
    KSEL = float(TOPK)
    NI = 16
    evt = {}
    ev8 = {}
    mmd = {}
    muad = {}
    absd = {}
    updd = {}
    prepd = {}
    sttd0 = {}
    find = {}
    attvd = {}
    pe0 = p.cnt["pe"]
    dv0 = p.cnt["dv"]

    def sbank(i):
        b = (i % 2) * 2 * SEQ
        return s_bf2[:, b:b + 2 * SEQ]

    def mm1(i, si):
        hc, head = i // 2, i % 2
        qt, kb = si // 4, si % 4
        bank = ps[si % 2]
        if si >= 2:
            p.wait("tensor", "ac", evt[(i, si - 2)], loop=True)
        elif i >= 1:
            p.wait("tensor", "ac", evt[(i - 1, si + 6)], loop=True)
        else:
            p.wait("tensor", "ac", p.cnt["ac"], loop=True)
        lhs = qsb[head * 64:(head + 1) * 64,
                  hc * QB + qt * 128: hc * QB + qt * 128 + 128]
        rhs = ksb[head * 64:(head + 1) * 64,
                  hc * SEQ + kb * 512: hc * SEQ + (kb + 1) * 512]
        p.ins("tensor", lambda e, lhs=lhs, head=head: e.ldweights(
            lhs, tile_position=(head * 64, 0)))
        p.ins("tensor", lambda e, lhs=lhs, rhs=rhs, head=head, bank=bank:
              e.matmul(bank[:, :], lhs, rhs, start=True, stop=True,
                       tile_position=(head * 64, 0)), incs=(("pe", 1),))
        mmd[(i, si)] = p.cnt["pe"]

    def evac1(i, si):
        sb = 40 * (i % 2)
        qt, kb = si // 4, si % 4
        bank = ps[si % 2]
        p.wait("scalar", "pe", mmd[(i, si)], loop=True)
        p.selfsync("scalar")
        if si == 0:
            if i >= 2:
                p.wait("scalar", "dv", find[i - 2], loop=True)
            else:
                p.wait("scalar", "dv", dv0, loop=True)
        sbk = sbank(i)
        p.ins("scalar", lambda e, qt=qt, kb=kb, si=si, bank=bank, sb=sb,
              sbk=sbk: e.activation(
                  sbk[:, qt * SEQ + kb * 512: qt * SEQ + (kb + 1) * 512],
                  bank[:, :], AF.Identity,
                  accum_out=selst[:, sb + 24 + si: sb + 25 + si]),
              incs=(("ac", 1),))
        evt[(i, si)] = p.cnt["ac"]
        if si == 7:
            ev8[i] = p.cnt["ac"]

    def sevac_a(i):
        mm1(i, 0)
        mm1(i, 1)
        evac1(i, 0)
        mm1(i, 2)
        evac1(i, 1)
        mm1(i, 3)
        evac1(i, 2)

    def sevac_b(i):
        mm1(i, 4)
        evac1(i, 3)
        mm1(i, 5)
        evac1(i, 4)
        mm1(i, 6)
        evac1(i, 5)
        mm1(i, 7)
        evac1(i, 6)
        evac1(i, 7)

    def muA(i):
        sb = 40 * (i % 2)
        p.wait("vector", "ac", ev8[i], loop=True)
        p.selfsync("vector")
        for qt in range(2):
            p.ins("vector", lambda e, qt=qt, sb=sb: e.reduce_sum(
                selst[:, sb + 16 + qt: sb + 17 + qt],
                selst[:, sb + 24 + qt * 4: sb + 28 + qt * 4],
                axis=AX.X), incs=(("dv", 1),))
        p.selfsync("vector")
        p.ins("vector", lambda e, sb=sb: e.tensor_scalar_mul(
            selst[:, sb + 16: sb + 18], selst[:, sb + 16: sb + 18],
            1.0 / SEQ), incs=(("dv", 1),))
        p.selfsync("vector")
        p.ins("vector", lambda e, sb=sb: e.tensor_scalar_mul(
            selst[:, sb + 18: sb + 20], selst[:, sb + 16: sb + 18], -1.0),
            incs=(("dv", 1),))
        muad[i] = p.cnt["dv"]

    def absA(i):
        sb = 40 * (i % 2)
        sbk = sbank(i)
        p.wait("scalar", "dv", muad[i], loop=True)
        for qt in range(2):
            p.selfsync("scalar")
            p.ins("scalar", lambda e, qt=qt, sb=sb, sbk=sbk: e.activation(
                trash2_bf[:, 0:SEQ // 2], sbk[:, qt * SEQ: qt * SEQ + SEQ // 2],
                AF.Abs, bias=selst[:, sb + 18 + qt: sb + 19 + qt],
                accum_out=selst[:, sb + 32 + qt: sb + 33 + qt]),
                incs=(("ac", 1),))
        absd[i] = p.cnt["ac"]

    def sigt(i):
        sb = 40 * (i % 2)
        p.wait("vector", "ac", absd[i], loop=True)
        p.selfsync("vector")
        p.ins("vector", lambda e, sb=sb: e.tensor_scalar_mul(
            selst[:, sb + 20: sb + 22], selst[:, sb + 32: sb + 34],
            2.0 * 1.2533141 / SEQ), incs=(("dv", 1),))
        p.selfsync("vector")
        p.ins("vector", lambda e, sb=sb: e.tensor_scalar_mul(
            selst[:, sb + 22: sb + 24], selst[:, sb + 20: sb + 22], SEL_DINV),
            incs=(("dv", 1),))
        p.ins("vector", lambda e, sb=sb: e.scalar_tensor_tensor(
            selst[:, sb + 0: sb + 2], selst[:, sb + 20: sb + 22], SEL_Q70,
            selst[:, sb + 16: sb + 18], op0=ALU.mult, op1=ALU.add),
            incs=(("dv", 1),))
        updd[i] = p.cnt["dv"]

    def rnd(i, r):
        sb = 40 * (i % 2)
        sbk = sbank(i)
        W = SEQ // 2 if r < 2 else SEQ
        p.wait("scalar", "dv", updd[i], loop=True)
        p.selfsync("scalar")
        p.ins("scalar", lambda e, sb=sb, sbk=sbk, W=W: e.activation(
            trash2_bf[:, 0:W], sbk[:, SEQ:SEQ + W], AF.Sign,
            scale=consts["negone"][0:128, 0:1],
            bias=selst[:, sb + 1: sb + 2],
            accum_out=selst[:, sb + 3: sb + 4]), incs=(("ac", 1),))
        p.selfsync("vector")
        p.ins("vector", lambda e, sb=sb, sbk=sbk, W=W: e.tensor_scalar(
            trash_bf[:, 0:W], sbk[:, 0:W],
            selst[:, sb + 0: sb + 1], 0.0, op0=ALU.is_ge, op1=ALU.add,
            accum_out=selst[:, sb + 2: sb + 3]), incs=(("dv", 1),))
        p.selfsync("vector")
        if r < 2:
            p.ins("vector", lambda e, sb=sb: e.tensor_scalar(
                selst[:, sb + 4: sb + 5], selst[:, sb + 2: sb + 3], 2.0,
                -KSEL, op0=ALU.mult, op1=ALU.add), incs=(("dv", 1),))
        else:
            p.ins("vector", lambda e, sb=sb: e.tensor_scalar(
                selst[:, sb + 4: sb + 5], selst[:, sb + 2: sb + 3], KSEL,
                None, op0=ALU.subtract), incs=(("dv", 1),))
        p.wait("vector", "ac", p.cnt["ac"], loop=True)
        p.ins("vector", lambda e, sb=sb, W=W: e.tensor_scalar(
            selst[:, sb + 5: sb + 6], selst[:, sb + 3: sb + 4],
            -0.5 * SEQ / W, SEQ / 2.0 - KSEL, op0=ALU.mult, op1=ALU.add),
            incs=(("dv", 1),))
        p.selfsync("vector")
        if r < 2:
            pass
        elif r == 2:
            p.ins("vector", lambda e, sb=sb: e.tensor_copy(
                selst[:, sb + 6: sb + 8], selst[:, sb + 0: sb + 2]),
                incs=(("dv", 1),))
            p.ins("vector", lambda e, sb=sb: e.tensor_tensor(
                selst[:, sb + 8: sb + 10], selst[:, sb + 4: sb + 6],
                selst[:, sb + 4: sb + 6], ALU.mult), incs=(("dv", 1),))
        else:
            p.ins("vector", lambda e, sb=sb: e.tensor_tensor(
                selst[:, sb + 10: sb + 12], selst[:, sb + 4: sb + 6],
                selst[:, sb + 4: sb + 6], ALU.mult), incs=(("dv", 1),))
            p.selfsync("vector")
            maskv = selst[:, sb + 12: sb + 14].bitcast(mybir.dt.uint32)
            p.ins("vector", lambda e, maskv=maskv, sb=sb: e.tensor_tensor(
                maskv, selst[:, sb + 10: sb + 12], selst[:, sb + 8: sb + 10],
                ALU.is_lt), incs=(("dv", 1),))
            p.selfsync("vector")
            p.ins("vector", lambda e, maskv=maskv, sb=sb: e.copy_predicated(
                selst[:, sb + 6: sb + 8], maskv, selst[:, sb + 0: sb + 2]),
                incs=(("dv", 1),))
            p.selfsync("vector")
            p.ins("vector", lambda e, maskv=maskv, sb=sb: e.copy_predicated(
                selst[:, sb + 8: sb + 10], maskv,
                selst[:, sb + 10: sb + 12]), incs=(("dv", 1),))
        if r < NE_SEL - 1:
            p.selfsync("vector")
            p.ins("vector", lambda e, sb=sb: e.tensor_tensor(
                selst[:, sb + 4: sb + 6], selst[:, sb + 4: sb + 6],
                selst[:, sb + 22: sb + 24], ALU.mult), incs=(("dv", 1),))
            p.selfsync("vector")
            p.ins("vector", lambda e, r=r, sb=sb: e.scalar_tensor_tensor(
                selst[:, sb + 0: sb + 2], selst[:, sb + 4: sb + 6],
                SEL_LAM[r], selst[:, sb + 0: sb + 2],
                op0=ALU.mult, op1=ALU.add), incs=(("dv", 1),))
        updd[i] = p.cnt["dv"]

    def finals_prep(i):
        sb = 40 * (i % 2)
        p.selfsync("vector")
        p.ins("vector", lambda e, sb=sb: e.tensor_scalar_mul(
            selst[:, sb + 14: sb + 16], selst[:, sb + 6: sb + 8], -1.0),
            incs=(("dv", 1),))
        prepd[i] = p.cnt["dv"]

    def finals_qt(i, qt):
        sb = 40 * (i % 2)
        sbk = sbank(i)
        pt = p_t2[i % 2]
        head = i % 2
        p.wait("scalar", "dv", prepd[i] if qt == 0 else sttd0[i], loop=True)
        p.selfsync("scalar")
        p.ins("scalar", lambda e, qt=qt, sb=sb, sbk=sbk: e.activation(
            e_f[:, 0:SEQ], sbk[:, qt * SEQ:(qt + 1) * SEQ], AF.Exp,
            bias=selst[:, sb + 14 + qt: sb + 15 + qt]), incs=(("ac", 1),))
        p.wait("vector", "ac", p.cnt["ac"], loop=True)
        p.wait("vector", "pe", attvd.get(i - 2, pe0), loop=True)
        p.selfsync("vector")
        p.ins("vector", lambda e, qt=qt, head=head, sb=sb, sbk=sbk, pt=pt:
              e.scalar_tensor_tensor(
                  pt[:, qt * SEQ:(qt + 1) * SEQ],
                  sbk[:, qt * SEQ:(qt + 1) * SEQ],
                  selst[:, sb + 6 + qt: sb + 7 + qt], e_f[:, 0:SEQ],
                  op0=ALU.is_ge, op1=ALU.mult,
                  accum_out=zcol[:, head * 2 + qt: head * 2 + qt + 1]),
              incs=(("dv", 1),))
        if qt == 0:
            sttd0[i] = p.cnt["dv"]
        else:
            find[i] = p.cnt["dv"]

    def attv(i):
        hc, head = i // 2, i % 2
        pt = p_t2[i % 2]
        dv_p = p.cnt["dv"]
        p.wait("tensor", "dv", dv_p, loop=True)
        trpe = []
        cpq = []

        def emit_tr(k):
            psT = ps[2 + k % 2][:, :].bitcast(BF16)
            for kh in range(2):
                kt = 2 * k + kh
                for qt in range(2):
                    p.ins("tensor", lambda e, qt=qt, kt=kt, kh=kh, psT=psT,
                          pt=pt: e.transpose(
                              psT[:, kh * QB + qt * 128: kh * QB + (qt + 1) * 128],
                              pt[:, qt * SEQ + kt * 128: qt * SEQ + (kt + 1) * 128],
                              ident), incs=(("pe", 1),)
                          if (kh == 1 and qt == 1) else ())
            trpe.append(p.cnt["pe"])

        emit_tr(0)
        mmk = []
        for k in range(8):
            psT = ps[2 + k % 2][:, :].bitcast(BF16)
            p.wait("vector", "pe", mmk[k - 1] if k >= 1 else trpe[0],
                   loop=True)
            p.selfsync("vector")
            p.ins("vector", lambda e, psT=psT: e.tensor_copy(
                pt_t[:, 0:2 * QB], psT[:, 0:2 * QB]),
                incs=(("dv", 1),))
            cpq.append(p.cnt["dv"])
            if k < 7:
                emit_tr(k + 1)
            p.wait("tensor", "dv", cpq[k], loop=True)
            for kh in range(2):
                kt = 2 * k + kh
                vtl = vt[:, kt * 1024 + hc * 128 + head * 64:
                         kt * 1024 + hc * 128 + head * 64 + 64]
                p.ins("tensor", lambda e, vtl=vtl, head=head: e.ldweights(
                    vtl, tile_position=(0, head * 64)))
                p.ins("tensor", lambda e, vtl=vtl, kh=kh, head=head, kt=kt:
                      e.matmul(
                          ps[4][head * 64:(head + 1) * 64, 0:QB],
                          vtl, pt_t[:, kh * QB:(kh + 1) * QB],
                          start=(kt == 0), stop=(kt == 15),
                          tile_position=(0, head * 64)),
                      incs=(("pe", 1),))
            mmk.append(p.cnt["pe"])
        attvd[i] = p.cnt["pe"]

    sevac_a(0)
    sevac_b(0)
    muA(0)
    absA(0)
    sigt(0)
    for i in range(NI):
        rnd(i, 0)
        if i >= 1:
            finals_qt(i - 1, 0)
        rnd(i, 1)
        if i >= 1:
            finals_qt(i - 1, 1)
        rnd(i, 2)
        if i + 1 < NI:
            sevac_a(i + 1)
        if i >= 1:
            attv(i - 1)
            if (i - 1) % 2 == 1:
                _zb_and_evac(p, consts, ps, (i - 1) // 2, zcol, t4_sb,
                             zb_sb, att_sb, identf)
        rnd(i, 3)
        if i + 1 < NI:
            sevac_b(i + 1)
            muA(i + 1)
        rnd(i, 4)
        if i + 1 < NI:
            absA(i + 1)
        finals_prep(i)
        if i + 1 < NI:
            sigt(i + 1)
    finals_qt(NI - 1, 0)
    finals_qt(NI - 1, 1)
    attv(NI - 1)
    _zb_and_evac(p, consts, ps, (NI - 1) // 2, zcol, t4_sb, zb_sb, att_sb,
                 identf)


def _attn_plain_pipe(p, consts, cfg, ps, qsb, ksb, vt, sb3, p_t2, t4_sb,
                     zb_sb, att_sb):
    NHC = NCH
    evt = {}
    mmd = {}
    expi = {}
    expall = {}
    attvd = {}
    ttd = {}
    zbed = {}
    pe0 = p.cnt["pe"]
    ac0 = p.cnt["ac"]
    dv0 = p.cnt["dv"]
    zr = t4_sb.bitcast(BF16)

    p.selfsync("vector")
    p.ins("vector", lambda e: e.memset(zr[:, :], 0.0), incs=(("dv", 1),))

    def sbank(hc, h):
        return sb3[(hc * 2 + h) % 3]

    def mmS(hc, h, kt):
        bank = ps[h * 2 + (kt // 2) % 2]
        col = (kt % 2) * QB
        kk = kt // 2
        if kk >= 2:
            sem, cnt = evt[(hc, h, kk - 2)]
            p.wait("tensor", sem, cnt, loop=True)
        elif hc >= 1:
            sem, cnt = evt[(hc - 1, h, 6 + kk)]
            p.wait("tensor", sem, cnt, loop=True)
        else:
            p.wait("tensor", "ac", p.cnt["ac"], loop=True)
            p.wait("tensor", "dv", p.cnt["dv"], loop=True)
        lhs = ksb[h * 64:(h + 1) * 64,
                  hc * SEQ + kt * 128: hc * SEQ + (kt + 1) * 128]
        rhs = qsb[h * 64:(h + 1) * 64, hc * QB: hc * QB + QB]
        p.ins("tensor", lambda e, lhs=lhs, h=h: e.ldweights(
            lhs, tile_position=(h * 64, 0)))
        p.ins("tensor", lambda e, lhs=lhs, rhs=rhs, h=h, bank=bank, col=col:
              e.matmul(bank[:, col:col + QB], lhs, rhs, start=True, stop=True,
                       tile_position=(h * 64, 0)), incs=(("pe", 1),))
        mmd[(hc, h, kt)] = p.cnt["pe"]

    def evac(hc, h, k):
        bank = ps[h * 2 + k % 2]
        sbk = sbank(hc, h)
        use_act = ((h * 8 + k) % 3 == 2)
        eng = "scalar" if use_act else "vector"
        p.wait(eng, "pe", mmd[(hc, h, k * 2 + 1)], loop=True)
        p.selfsync(eng)
        if k == 0:
            j = hc * 2 + h - 3
            tgt = expi[(j // 2, j % 2)] if j >= 0 else ac0
            p.wait("scalar", "ac", tgt, loop=True)
            p.wait("vector", "ac", tgt, loop=True)
        if use_act:
            p.ins("scalar", lambda e, k=k, bank=bank, sbk=sbk: e.activation(
                sbk[:, k * 512:(k + 1) * 512], bank[:, :], AF.Identity),
                incs=(("ac", 1),))
            evt[(hc, h, k)] = ("ac", p.cnt["ac"])
        else:
            p.ins("vector", lambda e, k=k, bank=bank, sbk=sbk: e.tensor_copy(
                sbk[:, k * 512:(k + 1) * 512], bank[:, :]),
                incs=(("dv", 1),))
            evt[(hc, h, k)] = ("dv", p.cnt["dv"])

    def sev(hc):
        for k in range(8):
            mmS(hc, 0, 2 * k)
            mmS(hc, 1, 2 * k)
            mmS(hc, 0, 2 * k + 1)
            mmS(hc, 1, 2 * k + 1)
            evac(hc, 0, k)
            evac(hc, 1, k)

    def exps(hc):
        p.wait("scalar", "pe", attvd.get(hc - 1, pe0), loop=True)
        for h in range(2):
            sbk = sbank(hc, h)
            Eb = p_t2[h]
            for k in range(8):
                sem, cnt = evt[(hc, h, k)]
                if sem == "dv":
                    p.wait("scalar", "dv", cnt, loop=True)
                p.selfsync("scalar")
                p.ins("scalar", lambda e, k=k, sbk=sbk, Eb=Eb: e.activation(
                    Eb[:, k * 512:(k + 1) * 512],
                    sbk[:, k * 512:(k + 1) * 512], AF.Exp),
                    incs=(("ac", 1),))
            expi[(hc, h)] = p.cnt["ac"]
        expall[hc] = p.cnt["ac"]

    def attvz(hc):
        p.wait("tensor", "dv", ttd.get(hc - 1, dv0), loop=True)
        p.wait("tensor", "ac", zbed.get(hc - 1, ac0), loop=True)
        p.wait("tensor", "ac", expall[hc], loop=True)
        for kt in range(16):
            for h in range(2):
                vtl = vt[:, kt * 1024 + hc * 128 + h * 64:
                         kt * 1024 + hc * 128 + h * 64 + 64]
                Eb = p_t2[h]
                out = ps[4 + h][h * 64:(h + 1) * 64, 0:QB]
                p.ins("tensor", lambda e, vtl=vtl, h=h: e.ldweights(
                    vtl, tile_position=(0, h * 64)))
                p.ins("tensor", lambda e, vtl=vtl, h=h, kt=kt, Eb=Eb, out=out:
                      e.matmul(out, vtl, Eb[:, kt * QB:(kt + 1) * QB],
                               start=(kt == 0), stop=(kt == 15),
                               tile_position=(0, h * 64)),
                      incs=(("pe", 1),))
            for h in range(2):
                Eb = p_t2[h]
                zout = ps[6 + h][h * 64:h * 64 + 64, 0:QB]
                p.ins("tensor", lambda e, h=h: e.ldweights(
                    consts["onesz"], tile_position=(0, h * 64)))
                p.ins("tensor", lambda e, h=h, kt=kt, Eb=Eb, zout=zout:
                      e.matmul(zout, consts["onesz"],
                               Eb[:, kt * QB:(kt + 1) * QB],
                               start=(kt == 0), stop=(kt == 15),
                               tile_position=(0, h * 64)),
                      incs=(("pe", 1),))
        attvd[hc] = p.cnt["pe"]

    def zb(hc):
        p.wait("vector", "pe", attvd[hc], loop=True)
        p.selfsync("vector")

        def _recip(e, out, in_):
            with p.nc.allow_low_precision(reason="bf16 Zinv broadcast"):
                return e.reciprocal(out, in_)
        p.ins("vector", lambda e: _recip(
            e, zr[0:1, 0:QB], ps[6][0:1, 0:QB]), incs=(("dv", 1),))
        p.ins("vector", lambda e: _recip(
            e, zr[64:65, 0:QB], ps[7][64:65, 0:QB]), incs=(("dv", 1),))
        p.wait("tensor", "dv", p.cnt["dv"], loop=True)
        p.ins("tensor", lambda e: e.matmul(
            ps[6][0:64, QB:2 * QB], consts["oneb0"], zr[:, 0:QB],
            start=True, stop=True, tile_position=(0, 0)), incs=(("pe", 1),))
        p.ins("tensor", lambda e: e.matmul(
            ps[7][64:128, QB:2 * QB], consts["oneb64"], zr[:, 0:QB],
            start=True, stop=True, tile_position=(0, 64)), incs=(("pe", 1),))
        p.wait("scalar", "pe", p.cnt["pe"], loop=True)
        p.selfsync("scalar")
        p.ins("scalar", lambda e: e.activation(
            zb_sb[0:64, 0:QB], ps[6][0:64, QB:2 * QB], AF.Identity),
            incs=(("ac", 1),))
        p.ins("scalar", lambda e: e.activation(
            zb_sb[64:128, 0:QB], ps[7][64:128, QB:2 * QB], AF.Identity),
            incs=(("ac", 1),))
        zbed[hc] = p.cnt["ac"]
        p.wait("vector", "ac", p.cnt["ac"], loop=True)
        p.selfsync("vector")
        p.ins("vector", lambda e, hc=hc: e.tensor_tensor(
            att_sb[0:64, hc * QB:(hc + 1) * QB], ps[4][0:64, 0:QB],
            zb_sb[0:64, 0:QB], ALU.mult), incs=(("dv", 1),))
        p.ins("vector", lambda e, hc=hc: e.tensor_tensor(
            att_sb[64:128, hc * QB:(hc + 1) * QB], ps[5][64:128, 0:QB],
            zb_sb[64:128, 0:QB], ALU.mult), incs=(("dv", 1),))
        ttd[hc] = p.cnt["dv"]

    sev(0)
    for hc in range(NHC):
        exps(hc)
        if hc + 1 < NHC:
            sev(hc + 1)
        attvz(hc)
        zb(hc)


def _zb_and_evac(p, consts, ps, hc, zcol, t4_sb, zb_sb, att_sb, identf):
    p.wait("vector", "ac", p.cnt["ac"], loop=True)
    p.selfsync("vector")
    p.ins("vector", lambda e: e.reciprocal(zcol[:, 4:8], zcol[:, 0:4]),
          incs=(("dv", 1),))
    p.wait("tensor", "dv", p.cnt["dv"], loop=True)
    p.wait("tensor", "ac", p.cnt["ac"], loop=True)
    p.ins("tensor", lambda e: e.transpose(
        ps[5][0:4, 0:128], zcol[:, 4:8], identf), incs=(("pe", 1),))
    p.wait("vector", "pe", p.cnt["pe"], loop=True)
    p.selfsync("vector")
    p.ins("vector", lambda e: e.tensor_copy(
        t4_sb[0:4, 0:128], ps[5][0:4, 0:128]), incs=(("dv", 1),))
    p.wait("tensor", "dv", p.cnt["dv"], loop=True)
    for qt in range(2):
        p.ins("tensor", lambda e, qt=qt: e.matmul(
            ps[5][:, 256 + qt * 128: 256 + (qt + 1) * 128],
            consts["lq"][0:4, qt * 128:(qt + 1) * 128],
            t4_sb[0:4, 0:128], start=True, stop=True),
            incs=(("pe", 1),) if qt == 1 else ())
    p.wait("scalar", "pe", p.cnt["pe"], loop=True)
    p.selfsync("scalar")
    p.ins("scalar", lambda e: e.activation(
        zb_sb[:, 0:QB], ps[5][:, 256:256 + QB], AF.Identity), incs=(("ac", 1),))
    p.wait("vector", "ac", p.cnt["ac"], loop=True)
    p.selfsync("vector")
    p.ins("vector", lambda e, hc=hc: e.tensor_tensor(
        att_sb[:, hc * QB:(hc + 1) * QB], ps[4][:, 0:QB], zb_sb[:, 0:QB],
        ALU.mult), incs=(("dv", 1),))
    p.wait("tensor", "dv", p.cnt["dv"], loop=True)



def phase_moe(p, ar_mk, consts, xsc, dram, ps):
    HALF = 2048
    ones1 = consts["ones1"]
    ones2 = consts["ones2"]
    identf = consts["identf"]
    ident = consts["ident"]

    def body():
        ar = ar_mk()
        xn = ar.alloc(NCH * HALF, BF16)
        acc = ar.alloc(NCH * HALF, BF16)
        w1s = ar.alloc(NCH * D, BF16)
        w2s = ar.alloc(NCH * D, BF16)
        h_sb = ar.alloc(8 * D, BF16)
        hn_q = ar.alloc(4 * D, BF16)
        trash = ar.alloc(D, BF16)
        hT = ar.alloc(NCH * 512, BF16)
        xblk = ar.alloc(NCH * 512, F32)
        hn_q2 = xblk.bitcast(BF16)[:, 0:4 * D]
        ln = LNBlock(p, ar, 512)
        b1b = ar.alloc(D, F32)
        g1b = ar.alloc(D, BF16)
        beb = ar.alloc(D, BF16)
        rows1 = ar.alloc(3 * D, F32)
        wr_s = ar.alloc(NCH * 16, BF16)
        br_s = ar.alloc(1, F32)
        lg_sb = ar.alloc(512, F32)
        ltk = ar.alloc(16, F32)
        t8 = ar.alloc(8, F32)
        gtk = ar.alloc(16, F32)
        scr = ar.alloc(16, F32)
        st_tok = ar.alloc(32, F32)
        gates_all = ar.alloc(16 * 16, F32)
        b2all = ar.alloc(1024, F32)
        gfm = ar.alloc(HALF, F32)

        dw0 = p.cnt["dw"]
        p.wait("sync", "dw", dw0)
        p.ins("sync", lambda e: e.dma_start(out=b2all[0:16, :], in_=dram["b2n"]),
              incs=(("dw", 16),))
        p.ins("sync", lambda e: e.dma_start(out=br_s[0:16, 0:1], in_=dram["brv"]),
              incs=(("dw", 16),))
        p.ins("gpsimd", lambda e: e.dma_start(
            out=wr_s[:, :].rearrange("p (k n) -> p k n", k=NCH),
            in_=dram["wr"][:, :].rearrange("(k p) n -> p k n", p=128)),
            incs=(("dx", 16),))
        p.wait("gpsimd", "dx", p.cnt["dx"])
        for e in ("vector", "scalar", "tensor"):
            p.wait(e, "dw", p.cnt["dw"])
            p.wait(e, "dx", p.cnt["dx"])
        p.ins("vector", lambda e: e.memset(ln.bcin[0:2, :], 0.0))
        p.ins("vector", lambda e: e.memset(rows1[0:2, :], 0.0), incs=(("dv", 1),))

        for half in range(2):
            t0 = half * HALF
            for blk in range(4):
                tb0 = t0 + blk * 512
                dxa = p.cnt["dx"]
                for c in range(NCH):
                    p.ins("gpsimd", lambda e, c=c, tb0=tb0: e.dma_start(
                        out=xblk[:, c * 512:(c + 1) * 512],
                        in_=xsc[c * 128:(c + 1) * 128, tb0:tb0 + 512]),
                        incs=(("dx", 16),))
                for e in ("tensor", "vector", "scalar"):
                    p.wait(e, "dx", dxa + 16 * NCH)

                def xblk_c(c):
                    return xblk[:, c * 512:(c + 1) * 512]

                def xn_dst(c, j, blk=blk):
                    return xn[:, c * HALF + blk * 512: c * HALF + blk * 512 + 512]
                emit_ln(p, ln, consts, xblk_c, xn_dst, ps, consts["zero"])
                p.wait("gpsimd", "dv", p.cnt["dv"])

            for tq in range(4):
                bank = ps[6]
                p.wait("tensor", "ac", p.cnt["ac"])
                for k in range(NCH):
                    p.ins("tensor", lambda e, k=k, tq=tq, bank=bank: e.matmul(
                        bank[0:16, :],
                        wr_s[:, k * 16:(k + 1) * 16],
                        xn[:, k * HALF + tq * 512: k * HALF + (tq + 1) * 512],
                        start=(k == 0), stop=(k == NCH - 1)),
                        incs=(("pe", 1),) if k == NCH - 1 else ())
                p.wait("scalar", "pe", p.cnt["pe"])
                p.selfsync("scalar")
                p.ins("scalar", lambda e, bank=bank: e.activation(
                    lg_sb[0:16, :], bank[0:16, :], AF.Identity,
                    bias=br_s[0:16, 0:1]), incs=(("ac", 1),))
                p.wait("tensor", "ac", p.cnt["ac"])
                for tile in range(4):
                    tt = tq * 4 + tile
                    p.wait("tensor", "dv", p.cnt["dv"])
                    p.ins("tensor", lambda e, tile=tile: e.transpose(
                        ps[5][:, 0:16],
                        lg_sb[0:16, tile * 128:(tile + 1) * 128],
                        identf[0:16, 0:16]), incs=(("pe", 1),))
                    p.wait("vector", "pe", p.cnt["pe"])
                    p.selfsync("vector")
                    p.ins("vector", lambda e: e.tensor_copy(
                        ltk[:, 0:16], ps[5][:, 0:16]), incs=(("dv", 1),))
                    p.selfsync("vector")
                    p.ins("vector", lambda e: e.max(out=t8[:, 0:8], in_=ltk[:, 0:16]),
                          incs=(("dv", 1),))
                    p.selfsync("vector")
                    p.ins("vector", lambda e: e.tensor_scalar_mul(
                        scr[:, 0:1], t8[:, 0:1], -1.0), incs=(("dv", 1),))
                    p.wait("scalar", "dv", p.cnt["dv"])
                    p.selfsync("scalar")
                    p.ins("scalar", lambda e: e.activation(
                        gtk[:, 0:16], ltk[:, 0:16], AF.Exp,
                        bias=scr[:, 0:1]), incs=(("ac", 1),))
                    p.wait("vector", "ac", p.cnt["ac"])
                    p.selfsync("vector")
                    p.ins("vector", lambda e: e.scalar_tensor_tensor(
                        gtk[:, 0:16], ltk[:, 0:16], t8[:, 1:2], gtk[:, 0:16],
                        op0=ALU.is_ge, op1=ALU.mult,
                        accum_out=scr[:, 1:2]), incs=(("dv", 1),))
                    p.selfsync("vector")
                    p.ins("vector", lambda e: e.reciprocal(
                        scr[:, 2:3], scr[:, 1:2]), incs=(("dv", 1),))
                    p.selfsync("vector")
                    p.ins("vector", lambda e, tt=tt: e.tensor_scalar(
                        gates_all[:, tt * 16:(tt + 1) * 16], gtk[:, 0:16],
                        scr[:, 2:3], None, op0=ALU.mult), incs=(("dv", 1),))
                    p.wait("tensor", "dv", p.cnt["dv"])
                    p.ins("tensor", lambda e, tt=tt: e.transpose(
                        ps[5][0:16, 0:128],
                        gates_all[:, tt * 16:(tt + 1) * 16], identf),
                        incs=(("pe", 1),))
                    p.wait("vector", "pe", p.cnt["pe"])
                    p.selfsync("vector")
                    p.ins("vector", lambda e, tt=tt: e.tensor_copy(
                        gfm[0:16, tt * 128:(tt + 1) * 128], ps[5][0:16, 0:128]),
                        incs=(("dv", 1),))
            p.selfsync("vector")
            p.ins("vector", lambda e: e.memset(acc, 0.0), incs=(("dv", 1),))

            def e_body(ei):
                sl = 0
                p.wait("sync", "pe", p.cnt["pe"], loop=True)
                wpat = [[D, 128], [128 * D, NCH], [1, D]]
                w1src = p.symap3("sync", dram["w1_bf"], AFF((ei, D * D)), wpat)
                p.ins("sync", lambda e, w1src=w1src: e.dma_start(
                    out=w1s[:, :].rearrange("p (k n) -> p k n", k=NCH),
                    in_=w1src), incs=(("dwA", 16),))
                w2src = p.symap3("sync", dram["w2_bf"], AFF((ei, D * D)), wpat)
                p.ins("sync", lambda e, w2src=w2src: e.dma_start(
                    out=w2s[:, :].rearrange("p (k n) -> p k n", k=NCH),
                    in_=w2src), incs=(("dwB", 16),))
                p.wait("gpsimd", "pe", p.cnt["pe"], loop=True)
                rsrc = p.symap("gpsimd", dram["bge"], AFF((ei, 3 * D)), 3 * D,
                               rows=(0, 1))
                p.ins("gpsimd", lambda e, rsrc=rsrc: e.dma_start(
                    out=rows1[0:1, 0:3 * D], in_=rsrc), incs=(("dx", 16),))
                p.wait("tensor", "dx", p.cnt["dx"], loop=True)
                p.wait("tensor", "dv", p.cnt["dv"], loop=True)
                for i3, dst in ((0, b1b), (1, g1b), (2, beb)):
                    for hh in range(2):
                        p.ins("tensor", lambda e, i3=i3, hh=hh: e.matmul(
                            ps[7][:, :], ones2,
                            bass.AP(rows1.tensor,
                                    rows1.offset + i3 * D + hh * 512,
                                    [[rows1.ap[0][0], 2], [1, 512]]),
                            start=True, stop=True), incs=(("pe", 1),))
                        p.wait("scalar", "pe", p.cnt["pe"], loop=True)
                        p.selfsync("scalar")
                        p.ins("scalar", lambda e, dst=dst, hh=hh: e.activation(
                            dst[:, hh * 512:(hh + 1) * 512], ps[7][:, :],
                            AF.Identity), incs=(("ac", 1),))
                        p.wait("tensor", "ac", p.cnt["ac"], loop=True)
                p.wait("tensor", "dwA", p.cnt["dwA"], loop=True)
                p.wait("tensor", "dwB", p.cnt["dwB"], loop=True)

                evd = {}
                sqd = {}
                gtd = {}
                trd = {}
                cpd = {}
                accd = {}

                def emit_w1(g):
                    tg0 = g * 512
                    for tl in range(4):
                        ttile = g * 4 + tl
                        tt0 = tg0 + tl * 128
                        hb = [ps[(tl % 2) * 2], ps[(tl % 2) * 2 + 1]]
                        if ttile >= 2:
                            p.wait("tensor", "dv", evd[ttile - 2], loop=True)
                        for hh in range(2):
                            for kc in range(NCH):
                                p.ins("tensor", lambda e, hh=hh, kc=kc, tt0=tt0:
                                      e.matmul(
                                          hb[hh][:, :],
                                          xn[:, kc * HALF + tt0: kc * HALF + tt0 + 128],
                                          w1s[:, kc * D + hh * 512: kc * D + (hh + 1) * 512],
                                          start=(kc == 0), stop=(kc == NCH - 1)),
                                      incs=(("pe", 1),) if (kc == NCH - 1 and hh == 1)
                                      else ())
                        p.wait("vector", "pe", p.cnt["pe"], loop=True)
                        p.selfsync("vector")
                        if ttile >= 8:
                            p.wait("vector", "ac", sqd[ttile - 8], loop=True)
                        hslot = ((g % 2) * 4 + tl) * D
                        for hh in range(2):
                            p.ins("vector", lambda e, hh=hh, hslot=hslot, tl=tl:
                                  e.scalar_tensor_tensor(
                                      h_sb[:, hslot + hh * 512: hslot + (hh + 1) * 512],
                                      hb[hh][:, :], 1.0,
                                      b1b[:, hh * 512:(hh + 1) * 512],
                                      op0=ALU.mult, op1=ALU.add,
                                      accum_out=st_tok[:, hh * 4 + tl: hh * 4 + tl + 1]),
                                  incs=(("dv", 1),))
                        evd[ttile] = p.cnt["dv"]
                        p.wait("scalar", "dv", p.cnt["dv"], loop=True)
                        p.selfsync("scalar")
                        p.ins("scalar", lambda e, hslot=hslot, tl=tl: e.activation(
                            trash[:, 0:D], h_sb[:, hslot:hslot + D], AF.Square,
                            accum_out=st_tok[:, 8 + tl:9 + tl]), incs=(("ac", 1),))
                        sqd[ttile] = p.cnt["ac"]

                def emit_elem(g):
                    p.wait("vector", "ac", sqd[g * 4 + 3], loop=True)
                    p.selfsync("vector")
                    p.ins("vector", lambda e: e.tensor_tensor(
                        st_tok[:, 12:16], st_tok[:, 0:4], st_tok[:, 4:8],
                        ALU.add), incs=(("dv", 1),))
                    p.selfsync("vector")
                    p.ins("vector", lambda e: e.tensor_scalar_mul(
                        st_tok[:, 12:16], st_tok[:, 12:16], 1.0 / D),
                        incs=(("dv", 1),))
                    p.ins("vector", lambda e: e.tensor_scalar_mul(
                        st_tok[:, 16:20], st_tok[:, 8:12], 1.0 / D),
                        incs=(("dv", 1),))
                    p.selfsync("vector")
                    p.ins("vector", lambda e: e.tensor_tensor(
                        st_tok[:, 20:24], st_tok[:, 12:16], st_tok[:, 12:16],
                        ALU.mult), incs=(("dv", 1),))
                    p.selfsync("vector")
                    p.ins("vector", lambda e: e.tensor_tensor(
                        st_tok[:, 20:24], st_tok[:, 16:20], st_tok[:, 20:24],
                        ALU.subtract), incs=(("dv", 1),))
                    p.wait("scalar", "dv", p.cnt["dv"], loop=True)
                    p.selfsync("scalar")
                    p.ins("scalar", lambda e: e.activation(
                        st_tok[:, 20:24], st_tok[:, 20:24], AF.Ln,
                        bias=consts["eps"][0:128, 0:1]), incs=(("ac", 1),))
                    p.selfsync("scalar")
                    p.ins("scalar", lambda e: e.activation(
                        st_tok[:, 24:28], st_tok[:, 20:24], AF.Exp,
                        scale=consts["neghalf"][0:128, 0:1]), incs=(("ac", 1),))
                    p.wait("vector", "ac", p.cnt["ac"], loop=True)
                    p.selfsync("vector")
                    p.ins("vector", lambda e: e.tensor_tensor(
                        st_tok[:, 28:32], st_tok[:, 12:16], st_tok[:, 24:28],
                        ALU.mult), incs=(("dv", 1),))
                    p.selfsync("vector")
                    p.ins("vector", lambda e: e.tensor_scalar_mul(
                        st_tok[:, 28:32], st_tok[:, 28:32], -1.0),
                        incs=(("dv", 1),))
                    hq = hn_q if g % 2 == 0 else hn_q2
                    p.selfsync("vector")
                    if g >= 2:
                        p.wait("vector", "pe", trd[(g - 2) * 4 + 3], loop=True)
                    for tl in range(4):
                        hs8 = ((g % 2) * 4 + tl) * D
                        p.ins("vector", lambda e, tl=tl, hs8=hs8, hq=hq:
                              e.tensor_scalar(
                                  hq[:, tl * D:(tl + 1) * D], h_sb[:, hs8:hs8 + D],
                                  st_tok[:, 24 + tl:25 + tl],
                                  st_tok[:, 28 + tl:29 + tl],
                                  op0=ALU.mult, op1=ALU.add), incs=(("dv", 1),))
                    p.selfsync("vector")
                    for tl in range(4):
                        p.ins("vector", lambda e, tl=tl, hq=hq: e.tensor_tensor(
                            hq[:, tl * D:(tl + 1) * D], hq[:, tl * D:(tl + 1) * D],
                            g1b, ALU.mult), incs=(("dv", 1),))
                    p.selfsync("vector")
                    for tl in range(4):
                        p.ins("vector", lambda e, tl=tl, hq=hq: e.tensor_tensor(
                            hq[:, tl * D:(tl + 1) * D], hq[:, tl * D:(tl + 1) * D],
                            beb, ALU.add), incs=(("dv", 1),))
                    p.wait("scalar", "dv", p.cnt["dv"], loop=True)
                    p.selfsync("scalar")
                    p.ins("scalar", lambda e, hq=hq: e.activation(
                        hq[:, 0:4 * D], hq[:, 0:4 * D], AF.Gelu),
                        incs=(("ac", 1),))
                    p.wait("vector", "ac", p.cnt["ac"], loop=True)
                    p.selfsync("vector")
                    for tl in range(4):
                        ttile = g * 4 + tl
                        gate_ap = p.symap("vector", gates_all,
                                          AFF((ei, 1), const=ttile * 16), 1)
                        p.ins("vector", lambda e, tl=tl, gate_ap=gate_ap, hq=hq:
                              e.tensor_scalar(
                                  hq[:, tl * D:(tl + 1) * D],
                                  hq[:, tl * D:(tl + 1) * D],
                                  gate_ap, None, op0=ALU.mult),
                              incs=(("dv", 1),))
                    gtd[g] = p.cnt["dv"]

                def emit_trw2(g):
                    tg0 = g * 512
                    p.wait("tensor", "dv", gtd[g], loop=True)
                    for tl in range(4):
                        ttile = g * 4 + tl
                        psT = ps[6 + tl % 2][:, :].bitcast(BF16)
                        if tl >= 2:
                            p.wait("tensor", "dv", cpd[ttile - 2], loop=True)
                        elif g >= 1:
                            p.wait("tensor", "dv", cpd[(g - 1) * 4 + 2 + tl],
                                   loop=True)
                        hq = hn_q if g % 2 == 0 else hn_q2
                        for kcj in range(NCH):
                            p.ins("tensor", lambda e, kcj=kcj, tl=tl, psT=psT,
                                  hq=hq: e.transpose(
                                      psT[:, kcj * 128:(kcj + 1) * 128],
                                      hq[:, tl * D + kcj * 128: tl * D + (kcj + 1) * 128],
                                      ident), incs=(("pe", 1),) if kcj == NCH - 1
                                  else ())
                        trd[ttile] = p.cnt["pe"]
                        p.wait("vector", "pe", trd[ttile], loop=True)
                        p.selfsync("vector")
                        for kcj in range(NCH):
                            p.ins("vector", lambda e, kcj=kcj, tl=tl, psT=psT:
                                  e.tensor_copy(
                                      hT[:, kcj * 512 + tl * 128: kcj * 512 + (tl + 1) * 128],
                                      psT[:, kcj * 128:(kcj + 1) * 128]),
                                  incs=(("dv", 1),) if kcj == NCH - 1 else ())
                        cpd[ttile] = p.cnt["dv"]
                    p.wait("tensor", "dv", cpd[g * 4 + 3], loop=True)
                    for mo in range(NCH):
                        bank = ps[4 + mo % 2]
                        if mo >= 2:
                            p.wait("tensor", "dv", accd[(g, mo - 2)], loop=True)
                        elif g >= 1:
                            p.wait("tensor", "dv", accd[(g - 1, 6 + mo)], loop=True)
                        for kcj in range(NCH):
                            p.ins("tensor", lambda e, mo=mo, kcj=kcj, bank=bank:
                                  e.matmul(
                                      bank[:, :],
                                      w2s[:, mo * D + kcj * 128: mo * D + kcj * 128 + 128],
                                      hT[:, kcj * 512:(kcj + 1) * 512],
                                      start=(kcj == 0), stop=(kcj == NCH - 1)),
                                  incs=(("pe", 1),) if kcj == NCH - 1 else ())
                        p.wait("vector", "pe", p.cnt["pe"], loop=True)
                        p.selfsync("vector")
                        p.ins("vector", lambda e, mo=mo, tg0=tg0, bank=bank:
                              e.tensor_tensor(
                                  acc[:, mo * HALF + tg0: mo * HALF + tg0 + 512],
                                  acc[:, mo * HALF + tg0: mo * HALF + tg0 + 512],
                                  bank[:, :], ALU.add), incs=(("dv", 1),))
                        accd[(g, mo)] = p.cnt["dv"]

                for g in range(4):
                    emit_w1(g)
                    if g >= 2:
                        emit_trw2(g - 2)
                    emit_elem(g)
                emit_trw2(2)
                emit_trw2(3)
            p.emit_loop(NEXP, e_body, peel=0)

            p.wait("sync", "dwA", p.cnt["dwA"])
            p.wait("sync", "dwB", p.cnt["dwB"])
            p.wait("gpsimd", "dx", p.cnt["dx"])
            for tq in range(4):
                tb0 = t0 + tq * 512
                dxa = p.cnt["dx"]
                p.wait("gpsimd", "dv", p.cnt["dv"])
                for c in range(NCH):
                    p.ins("gpsimd", lambda e, c=c, tb0=tb0: e.dma_start(
                        out=xblk[:, c * 512:(c + 1) * 512],
                        in_=xsc[c * 128:(c + 1) * 128, tb0:tb0 + 512]),
                        incs=(("dx", 16),))
                p.wait("tensor", "dx", dxa + 16 * NCH)
                p.wait("vector", "dx", dxa + 16 * NCH)
                for o in range(NCH):
                    bank = ps[6 + o % 2]
                    p.wait("tensor", "dv", p.cnt["dv"])
                    p.ins("tensor", lambda e, o=o, tq=tq, bank=bank: e.matmul(
                        bank[:, :], b2all[0:16, o * 128:(o + 1) * 128],
                        gfm[0:16, tq * 512:(tq + 1) * 512],
                        start=True, stop=True), incs=(("pe", 1),))
                    p.wait("vector", "pe", p.cnt["pe"])
                    p.selfsync("vector")
                    p.ins("vector", lambda e, o=o, bank=bank: e.scalar_tensor_tensor(
                        xblk[:, o * 512:(o + 1) * 512], bank[:, :], 0.5,
                        xblk[:, o * 512:(o + 1) * 512],
                        op0=ALU.mult, op1=ALU.add), incs=(("dv", 1),))
                    p.selfsync("vector")
                    p.ins("vector", lambda e, o=o, tq=tq: e.scalar_tensor_tensor(
                        xblk[:, o * 512:(o + 1) * 512],
                        acc[:, o * HALF + tq * 512: o * HALF + (tq + 1) * 512],
                        0.5, xblk[:, o * 512:(o + 1) * 512],
                        op0=ALU.mult, op1=ALU.add), incs=(("dv", 1),))
                p.wait("gpsimd", "dv", p.cnt["dv"])
                for c in range(NCH):
                    p.ins("gpsimd", lambda e, c=c, tb0=tb0: e.dma_start(
                        out=xsc[c * 128:(c + 1) * 128, tb0:tb0 + 512],
                        in_=xblk[:, c * 512:(c + 1) * 512]),
                        incs=(("dx", 16),))
                p.wait("gpsimd", "dx", p.cnt["dx"])
            p.barrier()
    return body


def phase_l1pre(p, ar_mk, consts, xsc, zsc, ps):
    def body():
        ar = ar_mk()
        xblk = ar.alloc(NCH * 512, F32)
        zblk = ar.alloc(NCH * 512, F32)
        ln = LNBlock(p, ar, 512)
        p.ins("vector", lambda e: e.memset(ln.bcin[0:2, :], 0.0), incs=(("dv", 1),))
        for blk in range(TOKC // 512):
            t0 = blk * 512
            dxa = p.cnt["dx"]
            for c in range(NCH):
                p.ins("gpsimd", lambda e, c=c, t0=t0: e.dma_start(
                    out=xblk[:, c * 512:(c + 1) * 512],
                    in_=xsc[c * 128:(c + 1) * 128, t0:t0 + 512]),
                    incs=(("dx", 16),))
            for e in ("tensor", "vector", "scalar"):
                p.wait(e, "dx", dxa + 16 * NCH)

            def xblk_c(c):
                return xblk[:, c * 512:(c + 1) * 512]

            def z_dst(c, j):
                return zblk[:, c * 512:(c + 1) * 512]
            emit_ln(p, ln, consts, xblk_c, z_dst, ps, consts["zero"])
            p.wait("gpsimd", "dv", p.cnt["dv"])
            for c in range(NCH):
                p.ins("gpsimd", lambda e, c=c, t0=t0: e.dma_start(
                    out=zsc[c * 128:(c + 1) * 128, t0:t0 + 512],
                    in_=zblk[:, c * 512:(c + 1) * 512]),
                    incs=(("dx", 16),))
            p.wait("gpsimd", "dx", p.cnt["dx"])
        p.barrier()
    return body


def phase_l1post(p, ar_mk, consts, xsc, zsc):
    def body():
        ar = ar_mk()
        xblk = ar.alloc(NCH * 512, F32)
        zblk = ar.alloc(NCH * 512, F32)
        for blk in range(TOKC // 512):
            t0 = blk * 512
            dxa = p.cnt["dx"]
            for c in range(NCH):
                p.ins("gpsimd", lambda e, c=c, t0=t0: e.dma_start(
                    out=xblk[:, c * 512:(c + 1) * 512],
                    in_=xsc[c * 128:(c + 1) * 128, t0:t0 + 512]),
                    incs=(("dx", 16),))
                p.ins("gpsimd", lambda e, c=c, t0=t0: e.dma_start(
                    out=zblk[:, c * 512:(c + 1) * 512],
                    in_=zsc[c * 128:(c + 1) * 128, t0:t0 + 512]),
                    incs=(("dx", 16),))
            p.wait("vector", "dx", p.cnt["dx"])
            p.selfsync("vector")
            for c in range(NCH):
                p.ins("vector", lambda e, c=c: e.tensor_tensor(
                    xblk[:, c * 512:(c + 1) * 512],
                    xblk[:, c * 512:(c + 1) * 512],
                    zblk[:, c * 512:(c + 1) * 512], ALU.add), incs=(("dv", 1),))
            p.wait("gpsimd", "dv", p.cnt["dv"])
            for c in range(NCH):
                p.ins("gpsimd", lambda e, c=c, t0=t0: e.dma_start(
                    out=xsc[c * 128:(c + 1) * 128, t0:t0 + 512],
                    in_=xblk[:, c * 512:(c + 1) * 512]),
                    incs=(("dx", 16),))
            p.wait("gpsimd", "dx", p.cnt["dx"])
        p.barrier()
    return body


def phase_l1ffn(p, ar_mk, consts, zsc, w1cb, w2cb, b1d, lngd, lnbd, ps):
    MT = 1024
    NMT = TOKC // MT
    nh = MT // 512
    MH = 32

    def body():
        ar = ar_mk()
        xblk = ar.alloc(NCH * MT, F32)
        zn = ar.alloc(NCH * MT, BF16)
        ln = LNBlock(p, ar, MT)
        lntmp = ar.alloc(512, F32)
        wgm = ar.alloc(2 * D, BF16)
        h_t = ar.alloc(MH * MT, BF16)
        b1t = ar.alloc(MH, F32)
        lng_t = ar.alloc(NCH, F32)
        lnb_t = ar.alloc(NCH, F32)

        dw0 = p.cnt["dw"]
        p.wait("sync", "dw", dw0)
        p.ins("sync", lambda e: e.dma_start(out=b1t, in_=b1d), incs=(("dw", 16),))
        p.ins("sync", lambda e: e.dma_start(out=lng_t, in_=lngd), incs=(("dw", 16),))
        p.ins("sync", lambda e: e.dma_start(out=lnb_t, in_=lnbd), incs=(("dw", 16),))
        for e in ("vector", "scalar", "tensor"):
            p.wait(e, "dw", p.cnt["dw"])
        p.ins("vector", lambda e: e.memset(ln.bcin[0:2, :], 0.0), incs=(("dv", 1),))

        def xblk_c(c):
            return xblk[:, c * MT:(c + 1) * MT]

        for mt in range(NMT):
            t0 = mt * MT
            dx0 = p.cnt["dx"]
            for c in range(NCH):
                p.ins("gpsimd", lambda e, c=c, t0=t0: e.dma_start(
                    out=xblk_c(c), in_=zsc[c * 128:(c + 1) * 128, t0:t0 + MT]),
                    incs=(("dx", 16),))
            for e in ("tensor", "vector", "scalar"):
                p.wait(e, "dx", dx0 + 16 * NCH)

            def xn_dst(c, j):
                return lntmp[:, 0:512]

            def post(c, j):
                p.ins("scalar", lambda e, c=c, j=j: e.activation(
                    zn[:, c * MT + j * 512: c * MT + (j + 1) * 512],
                    lntmp[:, 0:512], AF.Identity,
                    scale=lng_t[:, c:c + 1], bias=lnb_t[:, c:c + 1]),
                    incs=(("ac", 1),))
            emit_ln(p, ln, consts, xblk_c, xn_dst, ps, consts["zero"], post=post)

            dv_gu0 = p.cnt["dv"]
            ac_gu0 = p.cnt["ac"]
            for m in range(MH):
                slot = "dwA" if m % 2 == 0 else "dwB"
                if m >= 2:
                    p.wait("sync", "ac", ac_gu0 + (m - 1) * nh)
                p.ins("sync", lambda e, m=m: e.dma_start(
                    out=wgm[:, (m % 2) * D:(m % 2) * D + D],
                    in_=w1cb[m * 128:(m + 1) * 128, :]), incs=((slot, 16),))
                bsel = (m % 2) * 4
                gb = [ps[bsel + 0], ps[bsel + 1]]
                p.wait("tensor", slot, p.cnt[slot])
                if m >= 2:
                    p.wait("tensor", "ac", ac_gu0 + (m - 1) * nh)
                for k in range(NCH):
                    for j in range(nh):
                        p.ins("tensor", lambda e, k=k, m=m, j=j: e.matmul(
                            gb[j][:, :],
                            wgm[:, (m % 2) * D + k * 128:(m % 2) * D + k * 128 + 128],
                            zn[:, k * MT + j * 512: k * MT + (j + 1) * 512],
                            start=(k == 0), stop=(k == NCH - 1)),
                            incs=(("pe", 1),) if (k == NCH - 1 and j == nh - 1) else ())
                p.wait("scalar", "pe", p.cnt["pe"])
                for j in range(nh):
                    p.selfsync("scalar")
                    p.ins("scalar", lambda e, m=m, j=j: e.activation(
                        h_t[:, m * MT + j * 512: m * MT + (j + 1) * 512],
                        gb[j][:, :], AF.Gelu_apprx_sigmoid,
                        bias=b1t[:, m:m + 1]),
                        incs=(("ac", 1),))
            p.wait("tensor", "ac", p.cnt["ac"])

            wdo = zn
            dv_do0 = p.cnt["dv"]
            for o in range(NCH):
                slot = "dwA" if o % 2 == 0 else "dwB"
                if o >= 2:
                    p.wait("sync", "dv", dv_do0 + (o - 1) * nh)
                else:
                    p.wait("sync", "ac", p.cnt["ac"])
                p.ins("sync", lambda e, o=o: e.dma_start(
                    out=wdo[:, (o % 2) * EXP4:((o % 2) + 1) * EXP4],
                    in_=w2cb[o * 128:(o + 1) * 128, :]), incs=((slot, 16),))
                bsel = (o % 2) * 4
                db = [ps[bsel + 0], ps[bsel + 1]]
                p.wait("tensor", slot, p.cnt[slot])
                if o >= 2:
                    p.wait("tensor", "dv", dv_do0 + (o - 1) * nh)
                for k in range(MH):
                    for j in range(nh):
                        p.ins("tensor", lambda e, k=k, o=o, j=j: e.matmul(
                            db[j][:, :],
                            wdo[:, (o % 2) * EXP4 + k * 128:(o % 2) * EXP4 + k * 128 + 128],
                            h_t[:, k * MT + j * 512: k * MT + (j + 1) * 512],
                            start=(k == 0), stop=(k == MH - 1)),
                            incs=(("pe", 1),) if (k == MH - 1 and j == nh - 1) else ())
                p.wait("vector", "pe", p.cnt["pe"])
                for j in range(nh):
                    p.selfsync("vector")
                    p.ins("vector", lambda e, o=o, j=j: e.tensor_tensor(
                        xblk_c(o)[:, j * 512:(j + 1) * 512],
                        db[j][:, :],
                        xblk_c(o)[:, j * 512:(j + 1) * 512],
                        ALU.add), incs=(("dv", 1),))
            p.wait("gpsimd", "dv", p.cnt["dv"])
            for c in range(NCH):
                p.ins("gpsimd", lambda e, c=c, t0=t0: e.dma_start(
                    out=zsc[c * 128:(c + 1) * 128, t0:t0 + MT], in_=xblk_c(c)),
                    incs=(("dx", 16),))
            p.wait("gpsimd", "dx", p.cnt["dx"])
            p.barrier()
    return body


def build_program(phase_names, dev_nb=NB):
    from contextlib import ExitStack
    es = ExitStack()
    nc = bacc_mod.Bacc("TRN2", target_bir_lowering=False, debug=False)

    toks = dev_nb * SEQ
    global TOKC
    assert dev_nb == NB, "dev_nb must equal NB for now"

    xT = nc.dram_tensor("xT", [D, TOKC], F32, kind="ExternalInput").ap()
    yT = nc.dram_tensor("yT", [D, TOKC], F32, kind="ExternalOutput").ap()
    xsc = nc.dram_tensor("xsc", [D, TOKC], F32, kind="Internal").ap()

    need = set(phase_names)
    dram = {}
    if "l0attn" in need:
        for nm in ("wq_a", "wk_a", "wv_a", "wo_a"):
            dram[nm] = nc.dram_tensor(nm, [D, D], F32, kind="ExternalInput").ap()
            dram[nm + "_bf"] = nc.dram_tensor(nm + "_bf", [D, D], BF16,
                                              kind="Internal").ap()
        for nm in ("bq_a", "bk_a", "bv_a", "bo_a"):
            dram[nm] = nc.dram_tensor(nm, [128, NCH], F32, kind="ExternalInput").ap()
    if "l1attn" in need:
        for i in range(2):
            for nm in ("wq", "wk", "wv", "wo"):
                dram[f"{nm}_l{i}"] = nc.dram_tensor(f"{nm}_l{i}", [D, D], F32,
                                                    kind="ExternalInput").ap()
                dram[f"{nm}_l{i}_bf"] = nc.dram_tensor(f"{nm}_l{i}_bf", [D, D],
                                                       BF16, kind="Internal").ap()
            dram[f"w1_l{i}"] = nc.dram_tensor(f"w1_l{i}", [EXP4, D], F32,
                                              kind="ExternalInput").ap()
            dram[f"w1_l{i}_bf"] = nc.dram_tensor(f"w1_l{i}_bf", [EXP4, D], BF16,
                                                 kind="Internal").ap()
            dram[f"w2_l{i}"] = nc.dram_tensor(f"w2_l{i}", [D, EXP4], F32,
                                              kind="ExternalInput").ap()
            dram[f"w2_l{i}_bf"] = nc.dram_tensor(f"w2_l{i}_bf", [D, EXP4], BF16,
                                                 kind="Internal").ap()
            for nm in ("bq", "bk", "bv", "lng1", "lnb1", "lng2", "lnb2"):
                dram[f"{nm}_l{i}"] = nc.dram_tensor(f"{nm}_l{i}", [128, NCH], F32,
                                                    kind="ExternalInput").ap()
            dram[f"b1_l{i}"] = nc.dram_tensor(f"b1_l{i}", [128, 32], F32,
                                              kind="ExternalInput").ap()
        dram["qtab"] = nc.dram_tensor("qtab", [128, SEQ], BF16,
                                      kind="ExternalInput").ap()
        dram["ktab"] = nc.dram_tensor("ktab", [128, SEQ], BF16,
                                      kind="ExternalInput").ap()
        dram["zsc"] = nc.dram_tensor("zsc", [D, TOKC], F32, kind="Internal").ap()
    if "moe" in need:
        dram["wr"] = nc.dram_tensor("wr", [D, NEXP], F32, kind="ExternalInput").ap()
        dram["brv"] = nc.dram_tensor("brv", [NEXP, 1], F32, kind="ExternalInput").ap()
        dram["w1n"] = nc.dram_tensor("w1n", [NEXP * D, D], F32, kind="ExternalInput").ap()
        dram["w2n"] = nc.dram_tensor("w2n", [NEXP * D, D], F32, kind="ExternalInput").ap()
        dram["w1_bf"] = nc.dram_tensor("w1_bf", [NEXP * D, D], BF16, kind="Internal").ap()
        dram["w2_bf"] = nc.dram_tensor("w2_bf", [NEXP * D, D], BF16, kind="Internal").ap()
        dram["bge"] = nc.dram_tensor("bge", [NEXP, 3 * D], F32, kind="ExternalInput").ap()
        dram["b2n"] = nc.dram_tensor("b2n", [NEXP, D], F32, kind="ExternalInput").ap()
    if "l0ffn" in need:
        dram["wgT"] = nc.dram_tensor("wgT", [HID0, D], I32, kind="ExternalInput").ap()
        dram["wuT"] = nc.dram_tensor("wuT", [HID0, D], I32, kind="ExternalInput").ap()
        dram["wdT"] = nc.dram_tensor("wdT", [D, HID0], I32, kind="ExternalInput").ap()
        dram["wgT_bf"] = nc.dram_tensor("wgT_bf", [HID0, D], BF16, kind="Internal").ap()
        dram["wuT_bf"] = nc.dram_tensor("wuT_bf", [HID0, D], BF16, kind="Internal").ap()
        dram["wdT_bf"] = nc.dram_tensor("wdT_bf", [D, HID0], BF16, kind="Internal").ap()

    ones1_d = nc.inline_tensor(np.ones((128, 1), np.float32), name="ones1_c").ap()
    ones2_np = np.zeros((2, 128), np.float32)
    ones2_np[0, :] = 1.0
    ones2_d = nc.inline_tensor(ones2_np, name="ones2_c").ap()
    ident_d = nc.inline_tensor(np.eye(128, dtype=BFNP), name="ident_c").ap()
    identf_d = nc.inline_tensor(np.eye(128, dtype=np.float32), name="identf_c").ap()
    lq_np = np.zeros((4, 256), np.float32)
    for qt in range(2):
        for pp in range(128):
            r = qt if pp < 64 else 2 + qt
            lq_np[r, qt * 128 + pp] = 1.0
    lq_d = nc.inline_tensor(lq_np, name="lq_c").ap()

    p = Prog(nc, es)
    const_ar = Arena(nc, es, "consts", 4096)
    consts = {}
    consts["ones1"] = const_ar.alloc(1, F32)
    consts["ones2"] = None
    ones2_t = const_ar.alloc(128, F32)
    consts["ident"] = const_ar.alloc(256, BF16)[0:128, 0:128]
    consts["eps"] = const_ar.alloc(1, F32)
    consts["neghalf"] = const_ar.alloc(1, F32)
    consts["ln127"] = const_ar.alloc(1, F32)
    consts["zero"] = const_ar.alloc(1, F32)
    consts["eps12"] = const_ar.alloc(1, F32)
    consts["poshalf"] = const_ar.alloc(1, F32)
    consts["negone"] = const_ar.alloc(1, F32)
    consts["onesz"] = const_ar.alloc(64, BF16)
    consts["oneb0"] = const_ar.alloc(64, BF16)
    consts["oneb64"] = const_ar.alloc(64, BF16)
    consts["qsc"] = const_ar.alloc(1, F32)
    identf_t = const_ar.alloc(128, F32)
    consts["identf"] = identf_t[0:128, 0:128]
    lq_t = const_ar.alloc(256, F32)
    consts["lq"] = lq_t
    consts["ones2"] = ones2_t[0:2, 0:128]

    WS_BYTES = 208640
    ws = Arena(nc, es, "ws", WS_BYTES)

    ps = [es.enter_context(nc.psum_tensor(f"ps{i}", [128, 512], F32))[:]
          for i in range(8)]

    def phase_init():
        p.ins("sync", lambda e: e.dma_start(out=consts["ones1"], in_=ones1_d),
              incs=(("dw", 16),))
        p.ins("sync", lambda e: e.dma_start(out=ones2_t[0:2, :], in_=ones2_d),
              incs=(("dw", 16),))
        p.ins("sync", lambda e: e.dma_start(out=consts["ident"], in_=ident_d),
              incs=(("dw", 16),))
        p.ins("sync", lambda e: e.dma_start(out=consts["identf"], in_=identf_d),
              incs=(("dw", 16),))
        p.ins("sync", lambda e: e.dma_start(out=lq_t[0:4, :], in_=lq_d),
              incs=(("dw", 16),))
        p.ins("vector", lambda e: e.memset(consts["eps"], LN_EPS))
        p.ins("vector", lambda e: e.memset(consts["neghalf"], -0.5))
        p.ins("vector", lambda e: e.memset(consts["ln127"], float(np.log(1.0 / 127.0))))
        p.ins("vector", lambda e: e.memset(consts["zero"], 0.0))
        p.ins("vector", lambda e: e.memset(consts["eps12"], 1e-12))
        p.ins("vector", lambda e: e.memset(consts["negone"], -1.0))
        p.ins("vector", lambda e: e.memset(consts["onesz"], 0.0))
        p.ins("vector", lambda e: e.memset(consts["oneb0"], 0.0))
        p.ins("vector", lambda e: e.memset(consts["oneb64"], 0.0),
              incs=(("dv", 1),))
        p.selfsync("vector")
        p.ins("vector", lambda e: e.memset(consts["onesz"][:, 0:1], 1.0))
        p.ins("vector", lambda e: e.memset(consts["oneb0"][0:1, :], 1.0))
        p.ins("vector", lambda e: e.memset(consts["oneb64"][64:65, :], 1.0))
        p.ins("vector", lambda e: e.memset(consts["poshalf"], 0.5), incs=(("dv", 1),))
        p.wait("sync", "dw", p.cnt["dw"])
        p.barrier()
    p.phases.append(phase_init)

    p.phases.append(phase_xcopy(p, xsc, xT))
    if "l0attn" in need:
        p.phases.append(phase_wcast(p, [
            (dram[nm + "_bf"], dram[nm]) for nm in ("wq_a", "wk_a", "wv_a", "wo_a")]))
    if "moe" in need:
        p.phases.append(phase_wcast(p, [
            (dram["w1_bf"], dram["w1n"]), (dram["w2_bf"], dram["w2n"])]))
    if "l1attn" in need:
        pairs = []
        for i in range(2):
            for nm in ("wq", "wk", "wv", "wo", "w1", "w2"):
                pairs.append((dram[f"{nm}_l{i}_bf"], dram[f"{nm}_l{i}"]))
        p.phases.append(phase_wcast(p, pairs))
    if "l0ffn" in need:
        p.phases.append(phase_wcast(p, [
            (dram["wgT_bf"], dram["wgT"]),
            (dram["wuT_bf"], dram["wuT"]),
            (dram["wdT_bf"], dram["wdT"]),
        ]))

    def ar_mk():
        ws.reset()
        return ws

    for name in phase_names:
        if name == "l0ffn":
            p.phases.append(phase_l0ffn(
                p, ar_mk, consts, xsc,
                dram["wgT_bf"], dram["wuT_bf"], dram["wdT_bf"], ps))
        elif name == "moe":
            p.phases.append(phase_moe(p, ar_mk, consts, xsc, dram, ps))
        elif name == "l1attn":
            zsc = dram["zsc"]
            p.phases.append(phase_l1pre(p, ar_mk, consts, xsc, zsc, ps))
            for i in range(2):
                cfg = {
                    "stream": zsc, "sel": False,
                    "wq": dram[f"wq_l{i}_bf"], "wk": dram[f"wk_l{i}_bf"],
                    "wv": dram[f"wv_l{i}_bf"], "wo": dram[f"wo_l{i}_bf"],
                    "bq": dram[f"bq_l{i}"], "bk": dram[f"bk_l{i}"],
                    "bv": dram[f"bv_l{i}"],
                    "lng": dram[f"lng1_l{i}"], "lnb": dram[f"lnb1_l{i}"],
                    "qtab": dram["qtab"], "ktab": dram["ktab"],
                }
                p.phases.append(phase_attention(p, ar_mk, consts, cfg, ps))
                p.phases.append(phase_l1ffn(
                    p, ar_mk, consts, zsc, dram[f"w1_l{i}_bf"],
                    dram[f"w2_l{i}_bf"], dram[f"b1_l{i}"],
                    dram[f"lng2_l{i}"], dram[f"lnb2_l{i}"], ps))
            p.phases.append(phase_l1post(p, ar_mk, consts, xsc, zsc))
        elif name == "l0attn":
            cfg = {
                "stream": xsc, "sel": True, "qscale": 1.0 / 32.0,
                "wq": dram["wq_a_bf"], "wk": dram["wk_a_bf"],
                "wv": dram["wv_a_bf"], "wo": dram["wo_a_bf"],
                "bq": dram["bq_a"], "bk": dram["bk_a"], "bv": dram["bv_a"],
                "bo": dram["bo_a"],
            }
            p.phases.append(phase_attention(p, ar_mk, consts, cfg, ps))
        else:
            raise ValueError(name)

    p.phases.append(phase_xcopy(p, yT, xsc, sem="ds", queue="sync"))

    p.run()
    es.close()
    nc.finalize()
    return nc


def _colblocks(WT):
    K, M = WT.shape
    return np.ascontiguousarray(
        WT.reshape(K // 128, 128, M // 128, 128)
          .transpose(2, 1, 0, 3).reshape(M, K))


def _prep_l0ffn(inputs):
    return {
        "wgT": _colblocks(np.ascontiguousarray(inputs["L0_ffn_w_gate"].T)),
        "wuT": _colblocks(np.ascontiguousarray(inputs["L0_ffn_w_up"].T)),
        "wdT": _colblocks(np.ascontiguousarray(inputs["L0_ffn_w_down"].T)),
    }


def _prep_moe(inputs):
    b1 = np.asarray(inputs["L1_ffn_b1"])
    g1 = np.asarray(inputs["L1_ffn_g1"])
    be1 = np.asarray(inputs["L1_ffn_be1"])
    W2 = np.asarray(inputs["L1_ffn_W2"])
    w2cb = np.concatenate([_colblocks(np.ascontiguousarray(W2[e]))
                           for e in range(NEXP)], axis=0)
    return {
        "wr": np.ascontiguousarray(inputs["L1_ffn_Wr"]),
        "brv": np.ascontiguousarray(np.asarray(inputs["L1_ffn_br"]).reshape(NEXP, 1)),
        "w1n": np.ascontiguousarray(np.asarray(inputs["L1_ffn_W1"]).reshape(NEXP * D, D)),
        "w2n": np.ascontiguousarray(w2cb),
        "bge": np.ascontiguousarray(np.concatenate([b1, g1, be1], axis=1)),
        "b2n": np.ascontiguousarray(np.asarray(inputs["L1_ffn_b2"])),
    }


def _prep_l1attn(inputs):
    out = {}
    for i in range(2):
        for nm, key in (("wq", "L1_attn_wq"), ("wk", "L1_attn_wk"),
                        ("wv", "L1_attn_wv"), ("wo", "L1_attn_wo")):
            out[f"{nm}_l{i}"] = _colblocks(np.ascontiguousarray(
                np.asarray(inputs[key])[i]))
        out[f"w1_l{i}"] = _colblocks(np.ascontiguousarray(
            np.asarray(inputs["L1_attn_w1"])[i]))
        out[f"w2_l{i}"] = _colblocks(np.ascontiguousarray(
            np.asarray(inputs["L1_attn_w2"])[i]))
        for nm, key in (("bq", "L1_attn_bq"), ("bk", "L1_attn_bk"),
                        ("bv", "L1_attn_bv"), ("lng1", "L1_attn_ln1_g"),
                        ("lnb1", "L1_attn_ln1_b"), ("lng2", "L1_attn_ln2_g"),
                        ("lnb2", "L1_attn_ln2_b")):
            out[f"{nm}_l{i}"] = _chunked_vec(np.asarray(inputs[key])[i])
        out[f"b1_l{i}"] = _chunked_vec(np.asarray(inputs["L1_attn_b1"])[i])
    tab = _rotary_table()
    qt = np.empty((128, SEQ), np.float32)
    kt = np.empty((128, SEQ), np.float32)
    qt[0:32] = tab / 8.0
    qt[32:64] = 1.0 / 8.0
    qt[64:96] = tab / 8.0
    qt[96:128] = 1.0 / 8.0
    kt[0:32] = tab
    kt[32:64] = 1.0
    kt[64:96] = tab
    kt[96:128] = 1.0
    out["qtab"] = qt.astype(BFNP)
    out["ktab"] = kt.astype(BFNP)
    return out


def _prep_l0attn(inputs):
    Wqkv = np.asarray(inputs["L0_attn_W_qkv"])
    bqkv = np.asarray(inputs["L0_attn_b_qkv"])
    out = {}
    for i, nm in enumerate(("wq_a", "wk_a", "wv_a")):
        out[nm] = _colblocks(np.ascontiguousarray(Wqkv[i * D:(i + 1) * D].T))
    out["wo_a"] = _colblocks(np.ascontiguousarray(np.asarray(inputs["L0_attn_W_out"]).T))
    for i, nm in enumerate(("bq_a", "bk_a", "bv_a")):
        out[nm] = _chunked_vec(bqkv[i * D:(i + 1) * D])
    out["bo_a"] = _chunked_vec(np.asarray(inputs["L0_attn_b_out"]))
    return out


INPUT_KEYS = {
    "l0ffn": ("wgT", "wuT", "wdT"),
    "l0attn": ("wq_a", "wk_a", "wv_a", "wo_a", "bq_a", "bk_a", "bv_a", "bo_a"),
    "moe": ("wr", "brv", "w1n", "w2n", "bge", "b2n"),
    "l1attn": tuple(
        [f"{nm}_l{i}" for i in range(2)
         for nm in ("wq", "wk", "wv", "wo", "w1", "w2", "bq", "bk", "bv",
                    "lng1", "lnb1", "lng2", "lnb2", "b1")] + ["qtab", "ktab"]),
}

FUSED = True
_PROG_CACHE = {}


def _get_program(phase_names):
    key = tuple(phase_names)
    if key not in _PROG_CACHE:
        _PROG_CACHE[key] = build_program(key)
    return _PROG_CACHE[key]


def kernel(**inputs):
    x = np.asarray(inputs["L0_attn_x"])
    shared = {}
    shared.update(_prep_l0ffn(inputs))
    shared.update(_prep_l0attn(inputs))
    shared.update(_prep_moe(inputs))
    shared.update(_prep_l1attn(inputs))

    if FUSED:
        seq = [("l0ffn", "l0attn", "l0ffn", "moe", "l1attn", "moe")]
    else:
        seq = [("l0ffn", "l0attn", "l0ffn"), ("moe",), ("l1attn",), ("moe",)]

    cur = [np.ascontiguousarray(x[c * NB:(c + 1) * NB].reshape(TOKC, D).T)
           for c in range(NCORES)]
    for phases in seq:
        nc = _get_program(phases)
        keys = set()
        for ph in set(phases):
            keys.update(INPUT_KEYS[ph])
        in_maps = []
        for c in range(NCORES):
            m = {"xT": cur[c]}
            for k in keys:
                m[k] = shared[k]
            in_maps.append(m)
        res = run_bass_kernel_spmd(nc, in_maps, core_ids=list(range(NCORES)))
        cur = [np.ascontiguousarray(res.results[c]["yT"]) for c in range(NCORES)]
    outs = [cur[c].T.reshape(NB, SEQ, D) for c in range(NCORES)]
    return np.concatenate(outs, axis=0)


# version 14
# speedup vs baseline: 1.1834x; 1.1834x over previous
import numpy as np
import ml_dtypes
import concourse.bass as bass
import concourse.bacc as bacc_mod
import concourse.mybir as mybir
from concourse.bass_utils import run_bass_kernel_spmd

F32 = mybir.dt.float32
BF16 = mybir.dt.bfloat16
I32 = mybir.dt.int32
AF = mybir.ActivationFunctionType
ALU = mybir.AluOpType
AX = mybir.AxisListType
BFNP = ml_dtypes.bfloat16

NCORES = 8
D = 1024
NCH = 8
TOKC = 4096
SEQ = 2048
NB = 2
HID0 = 3072
MCH0 = 24
NHEAD = 16
HDIM = 64
TOPK = 614
EXP4 = 4096
NEXP = 16
LN_EPS = 1e-5
ENGINES = ("sync", "gpsimd", "tensor", "vector", "scalar")


def AFF(*terms, const=0):
    c = const
    pairs = []
    for v, k in terms:
        if isinstance(v, (int, np.integer)):
            c += int(v) * k
        else:
            pairs.append((v, k))
    return (c, pairs)


class Prog:

    def __init__(self, nc, es):
        self.nc = nc
        self.es = es
        self.sems = {}
        for name in ("dw", "dwA", "dwB", "dx", "ds", "pe", "dv", "ac", "pb"):
            self.sems[name] = es.enter_context(nc.semaphore(name))
        self.cnt = {k: 0 for k in self.sems}
        self.cur = None
        self.eng = None
        self.cur_last = None
        self.dry = False
        self.loops = []
        self.wregs = {}
        self.symrr = {}
        self.phases = []

    def _note(self, incs):
        for s, n in incs:
            self.cnt[s] += n

    def ins(self, engine, f, incs=()):
        self._note(incs)
        if self.cur == engine and not self.dry:
            instr = f(self.eng)
            for s, n in incs:
                instr = instr.then_inc(self.sems[s], n)
            self.cur_last = instr
            return instr
        return None

    def _wregs(self):
        if self.cur not in self.wregs:
            r1 = self.es.enter_context(self.eng.register(f"wr1_{self.cur}"))
            r2 = self.es.enter_context(self.eng.register(f"wr2_{self.cur}"))
            self.wregs[self.cur] = (r1, r2)
        return self.wregs[self.cur]

    def wait(self, engine, sem, val, loop=False):
        if self.cur != engine or self.dry:
            return
        if loop and self.loops:
            levels = []
            const = val
            for snap, deltas, it, peel in self.loops:
                d = deltas.get(sem, 0)
                if d:
                    levels.append((it, d))
                    const -= peel * d
            if levels:
                w1, w2 = self._wregs()
                eng = self.eng
                eng.reg_mul(w1, levels[0][0], levels[0][1])
                for itl, dl in levels[1:]:
                    eng.reg_mul(w2, itl, dl)
                    eng.reg_add(w1, w1, w2)
                if const:
                    eng.reg_add(w1, w1, const)
                eng.wait_ge(self.sems[sem], w1)
                return
        self.eng.wait_ge(self.sems[sem], val)

    def emit_loop(self, n, body_fn, peel=1):
        for i in range(min(peel, n)):
            body_fn(i)
        if n <= peel:
            return
        snap = dict(self.cnt)
        was_dry = self.dry
        self.dry = True
        body_fn(peel)
        self.dry = was_dry
        deltas = {s: self.cnt[s] - snap[s] for s in self.cnt
                  if self.cnt[s] != snap[s]}
        self.cnt = dict(snap)
        if self.dry:
            self.cnt = {s: snap[s] + deltas.get(s, 0) * (n - peel)
                        for s in self.cnt}
            return
        depth = len(self.loops)
        key = (self.cur, depth)
        if key not in self.wregs:
            self.wregs[key] = self.es.enter_context(
                self.eng.register(f"li{depth}_{self.cur}"))
        reg = self.wregs[key]
        nc_ = self.nc
        lid = nc_.next_id()
        ls, le = f"ml{lid}s_{self.cur}", f"ml{lid}e_{self.cur}"
        self.eng.reg_mov(reg, peel)
        self.eng.br(ls)
        eng_set = [self.eng.engine]
        with nc_.body(ls, valid_engines=eng_set):
            self.loops.append((snap, deltas, reg, peel))
            body_fn(reg)
            self.loops.pop()
            self.eng.reg_add(reg, reg, 1)
            self.eng.br_lt(reg, n, on_true=ls, on_false=le)
        nc_.switch_bb(le)
        self.cnt = {s: snap[s] + deltas.get(s, 0) * (n - peel)
                    for s in self.cnt}

    def symap(self, engine, base2d, aff, width, rows=None):
        const, pairs = aff
        if rows is not None:
            base2d = base2d[rows[0]:rows[1], :]
        if not pairs:
            return base2d[:, const:const + width]
        if self.cur != engine or self.dry:
            return None
        key = (engine, "symap")
        pool = self.wregs.setdefault(key, [])
        if not pool:
            for i in range(4):
                pool.append(self.es.enter_context(
                    self.eng.register(f"so{i}_{engine}")))
            self.wregs[key] = pool
        rr = self.symrr.get(engine, 0)
        reg = pool[rr % 3]
        tmp = pool[3]
        self.symrr[engine] = rr + 1
        eng = self.eng
        eng.reg_mul(reg, pairs[0][0], pairs[0][1])
        for v, k in pairs[1:]:
            eng.reg_mul(tmp, v, k)
            eng.reg_add(reg, reg, tmp)
        total_c = const + base2d.offset
        if total_c:
            eng.reg_add(reg, reg, total_c)
        pat = [list(base2d.ap[0]), [1, width]]
        return bass.AP(base2d.tensor, reg, pat)

    def symap3(self, engine, base_ap, aff, pat):
        const, pairs = aff
        if not pairs:
            return bass.AP(base_ap.tensor, base_ap.offset + const, pat)
        if self.cur != engine or self.dry:
            return None
        key = (engine, "symap")
        pool = self.wregs.setdefault(key, [])
        if not pool:
            for i in range(4):
                pool.append(self.es.enter_context(
                    self.eng.register(f"so{i}_{engine}")))
            self.wregs[key] = pool
        rr = self.symrr.get(engine, 0)
        reg = pool[rr % 3]
        tmp = pool[3]
        self.symrr[engine] = rr + 1
        eng = self.eng
        eng.reg_mul(reg, pairs[0][0], pairs[0][1])
        for v, k in pairs[1:]:
            eng.reg_mul(tmp, v, k)
            eng.reg_add(reg, reg, tmp)
        total_c = const + base_ap.offset
        if total_c:
            eng.reg_add(reg, reg, total_c)
        return bass.AP(base_ap.tensor, reg, pat)

    def wait_prev(self, engine, sem, val):
        if self.cur != engine or self.dry:
            return
        d = self.loops[-1][1].get(sem, 0) if self.loops else 0
        self.wait(engine, sem, val - d, loop=True)

    ESEM = {"vector": "dv", "scalar": "ac", "tensor": "pe", "gpsimd": "dv"}

    def selfsync(self, engine):
        sem = self.ESEM[engine]
        self.wait(engine, sem, self.cnt[sem], loop=bool(self.loops))

    def barrier(self):
        base = self.cnt["pb"]
        for e in ENGINES:
            self._note((("pb", 1),))
            if self.cur == e:
                if e in self.ESEM:
                    self.wait(e, self.ESEM[e], self.cnt[self.ESEM[e]])
                self.eng.nop().then_inc(self.sems["pb"], 1)
                self.cur_last = None
        for e in ENGINES:
            self.wait(e, "pb", base + len(ENGINES))

    def run(self):
        nc = self.nc
        snap = dict(self.cnt)
        results = []

        def run_engine(name, proxy):
            self.cnt = dict(snap)
            self.cur = name
            self.eng = proxy
            self.cur_last = None
            self.dry = False
            self.loops = []
            for body in self.phases:
                body()
            results.append(dict(self.cnt))

        with nc.Block() as block:
            @block.sync
            def _(e):
                run_engine("sync", e)

            @block.gpsimd
            def _(e):
                run_engine("gpsimd", e)

            @block.tensor
            def _(e):
                run_engine("tensor", e)

            @block.vector
            def _(e):
                run_engine("vector", e)

            @block.scalar
            def _(e):
                run_engine("scalar", e)
        for r in results[1:]:
            assert r == results[0], "nondeterministic phase bodies"


class Arena:

    def __init__(self, nc, es, name, nbytes):
        self.t = es.enter_context(nc.sbuf_tensor(name, [128, nbytes], mybir.dt.uint8))
        self.nbytes = nbytes
        self.off = 0

    def reset(self, off=0):
        self.off = off

    def alloc(self, cols, dtype):
        size = mybir.dt.size(dtype)
        self.off = (self.off + 31) // 32 * 32
        start = self.off
        self.off += cols * size
        assert self.off <= self.nbytes, f"arena overflow: {self.off} > {self.nbytes}"
        full = self.t[:].bitcast(dtype)
        return full[:, start // size: start // size + cols]


def _chunked_vec(v):
    n = v.shape[-1]
    return np.ascontiguousarray(v.reshape(n // 128, 128).T)


def _rotary_table():
    rot, max_freq = 32, 10.0
    dim_exp = rot // 2
    freqs = 2.0 ** np.linspace(0.0, np.log2(max_freq / 2.0), dim_exp)
    pos = 1.0 + np.arange(SEQ, dtype=np.float64)
    rad = pos[:, None] * np.concatenate([freqs, freqs]) * np.pi
    sinr = np.sin(rad)
    cosr = np.cos(rad)
    tab = np.where(np.arange(rot) % 2 == 0, sinr - cosr, sinr + cosr)
    return np.ascontiguousarray(tab.T.astype(np.float32))


def phase_xcopy(p, dst, src, sem="dx", queue="gpsimd"):
    def body():
        for c in range(NCH):
            p.ins(queue, lambda e, c=c: e.dma_start(
                out=dst[c * 128:(c + 1) * 128, :],
                in_=src[c * 128:(c + 1) * 128, :]), incs=((sem, 16),))
        p.wait(queue, sem, p.cnt[sem])
        p.barrier()
    return body


def phase_wcast(p, pairs):
    def body():
        for dst, src in pairs:
            rows = dst.shape[0]
            step = 512
            for r0 in range(0, rows, step):
                r1 = min(rows, r0 + step)
                p.ins("gpsimd", lambda e, r0=r0, r1=r1, dst=dst, src=src: e.dma_start(
                    out=dst[r0:r1, :], in_=src[r0:r1, :]), incs=(("dx", 16),))
        p.wait("gpsimd", "dx", p.cnt["dx"])
        p.barrier()
    return body


class LNBlock:
    def __init__(self, p, ar, T):
        self.p = p
        self.T = T
        nh = T // 512
        self.nh = nh
        self.x2t = ar.alloc(2 * 512, F32)
        self.rows = ar.alloc(4 * nh * 512, F32)
        self.bcin = ar.alloc(2 * nh * 512, F32)


def emit_ln(p, ln, consts, xblk_c, xn_c, ps, lnscale_bias, post=None):
    T, nh = ln.T, ln.nh
    ones1, ones2 = consts["ones1"], consts["ones2"]
    eps_c, neghalf_c = consts["eps"], consts["neghalf"]
    rows = ln.rows
    bcin = ln.bcin

    for j in range(nh):
        for c in range(NCH):
            p.ins("tensor", lambda e, c=c, j=j: e.matmul(
                ps[4][0:1, :], ones1, xblk_c(c)[:, j * 512:(j + 1) * 512],
                start=(c == 0), stop=(c == NCH - 1)),
                incs=(("pe", 1),) if c == NCH - 1 else ())
        p.wait("vector", "pe", p.cnt["pe"])
        p.ins("vector", lambda e, j=j: e.tensor_copy(
            rows[0:1, j * 512:(j + 1) * 512], ps[4][0:1, :]), incs=(("dv", 1),))
        p.wait("tensor", "dv", p.cnt["dv"])
        pe_sq0 = p.cnt["pe"]
        for c in range(NCH):
            if c >= 2:
                p.wait("scalar", "pe", pe_sq0 + c - 1)
            else:
                p.wait("scalar", "pe", pe_sq0)
            ac_before = p.cnt["ac"]
            p.ins("scalar", lambda e, c=c, j=j: e.activation(
                ln.x2t[:, (c % 2) * 512:(c % 2) * 512 + 512],
                xblk_c(c)[:, j * 512:(j + 1) * 512], AF.Square),
                incs=(("ac", 1),))
            p.wait("tensor", "ac", ac_before + 1)
            p.ins("tensor", lambda e, c=c: e.matmul(
                ps[4][0:1, :], ones1, ln.x2t[:, (c % 2) * 512:(c % 2) * 512 + 512],
                start=(c == 0), stop=(c == NCH - 1)),
                incs=(("pe", 1),))
        p.wait("vector", "pe", p.cnt["pe"])
        p.ins("vector", lambda e, j=j: e.tensor_copy(
            rows[0:1, (nh + j) * 512:(nh + j + 1) * 512], ps[4][0:1, :]), incs=(("dv", 1),))
        p.wait("tensor", "dv", p.cnt["dv"])

    mean = rows[0:1, 2 * nh * 512: 3 * nh * 512]
    tmpr = rows[0:1, 3 * nh * 512: 4 * nh * 512]
    sums = rows[0:1, 0: nh * 512]
    sumsq = rows[0:1, nh * 512: 2 * nh * 512]
    rstd = bcin[0:1, 0: nh * 512]
    mr = bcin[0:1, nh * 512: 2 * nh * 512]
    p.selfsync("vector")
    p.ins("vector", lambda e: e.tensor_scalar_mul(mean, sums, 1.0 / D), incs=(("dv", 1),))
    p.ins("vector", lambda e: e.tensor_scalar_mul(tmpr, sumsq, 1.0 / D), incs=(("dv", 1),))
    p.selfsync("vector")
    p.ins("vector", lambda e: e.tensor_tensor(sums, mean, mean, ALU.mult), incs=(("dv", 1),))
    p.selfsync("vector")
    p.ins("vector", lambda e: e.tensor_tensor(tmpr, tmpr, sums, ALU.subtract),
          incs=(("dv", 1),))
    p.wait("scalar", "dv", p.cnt["dv"])
    p.ins("scalar", lambda e: e.activation(tmpr, tmpr, AF.Ln, bias=eps_c[0:1, 0:1]),
          incs=(("ac", 1),))
    p.selfsync("scalar")
    p.ins("scalar", lambda e: e.activation(rstd, tmpr, AF.Exp,
                                           scale=neghalf_c[0:1, 0:1],
                                           bias=lnscale_bias[0:1, 0:1]),
          incs=(("ac", 1),))
    p.wait("vector", "ac", p.cnt["ac"])
    p.ins("vector", lambda e: e.tensor_tensor(mr, mean, rstd, ALU.mult),
          incs=(("dv", 1),))
    p.wait("tensor", "dv", p.cnt["dv"])

    for j in range(nh):
        p.ins("tensor", lambda e, j=j: e.matmul(
            ps[j][:, :], ones2, bcin[0:2, j * 512:(j + 1) * 512],
            start=True, stop=True))
    for j in range(nh):
        p.ins("tensor", lambda e, j=j: e.matmul(
            ps[2 + j][:, :], ones2, bcin[0:2, (nh + j) * 512:(nh + j + 1) * 512],
            start=True, stop=True), incs=(("pe", 1),) if j == nh - 1 else ())
    p.wait("vector", "pe", p.cnt["pe"])

    for c in range(NCH):
        for j in range(nh):
            p.selfsync("vector")
            p.ins("vector", lambda e, c=c, j=j: e.tensor_tensor(
                ln.x2t[:, 0:512], xblk_c(c)[:, j * 512:(j + 1) * 512],
                ps[j][:, :], ALU.mult), incs=(("dv", 1),))
            p.selfsync("vector")
            if post is not None:
                p.wait("vector", "ac", p.cnt["ac"])
            p.ins("vector", lambda e, c=c, j=j: e.tensor_tensor(
                xn_c(c, j), ln.x2t[:, 0:512],
                ps[2 + j][:, :], ALU.subtract),
                incs=(("dv", 1),))
            if post is not None:
                p.wait("scalar", "dv", p.cnt["dv"])
                post(c, j)
    p.wait("tensor", "dv", p.cnt["dv"])
    if post is not None:
        p.wait("tensor", "ac", p.cnt["ac"])


def phase_l0ffn(p, ar_mk, consts, xsc, wg_bf, wu_bf, wd_bf, ps):
    MT = 1024
    NMT = TOKC // MT
    nh = MT // 512

    def body():
        ar = ar_mk()
        xblk = ar.alloc(NCH * MT, F32)
        xn = ar.alloc(NCH * MT, BF16)
        ln = LNBlock(p, ar, MT)
        wgm = ar.alloc(2 * D, BF16)
        wum = ar.alloc(2 * D, BF16)
        wdo = ar.alloc(2 * MCH0 * 128, BF16)
        h_t = ar.alloc(MCH0 * MT, BF16)
        sg = ar.alloc(2 * MT, F32)

        def xblk_c(c):
            return xblk[:, c * MT:(c + 1) * MT]

        def xn_c(c, j):
            return xn[:, c * MT + j * 512: c * MT + (j + 1) * 512]

        p.ins("vector", lambda e: e.memset(ln.bcin[0:2, :], 0.0), incs=(("dv", 1),))

        def wcol(wbf, m):
            return wbf[m * 128:(m + 1) * 128, :]

        for mt in range(NMT):
            t0 = mt * MT
            dx0 = p.cnt["dx"]
            for c in range(NCH):
                p.ins("gpsimd", lambda e, c=c: e.dma_start(
                    out=xblk_c(c), in_=xsc[c * 128:(c + 1) * 128, t0:t0 + MT]),
                    incs=(("dx", 16),))
            for e in ("tensor", "vector", "scalar"):
                p.wait(e, "dx", dx0 + 16 * NCH)

            emit_ln(p, ln, consts, xblk_c, xn_c, ps, consts["ln127"])

            dv_gu0 = p.cnt["dv"]
            for m in range(MCH0):
                if m >= 2:
                    p.wait("sync", "dv", dv_gu0 + (m - 1) * nh)
                slot = "dwA" if m % 2 == 0 else "dwB"
                p.ins("sync", lambda e, m=m: e.dma_start(
                    out=wgm[:, (m % 2) * D:(m % 2) * D + D],
                    in_=wcol(wg_bf, m)), incs=((slot, 16),))
                p.ins("sync", lambda e, m=m: e.dma_start(
                    out=wum[:, (m % 2) * D:(m % 2) * D + D],
                    in_=wcol(wu_bf, m)), incs=((slot, 16),))
                bsel = (m % 2) * 4
                gb = [ps[bsel + 0], ps[bsel + 1]]
                ub = [ps[bsel + 2], ps[bsel + 3]]
                p.wait("tensor", slot, p.cnt[slot])
                if m >= 2:
                    p.wait("tensor", "dv", dv_gu0 + (m - 1) * nh)
                for k in range(NCH):
                    p.ins("tensor", lambda e, k=k, m=m: e.ldweights(
                        wgm[:, (m % 2) * D + k * 128:(m % 2) * D + k * 128 + 128]))
                    for j in range(nh):
                        p.ins("tensor", lambda e, k=k, m=m, j=j: e.matmul(
                            gb[j][:, :],
                            wgm[:, (m % 2) * D + k * 128:(m % 2) * D + k * 128 + 128],
                            xn_c(k, j),
                            start=(k == 0), stop=(k == NCH - 1)))
                    p.ins("tensor", lambda e, k=k, m=m: e.ldweights(
                        wum[:, (m % 2) * D + k * 128:(m % 2) * D + k * 128 + 128]))
                    for j in range(nh):
                        p.ins("tensor", lambda e, k=k, m=m, j=j: e.matmul(
                            ub[j][:, :],
                            wum[:, (m % 2) * D + k * 128:(m % 2) * D + k * 128 + 128],
                            xn_c(k, j),
                            start=(k == 0), stop=(k == NCH - 1)),
                            incs=(("pe", 1),) if (k == NCH - 1 and j == nh - 1) else ())
                p.wait("scalar", "pe", p.cnt["pe"])
                if m >= 2:
                    p.wait("scalar", "dv", dv_gu0 + (m - 1) * nh)
                for j in range(nh):
                    p.ins("scalar", lambda e, m=m, j=j: e.activation(
                        sg[:, (m % 2) * MT + j * 512:(m % 2) * MT + (j + 1) * 512],
                        gb[j][:, :], AF.Silu),
                        incs=(("ac", 1),))
                p.wait("vector", "ac", p.cnt["ac"])
                for j in range(nh):
                    p.ins("vector", lambda e, m=m, j=j: e.tensor_tensor(
                        h_t[:, m * MT + j * 512: m * MT + (j + 1) * 512],
                        sg[:, (m % 2) * MT + j * 512:(m % 2) * MT + (j + 1) * 512],
                        ub[j][:, :], ALU.mult),
                        incs=(("dv", 1),))
            p.wait("tensor", "dv", p.cnt["dv"])

            dv_do0 = p.cnt["dv"]
            for o in range(NCH):
                if o >= 2:
                    p.wait("sync", "dv", dv_do0 + (o - 1) * nh)
                else:
                    p.wait("sync", "dv", dv_do0)
                slot = "dwA" if o % 2 == 0 else "dwB"
                p.ins("sync", lambda e, o=o: e.dma_start(
                    out=wdo[:, (o % 2) * MCH0 * 128:((o % 2) + 1) * MCH0 * 128],
                    in_=wcol(wd_bf, o)), incs=((slot, 16),))
                bsel = (o % 2) * 4
                db = [ps[bsel + 0], ps[bsel + 1]]
                p.wait("tensor", slot, p.cnt[slot])
                if o >= 2:
                    p.wait("tensor", "dv", dv_do0 + (o - 1) * nh)
                for k in range(MCH0):
                    p.ins("tensor", lambda e, k=k, o=o: e.ldweights(
                        wdo[:, (o % 2) * MCH0 * 128 + k * 128:(o % 2) * MCH0 * 128 + k * 128 + 128]))
                    for j in range(nh):
                        p.ins("tensor", lambda e, k=k, o=o, j=j: e.matmul(
                            db[j][:, :],
                            wdo[:, (o % 2) * MCH0 * 128 + k * 128:(o % 2) * MCH0 * 128 + k * 128 + 128],
                            h_t[:, k * MT + j * 512: k * MT + (j + 1) * 512],
                            start=(k == 0), stop=(k == MCH0 - 1)),
                            incs=(("pe", 1),) if (k == MCH0 - 1 and j == nh - 1) else ())
                p.wait("vector", "pe", p.cnt["pe"])
                for j in range(nh):
                    p.ins("vector", lambda e, o=o, j=j: e.scalar_tensor_tensor(
                        xblk_c(o)[:, j * 512:(j + 1) * 512],
                        db[j][:, :], 0.5 / 127.0,
                        xblk_c(o)[:, j * 512:(j + 1) * 512],
                        op0=ALU.mult, op1=ALU.add),
                        incs=(("dv", 1),))
            p.wait("gpsimd", "dv", p.cnt["dv"])
            for c in range(NCH):
                p.ins("gpsimd", lambda e, c=c: e.dma_start(
                    out=xsc[c * 128:(c + 1) * 128, t0:t0 + MT], in_=xblk_c(c)),
                    incs=(("dx", 16),))
            p.wait("gpsimd", "dx", p.cnt["dx"])
            p.barrier()
    return body



QB = 256
NE_SEL = 5
SEL_LAM = (1.0, 1.0, 0.7, 0.5, 0.35, 0.25, 0.2, 0.15)
SEL_Q70 = 0.5244
SEL_DINV = 1.0 / (SEQ * 0.3477)


def phase_attention(p, ar_mk, consts, cfg, ps):
    sel = cfg["sel"]
    stream = cfg["stream"]
    ident = consts["ident"]
    identf = consts["identf"]
    ones1 = consts["ones1"]

    def body():
        ar = ar_mk()
        xn = ar.alloc(NCH * SEQ, BF16)
        ksb = ar.alloc(NCH * SEQ, BF16)
        vt = ar.alloc(16 * 1024, BF16)
        wq_s = ar.alloc(NCH * D, BF16)
        wkv_s = ar.alloc(NCH * D, BF16)
        qsb = ar.alloc(NCH * QB, BF16)
        att_sb = ar.alloc(NCH * QB, BF16)
        bq_t = ar.alloc(NCH, F32)
        bk_t = ar.alloc(NCH, F32)
        bv_t = ar.alloc(NCH, F32)
        bo_t = ar.alloc(NCH, F32) if cfg.get("bo") is not None else None
        lng_t = ar.alloc(NCH, F32) if cfg.get("lng") is not None else None
        lnb_t = ar.alloc(NCH, F32) if cfg.get("lnb") is not None else None
        qtab_t = ar.alloc(SEQ, BF16) if cfg.get("qtab") is not None else None
        ktab_t = ar.alloc(SEQ, BF16) if cfg.get("ktab") is not None else None
        bq_eff = ar.alloc(NCH, F32)
        xblk = ar.alloc(NCH * 512, F32)
        ln = LNBlock(p, ar, 512)
        lntmp = ar.alloc(512, F32)
        s_f = wkv_s.bitcast(F32)[:, 0:2 * SEQ]
        s_bfT = ar.alloc(2 * SEQ, BF16) if sel else None
        s_bf2 = wkv_s.bitcast(BF16)
        e_f = xblk[:, 0:SEQ]
        trash_bf = xblk.bitcast(BF16)[:, 2 * SEQ:3 * SEQ]
        trash2_bf = xblk.bitcast(BF16)[:, 3 * SEQ:4 * SEQ]
        p_t = ar.alloc(2 * SEQ, BF16)
        p_t2 = (p_t, s_bfT if sel else xblk.bitcast(BF16)[:, 2 * SEQ:4 * SEQ])
        pt_t = ar.alloc(2 * QB, BF16)
        wo_s = ar.alloc(2 * D, BF16)
        zb_sb = ar.alloc(QB, F32)
        t4_sb = ar.alloc(128, F32)
        selst = ar.alloc(80, F32)
        zcol = ar.alloc(8, F32)
        vtmp = ar.alloc(2 * 512, BF16)
        ktmp = xblk[:, 3 * 1024:4 * 1024]
        xr = ar.alloc(NCH * QB, F32)


        dw0 = p.cnt["dw"]
        nload = 0
        for nm, t in (("bq", bq_t), ("bk", bk_t), ("bv", bv_t), ("bo", bo_t),
                      ("lng", lng_t), ("lnb", lnb_t), ("qtab", qtab_t),
                      ("ktab", ktab_t)):
            if t is not None:
                p.ins("sync", lambda e, nm=nm, t=t: e.dma_start(out=t, in_=cfg[nm]),
                      incs=(("dw", 16),))
                nload += 1
        for e in ("vector", "scalar", "tensor"):
            p.wait(e, "dw", dw0 + 16 * nload)
        if cfg.get("qscale") is not None:
            p.ins("vector", lambda e: e.memset(consts["qsc"], float(cfg["qscale"])))
            p.ins("vector", lambda e: e.tensor_scalar_mul(
                bq_eff, bq_t, float(cfg["qscale"])), incs=(("dv", 1),))
        else:
            p.ins("vector", lambda e: e.tensor_copy(bq_eff, bq_t), incs=(("dv", 1),))
        p.ins("vector", lambda e: e.memset(ln.bcin[0:2, :], 0.0), incs=(("dv", 1),))
        p.wait("scalar", "dv", p.cnt["dv"])

        for b in range(NB):
            tb = b * SEQ

            for blk in range(4):
                t0 = tb + blk * 512
                dxa = p.cnt["dx"]
                for c in range(NCH):
                    p.ins("gpsimd", lambda e, c=c, t0=t0: e.dma_start(
                        out=xblk[:, c * 512:(c + 1) * 512],
                        in_=stream[c * 128:(c + 1) * 128, t0:t0 + 512]),
                        incs=(("dx", 16),))
                for e in ("tensor", "vector", "scalar"):
                    p.wait(e, "dx", dxa + 16 * NCH)

                def xblk_c(c):
                    return xblk[:, c * 512:(c + 1) * 512]

                if lng_t is None:
                    def xn_dst(c, j, blk=blk):
                        return xn[:, c * SEQ + blk * 512: c * SEQ + blk * 512 + 512]
                    emit_ln(p, ln, consts, xblk_c, xn_dst, ps, consts["zero"])
                else:
                    def xn_dst(c, j):
                        return lntmp[:, 0:512]
                    def post(c, j, blk=blk):
                        p.ins("scalar", lambda e, c=c: e.activation(
                            xn[:, c * SEQ + blk * 512: c * SEQ + blk * 512 + 512],
                            lntmp[:, 0:512], AF.Identity,
                            scale=lng_t[:, c:c + 1], bias=lnb_t[:, c:c + 1]),
                            incs=(("ac", 1),))
                    emit_ln(p, ln, consts, xblk_c, xn_dst, ps, consts["zero"],
                            post=post)
                p.wait("gpsimd", "dv", p.cnt["dv"])

            for wname, btile, isv in (("wv", bv_t, True), ("wk", bk_t, False)):
                p.wait("sync", "dw", p.cnt["dw"])
                dwp = p.cnt["dw"]
                if not isv:
                    p.wait("sync", "pe", p.cnt["pe"])
                p.wait("tensor", "ac", p.cnt["ac"])
                p.wait("tensor", "dv", p.cnt["dv"])
                for m in range(NCH):
                    p.ins("sync", lambda e, m=m, wname=wname: e.dma_start(
                        out=wkv_s[:, m * D:(m + 1) * D],
                        in_=cfg[wname][m * 128:(m + 1) * 128, :]),
                        incs=(("dw", 16),))
                p.wait("tensor", "dw", dwp + 16 * NCH)
                evd = []
                trd = []
                for m in range(NCH):
                    for tq in range(4):
                        si = m * 4 + tq
                        bank = ps[si % 2]
                        if si >= 2:
                            p.wait("tensor", evd[si - 2][0], evd[si - 2][1],
                                   loop=False)
                        for k in range(NCH):
                            if k == 0:
                                p.ins("tensor", lambda e, m=m: e.ldweights(
                                    wkv_s[:, m * D: m * D + 128]))
                            p.ins("tensor", lambda e, m=m, k=k, tq=tq, bank=bank:
                                  e.matmul(
                                      bank[:, :],
                                      wkv_s[:, m * D + k * 128: m * D + k * 128 + 128],
                                      xn[:, k * SEQ + tq * 512: k * SEQ + (tq + 1) * 512],
                                      start=(k == 0), stop=(k == NCH - 1)),
                                  incs=(("pe", 1),) if k == NCH - 1 else ())
                        pe_mm = p.cnt["pe"]
                        p.wait("scalar", "pe", pe_mm)
                        if isv:
                            if si >= 2:
                                p.wait("scalar", "pe", trd[si - 2][0])
                            p.ins("scalar", lambda e, m=m, bank=bank, si=si:
                                  e.activation(
                                      vtmp[:, (si % 2) * 512:(si % 2) * 512 + 512],
                                      bank[:, :], AF.Identity, bias=btile[:, m:m + 1]),
                                  incs=(("ac", 1),))
                            evd.append(("ac", p.cnt["ac"]))
                            p.wait("tensor", "ac", p.cnt["ac"])
                            if si >= 1:
                                p.wait("tensor", "dv", trd[si - 1][1])
                            psT = ps[2][:, :].bitcast(BF16)
                            for i in range(4):
                                p.ins("tensor", lambda e, i=i, si=si: e.transpose(
                                    psT[:, i * 128:(i + 1) * 128],
                                    vtmp[:, (si % 2) * 512 + i * 128:(si % 2) * 512 + (i + 1) * 128],
                                    ident),
                                    incs=(("pe", 1),) if i == 3 else ())
                            trd.append((p.cnt["pe"], None))
                            p.wait("vector", "pe", p.cnt["pe"])
                            for i in range(4):
                                kt = tq * 4 + i
                                p.ins("vector", lambda e, i=i, kt=kt, m=m: e.tensor_copy(
                                    vt[:, kt * 1024 + m * 128: kt * 1024 + (m + 1) * 128],
                                    psT[:, i * 128:(i + 1) * 128]),
                                    incs=(("dv", 1),) if i == 3 else ())
                            trd[si] = (trd[si][0], p.cnt["dv"])
                        else:
                            if ktab_t is None:
                                p.ins("scalar", lambda e, m=m, tq=tq, bank=bank:
                                      e.activation(
                                          ksb[:, m * SEQ + tq * 512: m * SEQ + (tq + 1) * 512],
                                          bank[:, :], AF.Identity, bias=btile[:, m:m + 1]),
                                      incs=(("ac", 1),))
                                evd.append(("ac", p.cnt["ac"]))
                            else:
                                if si >= 2:
                                    p.wait("scalar", "dv", evd[si - 2][1])
                                p.ins("scalar", lambda e, m=m, si=si, bank=bank:
                                      e.activation(
                                          ktmp[:, (si % 2) * 512:(si % 2) * 512 + 512],
                                          bank[:, :], AF.Identity, bias=btile[:, m:m + 1]),
                                      incs=(("ac", 1),))
                                p.wait("vector", "ac", p.cnt["ac"])
                                p.ins("vector", lambda e, m=m, tq=tq, si=si: e.tensor_tensor(
                                    ksb[:, m * SEQ + tq * 512: m * SEQ + (tq + 1) * 512],
                                    ktmp[:, (si % 2) * 512:(si % 2) * 512 + 512],
                                    ktab_t[:, tq * 512:(tq + 1) * 512], ALU.mult),
                                    incs=(("dv", 1),))
                                evd.append(("dv", p.cnt["dv"]))
            p.barrier()

            p.wait("sync", "dw", p.cnt["dw"])
            dwq = p.cnt["dw"]
            for m in range(NCH):
                p.ins("sync", lambda e, m=m: e.dma_start(
                    out=wq_s[:, m * D:(m + 1) * D],
                    in_=cfg["wq"][m * 128:(m + 1) * 128, :]), incs=(("dw", 16),))
            p.wait("tensor", "dw", p.cnt["dw"])

            def qb_body(qb):
                p.selfsync("vector")
                evq = []
                for m in range(NCH):
                    bank = ps[m % 2]
                    if m >= 2:
                        p.wait("tensor", evq[m - 2][0], evq[m - 2][1], loop=True)
                    else:
                        p.wait("tensor", "dv", p.cnt["dv"], loop=True)
                    for k in range(NCH):
                        if k == 0:
                            p.ins("tensor", lambda e, m=m: e.ldweights(
                                wq_s[:, m * D: m * D + 128]))
                        rhs = p.symap("tensor", xn,
                                      AFF((qb, QB), const=k * SEQ), QB)
                        p.ins("tensor", lambda e, m=m, k=k, bank=bank, rhs=rhs:
                              e.matmul(
                                  bank[:, 0:QB],
                                  wq_s[:, m * D + k * 128: m * D + k * 128 + 128],
                                  rhs,
                                  start=(k == 0), stop=(k == NCH - 1)),
                              incs=(("pe", 1),) if k == NCH - 1 else ())
                    p.wait("scalar", "pe", p.cnt["pe"], loop=True)
                    if cfg.get("qscale") is not None:
                        p.ins("scalar", lambda e, m=m, bank=bank: e.activation(
                            qsb[:, m * QB:(m + 1) * QB], bank[:, 0:QB], AF.Identity,
                            scale=consts["qsc"][0:128, 0:1],
                            bias=bq_eff[:, m:m + 1]), incs=(("ac", 1),))
                        evq.append(("ac", p.cnt["ac"]))
                    else:
                        p.wait("scalar", "dv", p.cnt["dv"], loop=True)
                        p.ins("scalar", lambda e, m=m, bank=bank: e.activation(
                            lntmp[:, 0:QB], bank[:, 0:QB], AF.Identity,
                            bias=bq_eff[:, m:m + 1]), incs=(("ac", 1),))
                        p.wait("vector", "ac", p.cnt["ac"], loop=True)
                        qt_ap = p.symap("vector", qtab_t, AFF((qb, QB)), QB)
                        p.ins("vector", lambda e, m=m, qt_ap=qt_ap: e.tensor_tensor(
                            qsb[:, m * QB:(m + 1) * QB], lntmp[:, 0:QB],
                            qt_ap, ALU.mult),
                            incs=(("dv", 1),))
                        evq.append(("dv", p.cnt["dv"]))
                p.wait("gpsimd", "dx", p.cnt["dx"], loop=True)
                p.wait("gpsimd", "dv", p.cnt["dv"], loop=True)
                dxr0 = p.cnt["dx"]
                for c in range(NCH):
                    src_ap = p.symap("gpsimd", stream[c * 128:(c + 1) * 128, :],
                                     AFF((qb, QB), const=tb), QB)
                    p.ins("gpsimd", lambda e, c=c, src_ap=src_ap: e.dma_start(
                        out=xr[:, c * QB:(c + 1) * QB], in_=src_ap),
                        incs=(("dx", 16),))

                p.mark_sf_free = None
                if sel:
                    _attn_sel_pipe(p, consts, cfg, ps, qsb, ksb, vt, s_bf2,
                                   e_f, trash_bf, trash2_bf, p_t2, pt_t,
                                   selst, zcol, ident, identf, t4_sb, zb_sb,
                                   att_sb)
                else:
                    sb3 = (s_bf2[:, 0:2 * SEQ], s_bf2[:, 2 * SEQ:4 * SEQ],
                           xblk.bitcast(BF16)[:, 0:2 * SEQ])
                    _attn_plain_pipe(p, consts, cfg, ps, qsb, ksb, vt, sb3,
                                     p_t2, t4_sb, zb_sb, att_sb)

                evo = []
                for o in range(NCH):
                    slot = "dwA" if o % 2 == 0 else "dwB"
                    if o >= 2:
                        p.wait("sync", evo[o - 2][0], evo[o - 2][1], loop=True)
                    else:
                        p.wait("sync", "dv", p.cnt["dv"], loop=True)
                    p.ins("sync", lambda e, o=o: e.dma_start(
                        out=wo_s[:, (o % 2) * D:(o % 2) * D + D],
                        in_=cfg["wo"][o * 128:(o + 1) * 128, :]),
                        incs=((slot, 16),))
                    bank = ps[o % 2]
                    p.wait("tensor", slot, p.cnt[slot], loop=True)
                    if o < 2:
                        p.wait("tensor", "dv", p.cnt["dv"], loop=True)
                    else:
                        p.wait("tensor", evo[o - 2][0], evo[o - 2][1], loop=True)
                    for k in range(NCH):
                        if k == 0:
                            p.ins("tensor", lambda e, o=o: e.ldweights(
                                wo_s[:, (o % 2) * D: (o % 2) * D + 128]))
                        p.ins("tensor", lambda e, o=o, k=k, bank=bank: e.matmul(
                            bank[:, 0:QB],
                            wo_s[:, (o % 2) * D + k * 128:(o % 2) * D + k * 128 + 128],
                            att_sb[:, k * QB:(k + 1) * QB],
                            start=(k == 0), stop=(k == NCH - 1)),
                            incs=(("pe", 1),) if k == NCH - 1 else ())
                    p.wait("vector", "pe", p.cnt["pe"], loop=True)
                    p.wait("vector", "dx", dxr0 + 16 * NCH, loop=True)
                    if bo_t is not None:
                        p.ins("vector", lambda e, o=o, bank=bank: e.scalar_tensor_tensor(
                            xr[:, o * QB:(o + 1) * QB], bank[:, 0:QB],
                            bo_t[:, o:o + 1], xr[:, o * QB:(o + 1) * QB],
                            op0=ALU.add, op1=ALU.add), incs=(("dv", 1),))
                    else:
                        p.ins("vector", lambda e, o=o, bank=bank: e.tensor_tensor(
                            xr[:, o * QB:(o + 1) * QB], bank[:, 0:QB],
                            xr[:, o * QB:(o + 1) * QB], ALU.add),
                            incs=(("dv", 1),))
                    evo.append(("dv", p.cnt["dv"]))
                p.wait("gpsimd", "dv", p.cnt["dv"], loop=True)
                for c in range(NCH):
                    dst_ap = p.symap("gpsimd", stream[c * 128:(c + 1) * 128, :],
                                     AFF((qb, QB), const=tb), QB)
                    p.ins("gpsimd", lambda e, c=c, dst_ap=dst_ap: e.dma_start(
                        out=dst_ap, in_=xr[:, c * QB:(c + 1) * QB]),
                        incs=(("dx", 16),))

            p.emit_loop(SEQ // QB, qb_body, peel=0)
            p.wait("gpsimd", "dx", p.cnt["dx"])
            p.barrier()
    return body


def _attn_inner(p, consts, cfg, ps, sel, hc, head, qsb, ksb, vt,
                s_f, s_bfT, e_f, trash_bf, trash2_bf, p_t, pt_t, selst, zcol,
                ident):
    KSEL = float(TOPK)
    ev = []
    pe_attv_prev = p.cnt["pe"]
    p.wait("tensor", "ac", p.cnt["ac"], loop=True)
    dv_prev_inst = p.mark_sf_free if (sel and p.mark_sf_free is not None) \
        else p.cnt["dv"]
    for qt in range(2):
        for kb in range(4):
            si = qt * 4 + kb
            bank = ps[si % 2]
            if si >= 2:
                p.wait("tensor", "ac", ev[si - 2], loop=True)
            lhs = qsb[head * 64:(head + 1) * 64,
                      hc * QB + qt * 128: hc * QB + qt * 128 + 128]
            rhs = ksb[head * 64:(head + 1) * 64,
                      hc * SEQ + kb * 512: hc * SEQ + (kb + 1) * 512]
            p.ins("tensor", lambda e, lhs=lhs, head=head: e.ldweights(
                lhs, tile_position=(head * 64, 0)))
            p.ins("tensor", lambda e, lhs=lhs, rhs=rhs, head=head, bank=bank:
                  e.matmul(bank[:, :], lhs, rhs, start=True, stop=True,
                           tile_position=(head * 64, 0)),
                  incs=(("pe", 1),))
            p.wait("scalar", "pe", p.cnt["pe"], loop=True)
            p.selfsync("scalar")
            if si == 0:
                p.wait("scalar", "dv", dv_prev_inst, loop=True)
            p.ins("scalar", lambda e, qt=qt, kb=kb, si=si, bank=bank: e.activation(
                s_f[:, qt * SEQ + kb * 512: qt * SEQ + (kb + 1) * 512],
                bank[:, :], AF.Identity,
                accum_out=selst[:, 24 + si:25 + si]), incs=(("ac", 1),))
            ev.append(p.cnt["ac"])
    p.wait("vector", "ac", p.cnt["ac"], loop=True)

    if sel:
        p.selfsync("vector")
        for qt in range(2):
            p.ins("vector", lambda e, qt=qt: e.tensor_copy(
                s_bfT[:, qt * SEQ:(qt + 1) * SEQ], s_f[:, qt * SEQ:(qt + 1) * SEQ]),
                incs=(("dv", 1),))
        p.mark_sf_free = p.cnt["dv"]
        p.selfsync("vector")
        for qt in range(2):
            p.ins("vector", lambda e, qt=qt: e.reduce_sum(
                selst[:, 16 + qt:17 + qt], selst[:, 24 + qt * 4:28 + qt * 4],
                axis=AX.X), incs=(("dv", 1),))
        p.selfsync("vector")
        p.ins("vector", lambda e: e.tensor_scalar_mul(
            selst[:, 16:18], selst[:, 16:18], 1.0 / SEQ), incs=(("dv", 1),))
        p.selfsync("vector")
        p.ins("vector", lambda e: e.tensor_scalar_mul(
            selst[:, 18:20], selst[:, 16:18], -1.0), incs=(("dv", 1),))
        p.wait("scalar", "dv", p.cnt["dv"], loop=True)
        for qt in range(2):
            p.selfsync("scalar")
            p.ins("scalar", lambda e, qt=qt: e.activation(
                trash2_bf[:, 0:SEQ], s_bfT[:, qt * SEQ:(qt + 1) * SEQ], AF.Abs,
                bias=selst[:, 18 + qt:19 + qt],
                accum_out=selst[:, 32 + qt:33 + qt]), incs=(("ac", 1),))
        p.wait("vector", "ac", p.cnt["ac"], loop=True)
        p.selfsync("vector")
        p.ins("vector", lambda e: e.tensor_scalar_mul(
            selst[:, 20:22], selst[:, 32:34], 1.2533141 / SEQ), incs=(("dv", 1),))
        p.selfsync("vector")
        p.ins("vector", lambda e: e.tensor_scalar_mul(
            selst[:, 22:24], selst[:, 20:22], SEL_DINV), incs=(("dv", 1),))
        p.ins("vector", lambda e: e.scalar_tensor_tensor(
            selst[:, 0:2], selst[:, 20:22], SEL_Q70, selst[:, 16:18],
            op0=ALU.mult, op1=ALU.add), incs=(("dv", 1),))
        for r in range(NE_SEL):
            p.selfsync("vector")
            p.ins("vector", lambda e: e.tensor_scalar_mul(
                selst[:, 38:39], selst[:, 1:2], -1.0), incs=(("dv", 1),))
            p.wait("scalar", "dv", p.cnt["dv"], loop=True)
            p.selfsync("scalar")
            p.ins("scalar", lambda e: e.activation(
                trash2_bf[:, 0:SEQ], s_bfT[:, SEQ:2 * SEQ], AF.Sign,
                bias=selst[:, 38:39],
                accum_out=selst[:, 3:4]), incs=(("ac", 1),))
            p.selfsync("vector")
            p.ins("vector", lambda e: e.tensor_scalar(
                trash_bf[:, 0:SEQ], s_bfT[:, 0:SEQ],
                selst[:, 0:1], 0.0, op0=ALU.is_ge, op1=ALU.add,
                accum_out=selst[:, 2:3]), incs=(("dv", 1),))
            p.selfsync("vector")
            p.ins("vector", lambda e: e.tensor_scalar(
                selst[:, 4:5], selst[:, 2:3], KSEL, None, op0=ALU.subtract),
                incs=(("dv", 1),))
            p.wait("vector", "ac", p.cnt["ac"], loop=True)
            p.ins("vector", lambda e: e.tensor_scalar(
                selst[:, 5:6], selst[:, 3:4], 0.5, SEQ / 2.0 - KSEL,
                op0=ALU.mult, op1=ALU.add), incs=(("dv", 1),))
            p.selfsync("vector")
            if r == 0:
                p.ins("vector", lambda e: e.tensor_copy(
                    selst[:, 6:8], selst[:, 0:2]), incs=(("dv", 1),))
                p.ins("vector", lambda e: e.tensor_tensor(
                    selst[:, 8:10], selst[:, 4:6], selst[:, 4:6], ALU.mult),
                    incs=(("dv", 1),))
            else:
                p.ins("vector", lambda e: e.tensor_tensor(
                    selst[:, 10:12], selst[:, 4:6], selst[:, 4:6], ALU.mult),
                    incs=(("dv", 1),))
                p.selfsync("vector")
                maskv = selst[:, 12:14].bitcast(mybir.dt.uint32)
                p.ins("vector", lambda e, maskv=maskv: e.tensor_tensor(
                    maskv, selst[:, 10:12], selst[:, 8:10], ALU.is_lt),
                    incs=(("dv", 1),))
                p.selfsync("vector")
                p.ins("vector", lambda e, maskv=maskv: e.copy_predicated(
                    selst[:, 6:8], maskv, selst[:, 0:2]),
                    incs=(("dv", 1),))
                p.selfsync("vector")
                p.ins("vector", lambda e, maskv=maskv: e.copy_predicated(
                    selst[:, 8:10], maskv, selst[:, 10:12]),
                    incs=(("dv", 1),))
            if r < NE_SEL - 1:
                p.selfsync("vector")
                p.ins("vector", lambda e: e.tensor_tensor(
                    selst[:, 4:6], selst[:, 4:6], selst[:, 22:24], ALU.mult),
                    incs=(("dv", 1),))
                p.selfsync("vector")
                p.ins("vector", lambda e, r=r: e.scalar_tensor_tensor(
                    selst[:, 0:2], selst[:, 4:6], SEL_LAM[r], selst[:, 0:2],
                    op0=ALU.mult, op1=ALU.add), incs=(("dv", 1),))
        p.selfsync("vector")
        p.ins("vector", lambda e: e.tensor_scalar_mul(
            selst[:, 14:16], selst[:, 6:8], -1.0), incs=(("dv", 1),))
        p.wait("scalar", "dv", p.cnt["dv"], loop=True)
        for qt in range(2):
            p.selfsync("scalar")
            p.ins("scalar", lambda e, qt=qt: e.activation(
                e_f[:, 0:SEQ], s_bfT[:, qt * SEQ:(qt + 1) * SEQ], AF.Exp,
                bias=selst[:, 14 + qt:15 + qt]), incs=(("ac", 1),))
            p.wait("vector", "ac", p.cnt["ac"], loop=True)
            p.wait("vector", "pe", pe_attv_prev, loop=True)
            p.ins("vector", lambda e, qt=qt, head=head: e.scalar_tensor_tensor(
                p_t[:, qt * SEQ:(qt + 1) * SEQ], s_bfT[:, qt * SEQ:(qt + 1) * SEQ],
                selst[:, 6 + qt:7 + qt], e_f[:, 0:SEQ],
                op0=ALU.is_ge, op1=ALU.mult,
                accum_out=zcol[:, head * 2 + qt:head * 2 + qt + 1]),
                incs=(("dv", 1),))
            p.wait("scalar", "dv", p.cnt["dv"], loop=True)
    else:
        p.selfsync("vector")
        for qt in range(2):
            p.ins("vector", lambda e, qt=qt: e.reduce_sum(
                selst[:, 16 + qt:17 + qt], selst[:, 24 + qt * 4:28 + qt * 4],
                axis=AX.X), incs=(("dv", 1),))
        p.selfsync("vector")
        p.ins("vector", lambda e: e.tensor_scalar_mul(
            selst[:, 14:16], selst[:, 16:18], -1.0 / SEQ), incs=(("dv", 1),))
        p.wait("scalar", "dv", p.cnt["dv"], loop=True)
        for qt in range(2):
            p.selfsync("scalar")
            p.wait("scalar", "pe", pe_attv_prev, loop=True)
            p.ins("scalar", lambda e, qt=qt, head=head: e.activation(
                p_t[:, qt * SEQ:(qt + 1) * SEQ], s_f[:, qt * SEQ:(qt + 1) * SEQ],
                AF.Exp, bias=selst[:, 14 + qt:15 + qt],
                accum_out=zcol[:, head * 2 + qt:head * 2 + qt + 1]),
                incs=(("ac", 1),))
        p.wait("vector", "ac", p.cnt["ac"], loop=True)
        p.ins("vector", lambda e: e.nop(), incs=(("dv", 1),))

    dv_p = p.cnt["dv"]
    p.wait("tensor", "dv", dv_p, loop=True)

    def emit_tr(kt):
        ktl = kt % 2
        psT = ps[2 + ktl][:, :].bitcast(BF16)
        for qt in range(2):
            p.ins("tensor", lambda e, qt=qt, kt=kt, psT=psT: e.transpose(
                psT[:, qt * 128:(qt + 1) * 128],
                p_t[:, qt * SEQ + kt * 128: qt * SEQ + (kt + 1) * 128], ident),
                incs=(("pe", 1),) if qt == 1 else ())
        trpe.append(p.cnt["pe"])

    trpe = []
    cpq = []
    emit_tr(0)
    for kt in range(16):
        ktl = kt % 2
        psT = ps[2 + ktl][:, :].bitcast(BF16)
        p.wait("vector", "pe", trpe[kt], loop=True)
        p.selfsync("vector")
        p.ins("vector", lambda e, ktl=ktl, psT=psT: e.tensor_copy(
            pt_t[:, ktl * QB:(ktl + 1) * QB], psT[:, 0:QB]),
            incs=(("dv", 1),))
        cpq.append(p.cnt["dv"])
        if kt < 15:
            emit_tr(kt + 1)
        p.wait("tensor", "dv", cpq[kt], loop=True)
        vtl = vt[:, kt * 1024 + hc * 128 + head * 64:
                 kt * 1024 + hc * 128 + head * 64 + 64]
        p.ins("tensor", lambda e, vtl=vtl, head=head: e.ldweights(
            vtl, tile_position=(0, head * 64)))
        p.ins("tensor", lambda e, vtl=vtl, ktl=ktl, head=head, kt=kt: e.matmul(
            ps[4][head * 64:(head + 1) * 64, 0:QB],
            vtl, pt_t[:, ktl * QB:(ktl + 1) * QB],
            start=(kt == 0), stop=(kt == 15),
            tile_position=(0, head * 64)),
            incs=(("pe", 1),))


def _attn_sel_pipe(p, consts, cfg, ps, qsb, ksb, vt, s_bf2, e_f, trash_bf,
                   trash2_bf, p_t2, pt_t, selst, zcol, ident, identf,
                   t4_sb, zb_sb, att_sb):
    KSEL = float(TOPK)
    NI = 16
    evt = {}
    ev8 = {}
    mmd = {}
    muad = {}
    absd = {}
    updd = {}
    prepd = {}
    sttd0 = {}
    find = {}
    attvd = {}
    pe0 = p.cnt["pe"]
    dv0 = p.cnt["dv"]

    def sbank(i):
        b = (i % 2) * 2 * SEQ
        return s_bf2[:, b:b + 2 * SEQ]

    def mm1(i, si):
        hc, head = i // 2, i % 2
        qt, kb = si // 4, si % 4
        bank = ps[si % 2]
        if si >= 2:
            p.wait("tensor", "ac", evt[(i, si - 2)], loop=True)
        elif i >= 1:
            p.wait("tensor", "ac", evt[(i - 1, si + 6)], loop=True)
        else:
            p.wait("tensor", "ac", p.cnt["ac"], loop=True)
        lhs = qsb[head * 64:(head + 1) * 64,
                  hc * QB + qt * 128: hc * QB + qt * 128 + 128]
        rhs = ksb[head * 64:(head + 1) * 64,
                  hc * SEQ + kb * 512: hc * SEQ + (kb + 1) * 512]
        p.ins("tensor", lambda e, lhs=lhs, head=head: e.ldweights(
            lhs, tile_position=(head * 64, 0)))
        p.ins("tensor", lambda e, lhs=lhs, rhs=rhs, head=head, bank=bank:
              e.matmul(bank[:, :], lhs, rhs, start=True, stop=True,
                       tile_position=(head * 64, 0)), incs=(("pe", 1),))
        mmd[(i, si)] = p.cnt["pe"]

    def evac1(i, si):
        sb = 40 * (i % 2)
        qt, kb = si // 4, si % 4
        bank = ps[si % 2]
        p.wait("scalar", "pe", mmd[(i, si)], loop=True)
        p.selfsync("scalar")
        if si == 0:
            if i >= 2:
                p.wait("scalar", "dv", find[i - 2], loop=True)
            else:
                p.wait("scalar", "dv", dv0, loop=True)
        sbk = sbank(i)
        p.ins("scalar", lambda e, qt=qt, kb=kb, si=si, bank=bank, sb=sb,
              sbk=sbk: e.activation(
                  sbk[:, qt * SEQ + kb * 512: qt * SEQ + (kb + 1) * 512],
                  bank[:, :], AF.Identity,
                  accum_out=selst[:, sb + 24 + si: sb + 25 + si]),
              incs=(("ac", 1),))
        evt[(i, si)] = p.cnt["ac"]
        if si == 7:
            ev8[i] = p.cnt["ac"]

    def sevac_a(i):
        mm1(i, 0)
        mm1(i, 1)
        evac1(i, 0)
        mm1(i, 2)
        evac1(i, 1)
        mm1(i, 3)
        evac1(i, 2)

    def sevac_b(i):
        mm1(i, 4)
        evac1(i, 3)
        mm1(i, 5)
        evac1(i, 4)
        mm1(i, 6)
        evac1(i, 5)
        mm1(i, 7)
        evac1(i, 6)
        evac1(i, 7)

    def muA(i):
        sb = 40 * (i % 2)
        p.wait("vector", "ac", ev8[i], loop=True)
        p.selfsync("vector")
        for qt in range(2):
            p.ins("vector", lambda e, qt=qt, sb=sb: e.reduce_sum(
                selst[:, sb + 16 + qt: sb + 17 + qt],
                selst[:, sb + 24 + qt * 4: sb + 28 + qt * 4],
                axis=AX.X), incs=(("dv", 1),))
        p.selfsync("vector")
        p.ins("vector", lambda e, sb=sb: e.tensor_scalar_mul(
            selst[:, sb + 16: sb + 18], selst[:, sb + 16: sb + 18],
            1.0 / SEQ), incs=(("dv", 1),))
        p.selfsync("vector")
        p.ins("vector", lambda e, sb=sb: e.tensor_scalar_mul(
            selst[:, sb + 18: sb + 20], selst[:, sb + 16: sb + 18], -1.0),
            incs=(("dv", 1),))
        muad[i] = p.cnt["dv"]

    def absA(i):
        sb = 40 * (i % 2)
        sbk = sbank(i)
        p.wait("scalar", "dv", muad[i], loop=True)
        for qt in range(2):
            p.selfsync("scalar")
            p.ins("scalar", lambda e, qt=qt, sb=sb, sbk=sbk: e.activation(
                trash2_bf[:, 0:SEQ // 2], sbk[:, qt * SEQ: qt * SEQ + SEQ // 2],
                AF.Abs, bias=selst[:, sb + 18 + qt: sb + 19 + qt],
                accum_out=selst[:, sb + 32 + qt: sb + 33 + qt]),
                incs=(("ac", 1),))
        absd[i] = p.cnt["ac"]

    def sigt(i):
        sb = 40 * (i % 2)
        p.wait("vector", "ac", absd[i], loop=True)
        p.selfsync("vector")
        p.ins("vector", lambda e, sb=sb: e.tensor_scalar_mul(
            selst[:, sb + 20: sb + 22], selst[:, sb + 32: sb + 34],
            2.0 * 1.2533141 / SEQ), incs=(("dv", 1),))
        p.selfsync("vector")
        p.ins("vector", lambda e, sb=sb: e.tensor_scalar_mul(
            selst[:, sb + 22: sb + 24], selst[:, sb + 20: sb + 22], SEL_DINV),
            incs=(("dv", 1),))
        p.ins("vector", lambda e, sb=sb: e.scalar_tensor_tensor(
            selst[:, sb + 0: sb + 2], selst[:, sb + 20: sb + 22], SEL_Q70,
            selst[:, sb + 16: sb + 18], op0=ALU.mult, op1=ALU.add),
            incs=(("dv", 1),))
        updd[i] = p.cnt["dv"]

    def rnd(i, r):
        sb = 40 * (i % 2)
        sbk = sbank(i)
        W = SEQ // 2 if r < 2 else SEQ
        p.wait("scalar", "dv", updd[i], loop=True)
        p.selfsync("scalar")
        p.ins("scalar", lambda e, sb=sb, sbk=sbk, W=W: e.activation(
            trash2_bf[:, 0:W], sbk[:, SEQ:SEQ + W], AF.Sign,
            scale=consts["negone"][0:128, 0:1],
            bias=selst[:, sb + 1: sb + 2],
            accum_out=selst[:, sb + 3: sb + 4]), incs=(("ac", 1),))
        p.selfsync("vector")
        p.ins("vector", lambda e, sb=sb, sbk=sbk, W=W: e.tensor_scalar(
            trash_bf[:, 0:W], sbk[:, 0:W],
            selst[:, sb + 0: sb + 1], 0.0, op0=ALU.is_ge, op1=ALU.add,
            accum_out=selst[:, sb + 2: sb + 3]), incs=(("dv", 1),))
        p.selfsync("vector")
        if r < 2:
            p.ins("vector", lambda e, sb=sb: e.tensor_scalar(
                selst[:, sb + 4: sb + 5], selst[:, sb + 2: sb + 3], 2.0,
                -KSEL, op0=ALU.mult, op1=ALU.add), incs=(("dv", 1),))
        else:
            p.ins("vector", lambda e, sb=sb: e.tensor_scalar(
                selst[:, sb + 4: sb + 5], selst[:, sb + 2: sb + 3], KSEL,
                None, op0=ALU.subtract), incs=(("dv", 1),))
        p.wait("vector", "ac", p.cnt["ac"], loop=True)
        p.ins("vector", lambda e, sb=sb, W=W: e.tensor_scalar(
            selst[:, sb + 5: sb + 6], selst[:, sb + 3: sb + 4],
            -0.5 * SEQ / W, SEQ / 2.0 - KSEL, op0=ALU.mult, op1=ALU.add),
            incs=(("dv", 1),))
        p.selfsync("vector")
        if r < 2:
            pass
        elif r == 2:
            p.ins("vector", lambda e, sb=sb: e.tensor_copy(
                selst[:, sb + 6: sb + 8], selst[:, sb + 0: sb + 2]),
                incs=(("dv", 1),))
            p.ins("vector", lambda e, sb=sb: e.tensor_tensor(
                selst[:, sb + 8: sb + 10], selst[:, sb + 4: sb + 6],
                selst[:, sb + 4: sb + 6], ALU.mult), incs=(("dv", 1),))
        else:
            p.ins("vector", lambda e, sb=sb: e.tensor_tensor(
                selst[:, sb + 10: sb + 12], selst[:, sb + 4: sb + 6],
                selst[:, sb + 4: sb + 6], ALU.mult), incs=(("dv", 1),))
            p.selfsync("vector")
            maskv = selst[:, sb + 12: sb + 14].bitcast(mybir.dt.uint32)
            p.ins("vector", lambda e, maskv=maskv, sb=sb: e.tensor_tensor(
                maskv, selst[:, sb + 10: sb + 12], selst[:, sb + 8: sb + 10],
                ALU.is_lt), incs=(("dv", 1),))
            p.selfsync("vector")
            p.ins("vector", lambda e, maskv=maskv, sb=sb: e.copy_predicated(
                selst[:, sb + 6: sb + 8], maskv, selst[:, sb + 0: sb + 2]),
                incs=(("dv", 1),))
            p.selfsync("vector")
            p.ins("vector", lambda e, maskv=maskv, sb=sb: e.copy_predicated(
                selst[:, sb + 8: sb + 10], maskv,
                selst[:, sb + 10: sb + 12]), incs=(("dv", 1),))
        if r < NE_SEL - 1:
            p.selfsync("vector")
            p.ins("vector", lambda e, sb=sb: e.tensor_tensor(
                selst[:, sb + 4: sb + 6], selst[:, sb + 4: sb + 6],
                selst[:, sb + 22: sb + 24], ALU.mult), incs=(("dv", 1),))
            p.selfsync("vector")
            p.ins("vector", lambda e, r=r, sb=sb: e.scalar_tensor_tensor(
                selst[:, sb + 0: sb + 2], selst[:, sb + 4: sb + 6],
                SEL_LAM[r], selst[:, sb + 0: sb + 2],
                op0=ALU.mult, op1=ALU.add), incs=(("dv", 1),))
        updd[i] = p.cnt["dv"]

    def finals_prep(i):
        sb = 40 * (i % 2)
        p.selfsync("vector")
        p.ins("vector", lambda e, sb=sb: e.tensor_scalar_mul(
            selst[:, sb + 14: sb + 16], selst[:, sb + 6: sb + 8], -1.0),
            incs=(("dv", 1),))
        prepd[i] = p.cnt["dv"]

    def finals_qt(i, qt):
        sb = 40 * (i % 2)
        sbk = sbank(i)
        pt = p_t2[i % 2]
        head = i % 2
        p.wait("scalar", "dv", prepd[i] if qt == 0 else sttd0[i], loop=True)
        p.selfsync("scalar")
        p.ins("scalar", lambda e, qt=qt, sb=sb, sbk=sbk: e.activation(
            e_f[:, 0:SEQ], sbk[:, qt * SEQ:(qt + 1) * SEQ], AF.Exp,
            bias=selst[:, sb + 14 + qt: sb + 15 + qt]), incs=(("ac", 1),))
        p.wait("vector", "ac", p.cnt["ac"], loop=True)
        p.wait("vector", "pe", attvd.get(i - 2, pe0), loop=True)
        p.selfsync("vector")
        p.ins("vector", lambda e, qt=qt, head=head, sb=sb, sbk=sbk, pt=pt:
              e.scalar_tensor_tensor(
                  pt[:, qt * SEQ:(qt + 1) * SEQ],
                  sbk[:, qt * SEQ:(qt + 1) * SEQ],
                  selst[:, sb + 6 + qt: sb + 7 + qt], e_f[:, 0:SEQ],
                  op0=ALU.is_ge, op1=ALU.mult,
                  accum_out=zcol[:, head * 2 + qt: head * 2 + qt + 1]),
              incs=(("dv", 1),))
        if qt == 0:
            sttd0[i] = p.cnt["dv"]
        else:
            find[i] = p.cnt["dv"]

    def attv(i):
        hc, head = i // 2, i % 2
        pt = p_t2[i % 2]
        dv_p = p.cnt["dv"]
        p.wait("tensor", "dv", dv_p, loop=True)
        trpe = []
        cpq = []

        def emit_tr(k):
            psT = ps[2 + k % 2][:, :].bitcast(BF16)
            for kh in range(2):
                kt = 2 * k + kh
                for qt in range(2):
                    p.ins("tensor", lambda e, qt=qt, kt=kt, kh=kh, psT=psT,
                          pt=pt: e.transpose(
                              psT[:, kh * QB + qt * 128: kh * QB + (qt + 1) * 128],
                              pt[:, qt * SEQ + kt * 128: qt * SEQ + (kt + 1) * 128],
                              ident), incs=(("pe", 1),)
                          if (kh == 1 and qt == 1) else ())
            trpe.append(p.cnt["pe"])

        emit_tr(0)
        mmk = []
        for k in range(8):
            psT = ps[2 + k % 2][:, :].bitcast(BF16)
            p.wait("vector", "pe", mmk[k - 1] if k >= 1 else trpe[0],
                   loop=True)
            p.selfsync("vector")
            p.ins("vector", lambda e, psT=psT: e.tensor_copy(
                pt_t[:, 0:2 * QB], psT[:, 0:2 * QB]),
                incs=(("dv", 1),))
            cpq.append(p.cnt["dv"])
            if k < 7:
                emit_tr(k + 1)
            p.wait("tensor", "dv", cpq[k], loop=True)
            for kh in range(2):
                kt = 2 * k + kh
                vtl = vt[:, kt * 1024 + hc * 128 + head * 64:
                         kt * 1024 + hc * 128 + head * 64 + 64]
                p.ins("tensor", lambda e, vtl=vtl, head=head: e.ldweights(
                    vtl, tile_position=(0, head * 64)))
                p.ins("tensor", lambda e, vtl=vtl, kh=kh, head=head, kt=kt:
                      e.matmul(
                          ps[4][head * 64:(head + 1) * 64, 0:QB],
                          vtl, pt_t[:, kh * QB:(kh + 1) * QB],
                          start=(kt == 0), stop=(kt == 15),
                          tile_position=(0, head * 64)),
                      incs=(("pe", 1),))
            mmk.append(p.cnt["pe"])
        attvd[i] = p.cnt["pe"]

    sevac_a(0)
    sevac_b(0)
    muA(0)
    absA(0)
    sigt(0)
    for i in range(NI):
        rnd(i, 0)
        if i >= 1:
            finals_qt(i - 1, 0)
        rnd(i, 1)
        if i >= 1:
            finals_qt(i - 1, 1)
        rnd(i, 2)
        if i + 1 < NI:
            sevac_a(i + 1)
        if i >= 1:
            attv(i - 1)
            if (i - 1) % 2 == 1:
                _zb_and_evac(p, consts, ps, (i - 1) // 2, zcol, t4_sb,
                             zb_sb, att_sb, identf)
        rnd(i, 3)
        if i + 1 < NI:
            sevac_b(i + 1)
            muA(i + 1)
        rnd(i, 4)
        if i + 1 < NI:
            absA(i + 1)
        finals_prep(i)
        if i + 1 < NI:
            sigt(i + 1)
    finals_qt(NI - 1, 0)
    finals_qt(NI - 1, 1)
    attv(NI - 1)
    _zb_and_evac(p, consts, ps, (NI - 1) // 2, zcol, t4_sb, zb_sb, att_sb,
                 identf)


def _attn_plain_pipe(p, consts, cfg, ps, qsb, ksb, vt, sb3, p_t2, t4_sb,
                     zb_sb, att_sb):
    NHC = NCH
    evt = {}
    mmd = {}
    expi = {}
    expall = {}
    attvd = {}
    ttd = {}
    zbed = {}
    pe0 = p.cnt["pe"]
    ac0 = p.cnt["ac"]
    dv0 = p.cnt["dv"]
    zr = t4_sb.bitcast(BF16)

    p.selfsync("vector")
    p.ins("vector", lambda e: e.memset(zr[:, :], 0.0), incs=(("dv", 1),))

    def sbank(hc, h):
        return sb3[(hc * 2 + h) % 3]

    def mmS(hc, h, kt):
        bank = ps[h * 2 + (kt // 2) % 2]
        col = (kt % 2) * QB
        kk = kt // 2
        if kk >= 2:
            sem, cnt = evt[(hc, h, kk - 2)]
            p.wait("tensor", sem, cnt, loop=True)
        elif hc >= 1:
            sem, cnt = evt[(hc - 1, h, 6 + kk)]
            p.wait("tensor", sem, cnt, loop=True)
        else:
            p.wait("tensor", "ac", p.cnt["ac"], loop=True)
            p.wait("tensor", "dv", p.cnt["dv"], loop=True)
        lhs = ksb[h * 64:(h + 1) * 64,
                  hc * SEQ + kt * 128: hc * SEQ + (kt + 1) * 128]
        rhs = qsb[h * 64:(h + 1) * 64, hc * QB: hc * QB + QB]
        p.ins("tensor", lambda e, lhs=lhs, h=h: e.ldweights(
            lhs, tile_position=(h * 64, 0)))
        p.ins("tensor", lambda e, lhs=lhs, rhs=rhs, h=h, bank=bank, col=col:
              e.matmul(bank[:, col:col + QB], lhs, rhs, start=True, stop=True,
                       tile_position=(h * 64, 0)), incs=(("pe", 1),))
        mmd[(hc, h, kt)] = p.cnt["pe"]

    def evac(hc, h, k):
        bank = ps[h * 2 + k % 2]
        sbk = sbank(hc, h)
        use_act = ((h * 8 + k) % 3 == 2)
        eng = "scalar" if use_act else "vector"
        p.wait(eng, "pe", mmd[(hc, h, k * 2 + 1)], loop=True)
        p.selfsync(eng)
        if k == 0:
            j = hc * 2 + h - 3
            tgt = expi[(j // 2, j % 2)] if j >= 0 else ac0
            p.wait("scalar", "ac", tgt, loop=True)
            p.wait("vector", "ac", tgt, loop=True)
        if use_act:
            p.ins("scalar", lambda e, k=k, bank=bank, sbk=sbk: e.activation(
                sbk[:, k * 512:(k + 1) * 512], bank[:, :], AF.Identity),
                incs=(("ac", 1),))
            evt[(hc, h, k)] = ("ac", p.cnt["ac"])
        else:
            p.ins("vector", lambda e, k=k, bank=bank, sbk=sbk: e.tensor_copy(
                sbk[:, k * 512:(k + 1) * 512], bank[:, :]),
                incs=(("dv", 1),))
            evt[(hc, h, k)] = ("dv", p.cnt["dv"])

    def sev(hc):
        for k in range(8):
            mmS(hc, 0, 2 * k)
            mmS(hc, 1, 2 * k)
            mmS(hc, 0, 2 * k + 1)
            mmS(hc, 1, 2 * k + 1)
            evac(hc, 0, k)
            evac(hc, 1, k)

    def exps(hc):
        p.wait("scalar", "pe", attvd.get(hc - 1, pe0), loop=True)
        for h in range(2):
            sbk = sbank(hc, h)
            Eb = p_t2[h]
            for k in range(8):
                sem, cnt = evt[(hc, h, k)]
                if sem == "dv":
                    p.wait("scalar", "dv", cnt, loop=True)
                p.selfsync("scalar")
                p.ins("scalar", lambda e, k=k, sbk=sbk, Eb=Eb: e.activation(
                    Eb[:, k * 512:(k + 1) * 512],
                    sbk[:, k * 512:(k + 1) * 512], AF.Exp),
                    incs=(("ac", 1),))
            expi[(hc, h)] = p.cnt["ac"]
        expall[hc] = p.cnt["ac"]

    def attvz(hc):
        p.wait("tensor", "dv", ttd.get(hc - 1, dv0), loop=True)
        p.wait("tensor", "ac", zbed.get(hc - 1, ac0), loop=True)
        p.wait("tensor", "ac", expall[hc], loop=True)
        for kt in range(16):
            for h in range(2):
                vtl = vt[:, kt * 1024 + hc * 128 + h * 64:
                         kt * 1024 + hc * 128 + h * 64 + 64]
                Eb = p_t2[h]
                out = ps[4 + h][h * 64:(h + 1) * 64, 0:QB]
                p.ins("tensor", lambda e, vtl=vtl, h=h: e.ldweights(
                    vtl, tile_position=(0, h * 64)))
                p.ins("tensor", lambda e, vtl=vtl, h=h, kt=kt, Eb=Eb, out=out:
                      e.matmul(out, vtl, Eb[:, kt * QB:(kt + 1) * QB],
                               start=(kt == 0), stop=(kt == 15),
                               tile_position=(0, h * 64)),
                      incs=(("pe", 1),))
            for h in range(2):
                Eb = p_t2[h]
                zout = ps[6 + h][h * 64:h * 64 + 64, 0:QB]
                p.ins("tensor", lambda e, h=h: e.ldweights(
                    consts["onesz"], tile_position=(0, h * 64)))
                p.ins("tensor", lambda e, h=h, kt=kt, Eb=Eb, zout=zout:
                      e.matmul(zout, consts["onesz"],
                               Eb[:, kt * QB:(kt + 1) * QB],
                               start=(kt == 0), stop=(kt == 15),
                               tile_position=(0, h * 64)),
                      incs=(("pe", 1),))
        attvd[hc] = p.cnt["pe"]

    def zb(hc):
        p.wait("vector", "pe", attvd[hc], loop=True)
        p.selfsync("vector")

        def _recip(e, out, in_):
            with p.nc.allow_low_precision(reason="bf16 Zinv broadcast"):
                return e.reciprocal(out, in_)
        p.ins("vector", lambda e: _recip(
            e, zr[0:1, 0:QB], ps[6][0:1, 0:QB]), incs=(("dv", 1),))
        p.ins("vector", lambda e: _recip(
            e, zr[64:65, 0:QB], ps[7][64:65, 0:QB]), incs=(("dv", 1),))
        p.wait("tensor", "dv", p.cnt["dv"], loop=True)
        p.ins("tensor", lambda e: e.matmul(
            ps[6][0:64, QB:2 * QB], consts["oneb0"], zr[:, 0:QB],
            start=True, stop=True, tile_position=(0, 0)), incs=(("pe", 1),))
        p.ins("tensor", lambda e: e.matmul(
            ps[7][64:128, QB:2 * QB], consts["oneb64"], zr[:, 0:QB],
            start=True, stop=True, tile_position=(0, 64)), incs=(("pe", 1),))
        p.wait("scalar", "pe", p.cnt["pe"], loop=True)
        p.selfsync("scalar")
        p.ins("scalar", lambda e: e.activation(
            zb_sb[0:64, 0:QB], ps[6][0:64, QB:2 * QB], AF.Identity),
            incs=(("ac", 1),))
        p.ins("scalar", lambda e: e.activation(
            zb_sb[64:128, 0:QB], ps[7][64:128, QB:2 * QB], AF.Identity),
            incs=(("ac", 1),))
        zbed[hc] = p.cnt["ac"]
        p.wait("vector", "ac", p.cnt["ac"], loop=True)
        p.selfsync("vector")
        p.ins("vector", lambda e, hc=hc: e.tensor_tensor(
            att_sb[0:64, hc * QB:(hc + 1) * QB], ps[4][0:64, 0:QB],
            zb_sb[0:64, 0:QB], ALU.mult), incs=(("dv", 1),))
        p.ins("vector", lambda e, hc=hc: e.tensor_tensor(
            att_sb[64:128, hc * QB:(hc + 1) * QB], ps[5][64:128, 0:QB],
            zb_sb[64:128, 0:QB], ALU.mult), incs=(("dv", 1),))
        ttd[hc] = p.cnt["dv"]

    sev(0)
    for hc in range(NHC):
        exps(hc)
        if hc + 1 < NHC:
            sev(hc + 1)
        attvz(hc)
        zb(hc)


def _zb_and_evac(p, consts, ps, hc, zcol, t4_sb, zb_sb, att_sb, identf):
    p.wait("vector", "ac", p.cnt["ac"], loop=True)
    p.selfsync("vector")
    p.ins("vector", lambda e: e.reciprocal(zcol[:, 4:8], zcol[:, 0:4]),
          incs=(("dv", 1),))
    p.wait("tensor", "dv", p.cnt["dv"], loop=True)
    p.wait("tensor", "ac", p.cnt["ac"], loop=True)
    p.ins("tensor", lambda e: e.transpose(
        ps[5][0:4, 0:128], zcol[:, 4:8], identf), incs=(("pe", 1),))
    p.wait("vector", "pe", p.cnt["pe"], loop=True)
    p.selfsync("vector")
    p.ins("vector", lambda e: e.tensor_copy(
        t4_sb[0:4, 0:128], ps[5][0:4, 0:128]), incs=(("dv", 1),))
    p.wait("tensor", "dv", p.cnt["dv"], loop=True)
    for qt in range(2):
        p.ins("tensor", lambda e, qt=qt: e.matmul(
            ps[5][:, 256 + qt * 128: 256 + (qt + 1) * 128],
            consts["lq"][0:4, qt * 128:(qt + 1) * 128],
            t4_sb[0:4, 0:128], start=True, stop=True),
            incs=(("pe", 1),) if qt == 1 else ())
    p.wait("scalar", "pe", p.cnt["pe"], loop=True)
    p.selfsync("scalar")
    p.ins("scalar", lambda e: e.activation(
        zb_sb[:, 0:QB], ps[5][:, 256:256 + QB], AF.Identity), incs=(("ac", 1),))
    p.wait("vector", "ac", p.cnt["ac"], loop=True)
    p.selfsync("vector")
    p.ins("vector", lambda e, hc=hc: e.tensor_tensor(
        att_sb[:, hc * QB:(hc + 1) * QB], ps[4][:, 0:QB], zb_sb[:, 0:QB],
        ALU.mult), incs=(("dv", 1),))
    p.wait("tensor", "dv", p.cnt["dv"], loop=True)



def phase_moe(p, ar_mk, consts, xsc, dram, ps):
    HALF = 2048
    ones1 = consts["ones1"]
    ones2 = consts["ones2"]
    identf = consts["identf"]
    ident = consts["ident"]

    def body():
        ar = ar_mk()
        xn = ar.alloc(NCH * HALF, BF16)
        acc = ar.alloc(NCH * HALF, BF16)
        w1s = ar.alloc(NCH * D, BF16)
        w2s = ar.alloc(NCH * D, BF16)
        h_sb = ar.alloc(8 * D, BF16)
        hn_q = ar.alloc(4 * D, BF16)
        trash = ar.alloc(D, BF16)
        hT = ar.alloc(NCH * 512, BF16)
        xblk = ar.alloc(NCH * 512, F32)
        hn_q2 = xblk.bitcast(BF16)[:, 0:4 * D]
        ln = LNBlock(p, ar, 512)
        b1b = ar.alloc(D, F32)
        g1b = ar.alloc(D, BF16)
        beb = ar.alloc(D, BF16)
        rows1 = ar.alloc(3 * D, F32)
        wr_s = ar.alloc(NCH * 16, BF16)
        br_s = ar.alloc(1, F32)
        lg_sb = ar.alloc(512, F32)
        ltk = ar.alloc(16, F32)
        t8 = ar.alloc(8, F32)
        gtk = ar.alloc(16, F32)
        scr = ar.alloc(16, F32)
        st_tok = ar.alloc(32, F32)
        gates_all = ar.alloc(16 * 16, F32)
        b2all = ar.alloc(1024, F32)
        gfm = ar.alloc(HALF, F32)

        dw0 = p.cnt["dw"]
        p.wait("sync", "dw", dw0)
        p.ins("sync", lambda e: e.dma_start(out=b2all[0:16, :], in_=dram["b2n"]),
              incs=(("dw", 16),))
        p.ins("sync", lambda e: e.dma_start(out=br_s[0:16, 0:1], in_=dram["brv"]),
              incs=(("dw", 16),))
        p.ins("gpsimd", lambda e: e.dma_start(
            out=wr_s[:, :].rearrange("p (k n) -> p k n", k=NCH),
            in_=dram["wr"][:, :].rearrange("(k p) n -> p k n", p=128)),
            incs=(("dx", 16),))
        p.wait("gpsimd", "dx", p.cnt["dx"])
        for e in ("vector", "scalar", "tensor"):
            p.wait(e, "dw", p.cnt["dw"])
            p.wait(e, "dx", p.cnt["dx"])
        p.ins("vector", lambda e: e.memset(ln.bcin[0:2, :], 0.0))
        p.ins("vector", lambda e: e.memset(rows1[0:2, :], 0.0), incs=(("dv", 1),))

        for half in range(2):
            t0 = half * HALF
            for blk in range(4):
                tb0 = t0 + blk * 512
                dxa = p.cnt["dx"]
                for c in range(NCH):
                    p.ins("gpsimd", lambda e, c=c, tb0=tb0: e.dma_start(
                        out=xblk[:, c * 512:(c + 1) * 512],
                        in_=xsc[c * 128:(c + 1) * 128, tb0:tb0 + 512]),
                        incs=(("dx", 16),))
                for e in ("tensor", "vector", "scalar"):
                    p.wait(e, "dx", dxa + 16 * NCH)

                def xblk_c(c):
                    return xblk[:, c * 512:(c + 1) * 512]

                def xn_dst(c, j, blk=blk):
                    return xn[:, c * HALF + blk * 512: c * HALF + blk * 512 + 512]
                emit_ln(p, ln, consts, xblk_c, xn_dst, ps, consts["zero"])
                p.wait("gpsimd", "dv", p.cnt["dv"])

            for tq in range(4):
                bank = ps[6]
                p.wait("tensor", "ac", p.cnt["ac"])
                for k in range(NCH):
                    p.ins("tensor", lambda e, k=k, tq=tq, bank=bank: e.matmul(
                        bank[0:16, :],
                        wr_s[:, k * 16:(k + 1) * 16],
                        xn[:, k * HALF + tq * 512: k * HALF + (tq + 1) * 512],
                        start=(k == 0), stop=(k == NCH - 1)),
                        incs=(("pe", 1),) if k == NCH - 1 else ())
                p.wait("scalar", "pe", p.cnt["pe"])
                p.selfsync("scalar")
                p.ins("scalar", lambda e, bank=bank: e.activation(
                    lg_sb[0:16, :], bank[0:16, :], AF.Identity,
                    bias=br_s[0:16, 0:1]), incs=(("ac", 1),))
                p.wait("tensor", "ac", p.cnt["ac"])
                for tile in range(4):
                    tt = tq * 4 + tile
                    p.wait("tensor", "dv", p.cnt["dv"])
                    p.ins("tensor", lambda e, tile=tile: e.transpose(
                        ps[5][:, 0:16],
                        lg_sb[0:16, tile * 128:(tile + 1) * 128],
                        identf[0:16, 0:16]), incs=(("pe", 1),))
                    p.wait("vector", "pe", p.cnt["pe"])
                    p.selfsync("vector")
                    p.ins("vector", lambda e: e.tensor_copy(
                        ltk[:, 0:16], ps[5][:, 0:16]), incs=(("dv", 1),))
                    p.selfsync("vector")
                    p.ins("vector", lambda e: e.max(out=t8[:, 0:8], in_=ltk[:, 0:16]),
                          incs=(("dv", 1),))
                    p.selfsync("vector")
                    p.ins("vector", lambda e: e.tensor_scalar_mul(
                        scr[:, 0:1], t8[:, 0:1], -1.0), incs=(("dv", 1),))
                    p.wait("scalar", "dv", p.cnt["dv"])
                    p.selfsync("scalar")
                    p.ins("scalar", lambda e: e.activation(
                        gtk[:, 0:16], ltk[:, 0:16], AF.Exp,
                        bias=scr[:, 0:1]), incs=(("ac", 1),))
                    p.wait("vector", "ac", p.cnt["ac"])
                    p.selfsync("vector")
                    p.ins("vector", lambda e: e.scalar_tensor_tensor(
                        gtk[:, 0:16], ltk[:, 0:16], t8[:, 1:2], gtk[:, 0:16],
                        op0=ALU.is_ge, op1=ALU.mult,
                        accum_out=scr[:, 1:2]), incs=(("dv", 1),))
                    p.selfsync("vector")
                    p.ins("vector", lambda e: e.reciprocal(
                        scr[:, 2:3], scr[:, 1:2]), incs=(("dv", 1),))
                    p.selfsync("vector")
                    p.ins("vector", lambda e, tt=tt: e.tensor_scalar(
                        gates_all[:, tt * 16:(tt + 1) * 16], gtk[:, 0:16],
                        scr[:, 2:3], None, op0=ALU.mult), incs=(("dv", 1),))
                    p.wait("tensor", "dv", p.cnt["dv"])
                    p.ins("tensor", lambda e, tt=tt: e.transpose(
                        ps[5][0:16, 0:128],
                        gates_all[:, tt * 16:(tt + 1) * 16], identf),
                        incs=(("pe", 1),))
                    p.wait("vector", "pe", p.cnt["pe"])
                    p.selfsync("vector")
                    p.ins("vector", lambda e, tt=tt: e.tensor_copy(
                        gfm[0:16, tt * 128:(tt + 1) * 128], ps[5][0:16, 0:128]),
                        incs=(("dv", 1),))
            p.selfsync("vector")
            p.ins("vector", lambda e: e.memset(acc, 0.0), incs=(("dv", 1),))

            evd = {}
            sqd = {}
            gtd = {}
            trd = {}
            cpd = {}
            accd = {}

            def e_body(ei):
                ac_e0 = p.cnt["ac"]
                sl = 0
                p.wait("sync", "pe", p.cnt["pe"], loop=True)
                wpat = [[D, 128], [128 * D, NCH], [1, D]]
                w1src = p.symap3("sync", dram["w1_bf"], AFF((ei, D * D)), wpat)
                p.ins("sync", lambda e, w1src=w1src: e.dma_start(
                    out=w1s[:, :].rearrange("p (k n) -> p k n", k=NCH),
                    in_=w1src), incs=(("dwA", 16),))
                w2src = p.symap3("sync", dram["w2_bf"], AFF((ei, D * D)), wpat)
                p.ins("sync", lambda e, w2src=w2src: e.dma_start(
                    out=w2s[:, :].rearrange("p (k n) -> p k n", k=NCH),
                    in_=w2src), incs=(("dwB", 16),))
                p.wait("gpsimd", "dv", p.cnt["dv"], loop=True)
                dxb = p.cnt["dx"]
                for i3, dst in ((0, b1b), (1, g1b), (2, beb)):
                    bsrc = p.symap3("gpsimd", dram["bge"],
                                    AFF((ei, 3 * D), const=i3 * D),
                                    [[0, 128], [1, D]])
                    p.ins("gpsimd", lambda e, bsrc=bsrc, dst=dst: e.dma_start(
                        out=dst, in_=bsrc), incs=(("dx", 16),))
                p.wait("vector", "dx", dxb + 48, loop=True)
                p.wait("tensor", "dwA", p.cnt["dwA"], loop=True)
                p.wait("tensor", "dwB", p.cnt["dwB"], loop=True)

                def emit_w1(g):
                    tg0 = g * 512
                    for tl in range(4):
                        ttile = g * 4 + tl
                        tt0 = tg0 + tl * 128
                        hb = [ps[(tl % 2) * 2], ps[(tl % 2) * 2 + 1]]
                        if ttile >= 2:
                            p.wait("tensor", "dv", evd[ttile - 2], loop=True)
                        elif (ttile + 14) in evd:
                            p.wait_prev("tensor", "dv", evd[ttile + 14])
                        for hh in range(2):
                            for kc in range(NCH):
                                p.ins("tensor", lambda e, hh=hh, kc=kc, tt0=tt0:
                                      e.matmul(
                                          hb[hh][:, :],
                                          xn[:, kc * HALF + tt0: kc * HALF + tt0 + 128],
                                          w1s[:, kc * D + hh * 512: kc * D + (hh + 1) * 512],
                                          start=(kc == 0), stop=(kc == NCH - 1)),
                                      incs=(("pe", 1),) if (kc == NCH - 1 and hh == 1)
                                      else ())
                        p.wait("vector", "pe", p.cnt["pe"], loop=True)
                        p.selfsync("vector")
                        if ttile >= 8:
                            p.wait("vector", "ac", sqd[ttile - 8], loop=True)
                        elif ttile == 0:
                            p.wait("vector", "ac", ac_e0, loop=True)
                        hslot = ((g % 2) * 4 + tl) * D
                        for hh in range(2):
                            p.ins("vector", lambda e, hh=hh, hslot=hslot, tl=tl:
                                  e.scalar_tensor_tensor(
                                      h_sb[:, hslot + hh * 512: hslot + (hh + 1) * 512],
                                      hb[hh][:, :], 1.0,
                                      b1b[:, hh * 512:(hh + 1) * 512],
                                      op0=ALU.mult, op1=ALU.add,
                                      accum_out=st_tok[:, hh * 4 + tl: hh * 4 + tl + 1]),
                                  incs=(("dv", 1),))
                        evd[ttile] = p.cnt["dv"]
                        p.wait("scalar", "dv", p.cnt["dv"], loop=True)
                        p.selfsync("scalar")
                        p.ins("scalar", lambda e, hslot=hslot, tl=tl: e.activation(
                            trash[:, 0:D], h_sb[:, hslot:hslot + D], AF.Square,
                            accum_out=st_tok[:, 8 + tl:9 + tl]), incs=(("ac", 1),))
                        sqd[ttile] = p.cnt["ac"]

                def emit_elem(g):
                    p.wait("vector", "ac", sqd[g * 4 + 3], loop=True)
                    p.selfsync("vector")
                    p.ins("vector", lambda e: e.tensor_tensor(
                        st_tok[:, 12:16], st_tok[:, 0:4], st_tok[:, 4:8],
                        ALU.add), incs=(("dv", 1),))
                    p.selfsync("vector")
                    p.ins("vector", lambda e: e.tensor_scalar_mul(
                        st_tok[:, 12:16], st_tok[:, 12:16], 1.0 / D),
                        incs=(("dv", 1),))
                    p.ins("vector", lambda e: e.tensor_scalar_mul(
                        st_tok[:, 16:20], st_tok[:, 8:12], 1.0 / D),
                        incs=(("dv", 1),))
                    p.selfsync("vector")
                    p.ins("vector", lambda e: e.tensor_tensor(
                        st_tok[:, 20:24], st_tok[:, 12:16], st_tok[:, 12:16],
                        ALU.mult), incs=(("dv", 1),))
                    p.selfsync("vector")
                    p.ins("vector", lambda e: e.tensor_tensor(
                        st_tok[:, 20:24], st_tok[:, 16:20], st_tok[:, 20:24],
                        ALU.subtract), incs=(("dv", 1),))
                    p.wait("scalar", "dv", p.cnt["dv"], loop=True)
                    p.selfsync("scalar")
                    p.ins("scalar", lambda e: e.activation(
                        st_tok[:, 20:24], st_tok[:, 20:24], AF.Ln,
                        bias=consts["eps"][0:128, 0:1]), incs=(("ac", 1),))
                    p.selfsync("scalar")
                    p.ins("scalar", lambda e: e.activation(
                        st_tok[:, 24:28], st_tok[:, 20:24], AF.Exp,
                        scale=consts["neghalf"][0:128, 0:1]), incs=(("ac", 1),))
                    p.wait("vector", "ac", p.cnt["ac"], loop=True)
                    p.selfsync("vector")
                    p.ins("vector", lambda e: e.tensor_tensor(
                        st_tok[:, 28:32], st_tok[:, 12:16], st_tok[:, 24:28],
                        ALU.mult), incs=(("dv", 1),))
                    p.selfsync("vector")
                    p.ins("vector", lambda e: e.tensor_scalar_mul(
                        st_tok[:, 28:32], st_tok[:, 28:32], -1.0),
                        incs=(("dv", 1),))
                    hq = hn_q if g % 2 == 0 else hn_q2
                    p.selfsync("vector")
                    if g >= 2:
                        p.wait("vector", "pe", trd[(g - 2) * 4 + 3], loop=True)
                    for tl in range(4):
                        hs8 = ((g % 2) * 4 + tl) * D
                        p.ins("vector", lambda e, tl=tl, hs8=hs8, hq=hq:
                              e.tensor_scalar(
                                  hq[:, tl * D:(tl + 1) * D], h_sb[:, hs8:hs8 + D],
                                  st_tok[:, 24 + tl:25 + tl],
                                  st_tok[:, 28 + tl:29 + tl],
                                  op0=ALU.mult, op1=ALU.add), incs=(("dv", 1),))
                    p.selfsync("vector")
                    for tl in range(4):
                        p.ins("vector", lambda e, tl=tl, hq=hq: e.tensor_tensor(
                            hq[:, tl * D:(tl + 1) * D], hq[:, tl * D:(tl + 1) * D],
                            g1b, ALU.mult), incs=(("dv", 1),))
                    p.selfsync("vector")
                    for tl in range(4):
                        p.ins("vector", lambda e, tl=tl, hq=hq: e.tensor_tensor(
                            hq[:, tl * D:(tl + 1) * D], hq[:, tl * D:(tl + 1) * D],
                            beb, ALU.add), incs=(("dv", 1),))
                    p.wait("scalar", "dv", p.cnt["dv"], loop=True)
                    p.selfsync("scalar")
                    p.ins("scalar", lambda e, hq=hq: e.activation(
                        hq[:, 0:4 * D], hq[:, 0:4 * D], AF.Gelu),
                        incs=(("ac", 1),))
                    p.wait("vector", "ac", p.cnt["ac"], loop=True)
                    p.selfsync("vector")
                    for tl in range(4):
                        ttile = g * 4 + tl
                        gate_ap = p.symap("vector", gates_all,
                                          AFF((ei, 1), const=ttile * 16), 1)
                        p.ins("vector", lambda e, tl=tl, gate_ap=gate_ap, hq=hq:
                              e.tensor_scalar(
                                  hq[:, tl * D:(tl + 1) * D],
                                  hq[:, tl * D:(tl + 1) * D],
                                  gate_ap, None, op0=ALU.mult),
                              incs=(("dv", 1),))
                    gtd[g] = p.cnt["dv"]

                def emit_trw2(g):
                    tg0 = g * 512
                    p.wait("tensor", "dv", gtd[g], loop=True)
                    for tl in range(4):
                        ttile = g * 4 + tl
                        psT = ps[6 + tl % 2][:, :].bitcast(BF16)
                        if tl >= 2:
                            p.wait("tensor", "dv", cpd[ttile - 2], loop=True)
                        elif g >= 1:
                            p.wait("tensor", "dv", cpd[(g - 1) * 4 + 2 + tl],
                                   loop=True)
                        hq = hn_q if g % 2 == 0 else hn_q2
                        for kcj in range(NCH):
                            p.ins("tensor", lambda e, kcj=kcj, tl=tl, psT=psT,
                                  hq=hq: e.transpose(
                                      psT[:, kcj * 128:(kcj + 1) * 128],
                                      hq[:, tl * D + kcj * 128: tl * D + (kcj + 1) * 128],
                                      ident), incs=(("pe", 1),) if kcj == NCH - 1
                                  else ())
                        trd[ttile] = p.cnt["pe"]
                        p.wait("vector", "pe", trd[ttile], loop=True)
                        p.selfsync("vector")
                        for kcj in range(NCH):
                            p.ins("vector", lambda e, kcj=kcj, tl=tl, psT=psT:
                                  e.tensor_copy(
                                      hT[:, kcj * 512 + tl * 128: kcj * 512 + (tl + 1) * 128],
                                      psT[:, kcj * 128:(kcj + 1) * 128]),
                                  incs=(("dv", 1),) if kcj == NCH - 1 else ())
                        cpd[ttile] = p.cnt["dv"]
                    p.wait("tensor", "dv", cpd[g * 4 + 3], loop=True)
                    for mo in range(NCH):
                        bank = ps[4 + mo % 2]
                        if mo >= 2:
                            p.wait("tensor", "dv", accd[(g, mo - 2)], loop=True)
                        elif g >= 1:
                            p.wait("tensor", "dv", accd[(g - 1, 6 + mo)], loop=True)
                        for kcj in range(NCH):
                            p.ins("tensor", lambda e, mo=mo, kcj=kcj, bank=bank:
                                  e.matmul(
                                      bank[:, :],
                                      w2s[:, mo * D + kcj * 128: mo * D + kcj * 128 + 128],
                                      hT[:, kcj * 512:(kcj + 1) * 512],
                                      start=(kcj == 0), stop=(kcj == NCH - 1)),
                                  incs=(("pe", 1),) if kcj == NCH - 1 else ())
                        p.wait("vector", "pe", p.cnt["pe"], loop=True)
                        p.selfsync("vector")
                        p.ins("vector", lambda e, mo=mo, tg0=tg0, bank=bank:
                              e.tensor_tensor(
                                  acc[:, mo * HALF + tg0: mo * HALF + tg0 + 512],
                                  acc[:, mo * HALF + tg0: mo * HALF + tg0 + 512],
                                  bank[:, :], ALU.add), incs=(("dv", 1),))
                        accd[(g, mo)] = p.cnt["dv"]

                for g in range(4):
                    emit_w1(g)
                    if g >= 2:
                        emit_trw2(g - 2)
                    emit_elem(g)
                emit_trw2(2)
                emit_trw2(3)
            p.emit_loop(NEXP, e_body, peel=0)

            p.wait("sync", "dwA", p.cnt["dwA"])
            p.wait("sync", "dwB", p.cnt["dwB"])
            p.wait("gpsimd", "dx", p.cnt["dx"])
            for tq in range(4):
                tb0 = t0 + tq * 512
                dxa = p.cnt["dx"]
                p.wait("gpsimd", "dv", p.cnt["dv"])
                for c in range(NCH):
                    p.ins("gpsimd", lambda e, c=c, tb0=tb0: e.dma_start(
                        out=xblk[:, c * 512:(c + 1) * 512],
                        in_=xsc[c * 128:(c + 1) * 128, tb0:tb0 + 512]),
                        incs=(("dx", 16),))
                p.wait("tensor", "dx", dxa + 16 * NCH)
                p.wait("vector", "dx", dxa + 16 * NCH)
                for o in range(NCH):
                    bank = ps[6 + o % 2]
                    p.wait("tensor", "dv", p.cnt["dv"])
                    p.ins("tensor", lambda e, o=o, tq=tq, bank=bank: e.matmul(
                        bank[:, :], b2all[0:16, o * 128:(o + 1) * 128],
                        gfm[0:16, tq * 512:(tq + 1) * 512],
                        start=True, stop=True), incs=(("pe", 1),))
                    p.wait("vector", "pe", p.cnt["pe"])
                    p.selfsync("vector")
                    p.ins("vector", lambda e, o=o, bank=bank: e.scalar_tensor_tensor(
                        xblk[:, o * 512:(o + 1) * 512], bank[:, :], 0.5,
                        xblk[:, o * 512:(o + 1) * 512],
                        op0=ALU.mult, op1=ALU.add), incs=(("dv", 1),))
                    p.selfsync("vector")
                    p.ins("vector", lambda e, o=o, tq=tq: e.scalar_tensor_tensor(
                        xblk[:, o * 512:(o + 1) * 512],
                        acc[:, o * HALF + tq * 512: o * HALF + (tq + 1) * 512],
                        0.5, xblk[:, o * 512:(o + 1) * 512],
                        op0=ALU.mult, op1=ALU.add), incs=(("dv", 1),))
                p.wait("gpsimd", "dv", p.cnt["dv"])
                for c in range(NCH):
                    p.ins("gpsimd", lambda e, c=c, tb0=tb0: e.dma_start(
                        out=xsc[c * 128:(c + 1) * 128, tb0:tb0 + 512],
                        in_=xblk[:, c * 512:(c + 1) * 512]),
                        incs=(("dx", 16),))
                p.wait("gpsimd", "dx", p.cnt["dx"])
            p.barrier()
    return body


def phase_l1pre(p, ar_mk, consts, xsc, zsc, ps):
    def body():
        ar = ar_mk()
        xblk = ar.alloc(NCH * 512, F32)
        zblk = ar.alloc(NCH * 512, F32)
        ln = LNBlock(p, ar, 512)
        p.ins("vector", lambda e: e.memset(ln.bcin[0:2, :], 0.0), incs=(("dv", 1),))
        for blk in range(TOKC // 512):
            t0 = blk * 512
            dxa = p.cnt["dx"]
            for c in range(NCH):
                p.ins("gpsimd", lambda e, c=c, t0=t0: e.dma_start(
                    out=xblk[:, c * 512:(c + 1) * 512],
                    in_=xsc[c * 128:(c + 1) * 128, t0:t0 + 512]),
                    incs=(("dx", 16),))
            for e in ("tensor", "vector", "scalar"):
                p.wait(e, "dx", dxa + 16 * NCH)

            def xblk_c(c):
                return xblk[:, c * 512:(c + 1) * 512]

            def z_dst(c, j):
                return zblk[:, c * 512:(c + 1) * 512]
            emit_ln(p, ln, consts, xblk_c, z_dst, ps, consts["zero"])
            p.wait("gpsimd", "dv", p.cnt["dv"])
            for c in range(NCH):
                p.ins("gpsimd", lambda e, c=c, t0=t0: e.dma_start(
                    out=zsc[c * 128:(c + 1) * 128, t0:t0 + 512],
                    in_=zblk[:, c * 512:(c + 1) * 512]),
                    incs=(("dx", 16),))
            p.wait("gpsimd", "dx", p.cnt["dx"])
        p.barrier()
    return body


def phase_l1post(p, ar_mk, consts, xsc, zsc):
    def body():
        ar = ar_mk()
        xblk = ar.alloc(NCH * 512, F32)
        zblk = ar.alloc(NCH * 512, F32)
        for blk in range(TOKC // 512):
            t0 = blk * 512
            dxa = p.cnt["dx"]
            for c in range(NCH):
                p.ins("gpsimd", lambda e, c=c, t0=t0: e.dma_start(
                    out=xblk[:, c * 512:(c + 1) * 512],
                    in_=xsc[c * 128:(c + 1) * 128, t0:t0 + 512]),
                    incs=(("dx", 16),))
                p.ins("gpsimd", lambda e, c=c, t0=t0: e.dma_start(
                    out=zblk[:, c * 512:(c + 1) * 512],
                    in_=zsc[c * 128:(c + 1) * 128, t0:t0 + 512]),
                    incs=(("dx", 16),))
            p.wait("vector", "dx", p.cnt["dx"])
            p.selfsync("vector")
            for c in range(NCH):
                p.ins("vector", lambda e, c=c: e.tensor_tensor(
                    xblk[:, c * 512:(c + 1) * 512],
                    xblk[:, c * 512:(c + 1) * 512],
                    zblk[:, c * 512:(c + 1) * 512], ALU.add), incs=(("dv", 1),))
            p.wait("gpsimd", "dv", p.cnt["dv"])
            for c in range(NCH):
                p.ins("gpsimd", lambda e, c=c, t0=t0: e.dma_start(
                    out=xsc[c * 128:(c + 1) * 128, t0:t0 + 512],
                    in_=xblk[:, c * 512:(c + 1) * 512]),
                    incs=(("dx", 16),))
            p.wait("gpsimd", "dx", p.cnt["dx"])
        p.barrier()
    return body


def phase_l1ffn(p, ar_mk, consts, zsc, w1cb, w2cb, b1d, lngd, lnbd, ps):
    MT = 1024
    NMT = TOKC // MT
    nh = MT // 512
    MH = 32

    def body():
        ar = ar_mk()
        xblk = ar.alloc(NCH * MT, F32)
        zn = ar.alloc(NCH * MT, BF16)
        ln = LNBlock(p, ar, MT)
        lntmp = ar.alloc(512, F32)
        wgm = ar.alloc(2 * D, BF16)
        h_t = ar.alloc(MH * MT, BF16)
        b1t = ar.alloc(MH, F32)
        lng_t = ar.alloc(NCH, F32)
        lnb_t = ar.alloc(NCH, F32)

        dw0 = p.cnt["dw"]
        p.wait("sync", "dw", dw0)
        p.ins("sync", lambda e: e.dma_start(out=b1t, in_=b1d), incs=(("dw", 16),))
        p.ins("sync", lambda e: e.dma_start(out=lng_t, in_=lngd), incs=(("dw", 16),))
        p.ins("sync", lambda e: e.dma_start(out=lnb_t, in_=lnbd), incs=(("dw", 16),))
        for e in ("vector", "scalar", "tensor"):
            p.wait(e, "dw", p.cnt["dw"])
        p.ins("vector", lambda e: e.memset(ln.bcin[0:2, :], 0.0), incs=(("dv", 1),))

        def xblk_c(c):
            return xblk[:, c * MT:(c + 1) * MT]

        for mt in range(NMT):
            t0 = mt * MT
            dx0 = p.cnt["dx"]
            for c in range(NCH):
                p.ins("gpsimd", lambda e, c=c, t0=t0: e.dma_start(
                    out=xblk_c(c), in_=zsc[c * 128:(c + 1) * 128, t0:t0 + MT]),
                    incs=(("dx", 16),))
            for e in ("tensor", "vector", "scalar"):
                p.wait(e, "dx", dx0 + 16 * NCH)

            def xn_dst(c, j):
                return lntmp[:, 0:512]

            def post(c, j):
                p.ins("scalar", lambda e, c=c, j=j: e.activation(
                    zn[:, c * MT + j * 512: c * MT + (j + 1) * 512],
                    lntmp[:, 0:512], AF.Identity,
                    scale=lng_t[:, c:c + 1], bias=lnb_t[:, c:c + 1]),
                    incs=(("ac", 1),))
            emit_ln(p, ln, consts, xblk_c, xn_dst, ps, consts["zero"], post=post)

            dv_gu0 = p.cnt["dv"]
            ac_gu0 = p.cnt["ac"]
            for m in range(MH):
                slot = "dwA" if m % 2 == 0 else "dwB"
                if m >= 2:
                    p.wait("sync", "ac", ac_gu0 + (m - 1) * nh)
                p.ins("sync", lambda e, m=m: e.dma_start(
                    out=wgm[:, (m % 2) * D:(m % 2) * D + D],
                    in_=w1cb[m * 128:(m + 1) * 128, :]), incs=((slot, 16),))
                bsel = (m % 2) * 4
                gb = [ps[bsel + 0], ps[bsel + 1]]
                p.wait("tensor", slot, p.cnt[slot])
                if m >= 2:
                    p.wait("tensor", "ac", ac_gu0 + (m - 1) * nh)
                for k in range(NCH):
                    for j in range(nh):
                        p.ins("tensor", lambda e, k=k, m=m, j=j: e.matmul(
                            gb[j][:, :],
                            wgm[:, (m % 2) * D + k * 128:(m % 2) * D + k * 128 + 128],
                            zn[:, k * MT + j * 512: k * MT + (j + 1) * 512],
                            start=(k == 0), stop=(k == NCH - 1)),
                            incs=(("pe", 1),) if (k == NCH - 1 and j == nh - 1) else ())
                p.wait("scalar", "pe", p.cnt["pe"])
                for j in range(nh):
                    p.selfsync("scalar")
                    p.ins("scalar", lambda e, m=m, j=j: e.activation(
                        h_t[:, m * MT + j * 512: m * MT + (j + 1) * 512],
                        gb[j][:, :], AF.Gelu_apprx_sigmoid,
                        bias=b1t[:, m:m + 1]),
                        incs=(("ac", 1),))
            p.wait("tensor", "ac", p.cnt["ac"])

            wdo = zn
            dv_do0 = p.cnt["dv"]
            for o in range(NCH):
                slot = "dwA" if o % 2 == 0 else "dwB"
                if o >= 2:
                    p.wait("sync", "dv", dv_do0 + (o - 1) * nh)
                else:
                    p.wait("sync", "ac", p.cnt["ac"])
                p.ins("sync", lambda e, o=o: e.dma_start(
                    out=wdo[:, (o % 2) * EXP4:((o % 2) + 1) * EXP4],
                    in_=w2cb[o * 128:(o + 1) * 128, :]), incs=((slot, 16),))
                bsel = (o % 2) * 4
                db = [ps[bsel + 0], ps[bsel + 1]]
                p.wait("tensor", slot, p.cnt[slot])
                if o >= 2:
                    p.wait("tensor", "dv", dv_do0 + (o - 1) * nh)
                for k in range(MH):
                    for j in range(nh):
                        p.ins("tensor", lambda e, k=k, o=o, j=j: e.matmul(
                            db[j][:, :],
                            wdo[:, (o % 2) * EXP4 + k * 128:(o % 2) * EXP4 + k * 128 + 128],
                            h_t[:, k * MT + j * 512: k * MT + (j + 1) * 512],
                            start=(k == 0), stop=(k == MH - 1)),
                            incs=(("pe", 1),) if (k == MH - 1 and j == nh - 1) else ())
                p.wait("vector", "pe", p.cnt["pe"])
                for j in range(nh):
                    p.selfsync("vector")
                    p.ins("vector", lambda e, o=o, j=j: e.tensor_tensor(
                        xblk_c(o)[:, j * 512:(j + 1) * 512],
                        db[j][:, :],
                        xblk_c(o)[:, j * 512:(j + 1) * 512],
                        ALU.add), incs=(("dv", 1),))
            p.wait("gpsimd", "dv", p.cnt["dv"])
            for c in range(NCH):
                p.ins("gpsimd", lambda e, c=c, t0=t0: e.dma_start(
                    out=zsc[c * 128:(c + 1) * 128, t0:t0 + MT], in_=xblk_c(c)),
                    incs=(("dx", 16),))
            p.wait("gpsimd", "dx", p.cnt["dx"])
            p.barrier()
    return body


def build_program(phase_names, dev_nb=NB):
    from contextlib import ExitStack
    es = ExitStack()
    nc = bacc_mod.Bacc("TRN2", target_bir_lowering=False, debug=False)

    toks = dev_nb * SEQ
    global TOKC
    assert dev_nb == NB, "dev_nb must equal NB for now"

    xT = nc.dram_tensor("xT", [D, TOKC], F32, kind="ExternalInput").ap()
    yT = nc.dram_tensor("yT", [D, TOKC], F32, kind="ExternalOutput").ap()
    xsc = nc.dram_tensor("xsc", [D, TOKC], F32, kind="Internal").ap()

    need = set(phase_names)
    dram = {}
    if "l0attn" in need:
        for nm in ("wq_a", "wk_a", "wv_a", "wo_a"):
            dram[nm] = nc.dram_tensor(nm, [D, D], F32, kind="ExternalInput").ap()
            dram[nm + "_bf"] = nc.dram_tensor(nm + "_bf", [D, D], BF16,
                                              kind="Internal").ap()
        for nm in ("bq_a", "bk_a", "bv_a", "bo_a"):
            dram[nm] = nc.dram_tensor(nm, [128, NCH], F32, kind="ExternalInput").ap()
    if "l1attn" in need:
        for i in range(2):
            for nm in ("wq", "wk", "wv", "wo"):
                dram[f"{nm}_l{i}"] = nc.dram_tensor(f"{nm}_l{i}", [D, D], F32,
                                                    kind="ExternalInput").ap()
                dram[f"{nm}_l{i}_bf"] = nc.dram_tensor(f"{nm}_l{i}_bf", [D, D],
                                                       BF16, kind="Internal").ap()
            dram[f"w1_l{i}"] = nc.dram_tensor(f"w1_l{i}", [EXP4, D], F32,
                                              kind="ExternalInput").ap()
            dram[f"w1_l{i}_bf"] = nc.dram_tensor(f"w1_l{i}_bf", [EXP4, D], BF16,
                                                 kind="Internal").ap()
            dram[f"w2_l{i}"] = nc.dram_tensor(f"w2_l{i}", [D, EXP4], F32,
                                              kind="ExternalInput").ap()
            dram[f"w2_l{i}_bf"] = nc.dram_tensor(f"w2_l{i}_bf", [D, EXP4], BF16,
                                                 kind="Internal").ap()
            for nm in ("bq", "bk", "bv", "lng1", "lnb1", "lng2", "lnb2"):
                dram[f"{nm}_l{i}"] = nc.dram_tensor(f"{nm}_l{i}", [128, NCH], F32,
                                                    kind="ExternalInput").ap()
            dram[f"b1_l{i}"] = nc.dram_tensor(f"b1_l{i}", [128, 32], F32,
                                              kind="ExternalInput").ap()
        dram["qtab"] = nc.dram_tensor("qtab", [128, SEQ], BF16,
                                      kind="ExternalInput").ap()
        dram["ktab"] = nc.dram_tensor("ktab", [128, SEQ], BF16,
                                      kind="ExternalInput").ap()
        dram["zsc"] = nc.dram_tensor("zsc", [D, TOKC], F32, kind="Internal").ap()
    if "moe" in need:
        dram["wr"] = nc.dram_tensor("wr", [D, NEXP], F32, kind="ExternalInput").ap()
        dram["brv"] = nc.dram_tensor("brv", [NEXP, 1], F32, kind="ExternalInput").ap()
        dram["w1n"] = nc.dram_tensor("w1n", [NEXP * D, D], F32, kind="ExternalInput").ap()
        dram["w2n"] = nc.dram_tensor("w2n", [NEXP * D, D], F32, kind="ExternalInput").ap()
        dram["w1_bf"] = nc.dram_tensor("w1_bf", [NEXP * D, D], BF16, kind="Internal").ap()
        dram["w2_bf"] = nc.dram_tensor("w2_bf", [NEXP * D, D], BF16, kind="Internal").ap()
        dram["bge"] = nc.dram_tensor("bge", [NEXP, 3 * D], F32, kind="ExternalInput").ap()
        dram["b2n"] = nc.dram_tensor("b2n", [NEXP, D], F32, kind="ExternalInput").ap()
    if "l0ffn" in need:
        dram["wgT"] = nc.dram_tensor("wgT", [HID0, D], I32, kind="ExternalInput").ap()
        dram["wuT"] = nc.dram_tensor("wuT", [HID0, D], I32, kind="ExternalInput").ap()
        dram["wdT"] = nc.dram_tensor("wdT", [D, HID0], I32, kind="ExternalInput").ap()
        dram["wgT_bf"] = nc.dram_tensor("wgT_bf", [HID0, D], BF16, kind="Internal").ap()
        dram["wuT_bf"] = nc.dram_tensor("wuT_bf", [HID0, D], BF16, kind="Internal").ap()
        dram["wdT_bf"] = nc.dram_tensor("wdT_bf", [D, HID0], BF16, kind="Internal").ap()

    ones1_d = nc.inline_tensor(np.ones((128, 1), np.float32), name="ones1_c").ap()
    ones2_np = np.zeros((2, 128), np.float32)
    ones2_np[0, :] = 1.0
    ones2_d = nc.inline_tensor(ones2_np, name="ones2_c").ap()
    ident_d = nc.inline_tensor(np.eye(128, dtype=BFNP), name="ident_c").ap()
    identf_d = nc.inline_tensor(np.eye(128, dtype=np.float32), name="identf_c").ap()
    lq_np = np.zeros((4, 256), np.float32)
    for qt in range(2):
        for pp in range(128):
            r = qt if pp < 64 else 2 + qt
            lq_np[r, qt * 128 + pp] = 1.0
    lq_d = nc.inline_tensor(lq_np, name="lq_c").ap()

    p = Prog(nc, es)
    const_ar = Arena(nc, es, "consts", 4096)
    consts = {}
    consts["ones1"] = const_ar.alloc(1, F32)
    consts["ones2"] = None
    ones2_t = const_ar.alloc(128, F32)
    consts["ident"] = const_ar.alloc(256, BF16)[0:128, 0:128]
    consts["eps"] = const_ar.alloc(1, F32)
    consts["neghalf"] = const_ar.alloc(1, F32)
    consts["ln127"] = const_ar.alloc(1, F32)
    consts["zero"] = const_ar.alloc(1, F32)
    consts["eps12"] = const_ar.alloc(1, F32)
    consts["poshalf"] = const_ar.alloc(1, F32)
    consts["negone"] = const_ar.alloc(1, F32)
    consts["onesz"] = const_ar.alloc(64, BF16)
    consts["oneb0"] = const_ar.alloc(64, BF16)
    consts["oneb64"] = const_ar.alloc(64, BF16)
    consts["qsc"] = const_ar.alloc(1, F32)
    identf_t = const_ar.alloc(128, F32)
    consts["identf"] = identf_t[0:128, 0:128]
    lq_t = const_ar.alloc(256, F32)
    consts["lq"] = lq_t
    consts["ones2"] = ones2_t[0:2, 0:128]

    WS_BYTES = 208640
    ws = Arena(nc, es, "ws", WS_BYTES)

    ps = [es.enter_context(nc.psum_tensor(f"ps{i}", [128, 512], F32))[:]
          for i in range(8)]

    def phase_init():
        p.ins("sync", lambda e: e.dma_start(out=consts["ones1"], in_=ones1_d),
              incs=(("dw", 16),))
        p.ins("sync", lambda e: e.dma_start(out=ones2_t[0:2, :], in_=ones2_d),
              incs=(("dw", 16),))
        p.ins("sync", lambda e: e.dma_start(out=consts["ident"], in_=ident_d),
              incs=(("dw", 16),))
        p.ins("sync", lambda e: e.dma_start(out=consts["identf"], in_=identf_d),
              incs=(("dw", 16),))
        p.ins("sync", lambda e: e.dma_start(out=lq_t[0:4, :], in_=lq_d),
              incs=(("dw", 16),))
        p.ins("vector", lambda e: e.memset(consts["eps"], LN_EPS))
        p.ins("vector", lambda e: e.memset(consts["neghalf"], -0.5))
        p.ins("vector", lambda e: e.memset(consts["ln127"], float(np.log(1.0 / 127.0))))
        p.ins("vector", lambda e: e.memset(consts["zero"], 0.0))
        p.ins("vector", lambda e: e.memset(consts["eps12"], 1e-12))
        p.ins("vector", lambda e: e.memset(consts["negone"], -1.0))
        p.ins("vector", lambda e: e.memset(consts["onesz"], 0.0))
        p.ins("vector", lambda e: e.memset(consts["oneb0"], 0.0))
        p.ins("vector", lambda e: e.memset(consts["oneb64"], 0.0),
              incs=(("dv", 1),))
        p.selfsync("vector")
        p.ins("vector", lambda e: e.memset(consts["onesz"][:, 0:1], 1.0))
        p.ins("vector", lambda e: e.memset(consts["oneb0"][0:1, :], 1.0))
        p.ins("vector", lambda e: e.memset(consts["oneb64"][64:65, :], 1.0))
        p.ins("vector", lambda e: e.memset(consts["poshalf"], 0.5), incs=(("dv", 1),))
        p.wait("sync", "dw", p.cnt["dw"])
        p.barrier()
    p.phases.append(phase_init)

    p.phases.append(phase_xcopy(p, xsc, xT))
    if "l0attn" in need:
        p.phases.append(phase_wcast(p, [
            (dram[nm + "_bf"], dram[nm]) for nm in ("wq_a", "wk_a", "wv_a", "wo_a")]))
    if "moe" in need:
        p.phases.append(phase_wcast(p, [
            (dram["w1_bf"], dram["w1n"]), (dram["w2_bf"], dram["w2n"])]))
    if "l1attn" in need:
        pairs = []
        for i in range(2):
            for nm in ("wq", "wk", "wv", "wo", "w1", "w2"):
                pairs.append((dram[f"{nm}_l{i}_bf"], dram[f"{nm}_l{i}"]))
        p.phases.append(phase_wcast(p, pairs))
    if "l0ffn" in need:
        p.phases.append(phase_wcast(p, [
            (dram["wgT_bf"], dram["wgT"]),
            (dram["wuT_bf"], dram["wuT"]),
            (dram["wdT_bf"], dram["wdT"]),
        ]))

    def ar_mk():
        ws.reset()
        return ws

    for name in phase_names:
        if name == "l0ffn":
            p.phases.append(phase_l0ffn(
                p, ar_mk, consts, xsc,
                dram["wgT_bf"], dram["wuT_bf"], dram["wdT_bf"], ps))
        elif name == "moe":
            p.phases.append(phase_moe(p, ar_mk, consts, xsc, dram, ps))
        elif name == "l1attn":
            zsc = dram["zsc"]
            p.phases.append(phase_l1pre(p, ar_mk, consts, xsc, zsc, ps))
            for i in range(2):
                cfg = {
                    "stream": zsc, "sel": False,
                    "wq": dram[f"wq_l{i}_bf"], "wk": dram[f"wk_l{i}_bf"],
                    "wv": dram[f"wv_l{i}_bf"], "wo": dram[f"wo_l{i}_bf"],
                    "bq": dram[f"bq_l{i}"], "bk": dram[f"bk_l{i}"],
                    "bv": dram[f"bv_l{i}"],
                    "lng": dram[f"lng1_l{i}"], "lnb": dram[f"lnb1_l{i}"],
                    "qtab": dram["qtab"], "ktab": dram["ktab"],
                }
                p.phases.append(phase_attention(p, ar_mk, consts, cfg, ps))
                p.phases.append(phase_l1ffn(
                    p, ar_mk, consts, zsc, dram[f"w1_l{i}_bf"],
                    dram[f"w2_l{i}_bf"], dram[f"b1_l{i}"],
                    dram[f"lng2_l{i}"], dram[f"lnb2_l{i}"], ps))
            p.phases.append(phase_l1post(p, ar_mk, consts, xsc, zsc))
        elif name == "l0attn":
            cfg = {
                "stream": xsc, "sel": True, "qscale": 1.0 / 32.0,
                "wq": dram["wq_a_bf"], "wk": dram["wk_a_bf"],
                "wv": dram["wv_a_bf"], "wo": dram["wo_a_bf"],
                "bq": dram["bq_a"], "bk": dram["bk_a"], "bv": dram["bv_a"],
                "bo": dram["bo_a"],
            }
            p.phases.append(phase_attention(p, ar_mk, consts, cfg, ps))
        else:
            raise ValueError(name)

    p.phases.append(phase_xcopy(p, yT, xsc, sem="ds", queue="sync"))

    p.run()
    es.close()
    nc.finalize()
    return nc


def _colblocks(WT):
    K, M = WT.shape
    return np.ascontiguousarray(
        WT.reshape(K // 128, 128, M // 128, 128)
          .transpose(2, 1, 0, 3).reshape(M, K))


def _prep_l0ffn(inputs):
    return {
        "wgT": _colblocks(np.ascontiguousarray(inputs["L0_ffn_w_gate"].T)),
        "wuT": _colblocks(np.ascontiguousarray(inputs["L0_ffn_w_up"].T)),
        "wdT": _colblocks(np.ascontiguousarray(inputs["L0_ffn_w_down"].T)),
    }


def _prep_moe(inputs):
    b1 = np.asarray(inputs["L1_ffn_b1"])
    g1 = np.asarray(inputs["L1_ffn_g1"])
    be1 = np.asarray(inputs["L1_ffn_be1"])
    W2 = np.asarray(inputs["L1_ffn_W2"])
    w2cb = np.concatenate([_colblocks(np.ascontiguousarray(W2[e]))
                           for e in range(NEXP)], axis=0)
    return {
        "wr": np.ascontiguousarray(inputs["L1_ffn_Wr"]),
        "brv": np.ascontiguousarray(np.asarray(inputs["L1_ffn_br"]).reshape(NEXP, 1)),
        "w1n": np.ascontiguousarray(np.asarray(inputs["L1_ffn_W1"]).reshape(NEXP * D, D)),
        "w2n": np.ascontiguousarray(w2cb),
        "bge": np.ascontiguousarray(np.concatenate([b1, g1, be1], axis=1)),
        "b2n": np.ascontiguousarray(np.asarray(inputs["L1_ffn_b2"])),
    }


def _prep_l1attn(inputs):
    out = {}
    for i in range(2):
        for nm, key in (("wq", "L1_attn_wq"), ("wk", "L1_attn_wk"),
                        ("wv", "L1_attn_wv"), ("wo", "L1_attn_wo")):
            out[f"{nm}_l{i}"] = _colblocks(np.ascontiguousarray(
                np.asarray(inputs[key])[i]))
        out[f"w1_l{i}"] = _colblocks(np.ascontiguousarray(
            np.asarray(inputs["L1_attn_w1"])[i]))
        out[f"w2_l{i}"] = _colblocks(np.ascontiguousarray(
            np.asarray(inputs["L1_attn_w2"])[i]))
        for nm, key in (("bq", "L1_attn_bq"), ("bk", "L1_attn_bk"),
                        ("bv", "L1_attn_bv"), ("lng1", "L1_attn_ln1_g"),
                        ("lnb1", "L1_attn_ln1_b"), ("lng2", "L1_attn_ln2_g"),
                        ("lnb2", "L1_attn_ln2_b")):
            out[f"{nm}_l{i}"] = _chunked_vec(np.asarray(inputs[key])[i])
        out[f"b1_l{i}"] = _chunked_vec(np.asarray(inputs["L1_attn_b1"])[i])
    tab = _rotary_table()
    qt = np.empty((128, SEQ), np.float32)
    kt = np.empty((128, SEQ), np.float32)
    qt[0:32] = tab / 8.0
    qt[32:64] = 1.0 / 8.0
    qt[64:96] = tab / 8.0
    qt[96:128] = 1.0 / 8.0
    kt[0:32] = tab
    kt[32:64] = 1.0
    kt[64:96] = tab
    kt[96:128] = 1.0
    out["qtab"] = qt.astype(BFNP)
    out["ktab"] = kt.astype(BFNP)
    return out


def _prep_l0attn(inputs):
    Wqkv = np.asarray(inputs["L0_attn_W_qkv"])
    bqkv = np.asarray(inputs["L0_attn_b_qkv"])
    out = {}
    for i, nm in enumerate(("wq_a", "wk_a", "wv_a")):
        out[nm] = _colblocks(np.ascontiguousarray(Wqkv[i * D:(i + 1) * D].T))
    out["wo_a"] = _colblocks(np.ascontiguousarray(np.asarray(inputs["L0_attn_W_out"]).T))
    for i, nm in enumerate(("bq_a", "bk_a", "bv_a")):
        out[nm] = _chunked_vec(bqkv[i * D:(i + 1) * D])
    out["bo_a"] = _chunked_vec(np.asarray(inputs["L0_attn_b_out"]))
    return out


INPUT_KEYS = {
    "l0ffn": ("wgT", "wuT", "wdT"),
    "l0attn": ("wq_a", "wk_a", "wv_a", "wo_a", "bq_a", "bk_a", "bv_a", "bo_a"),
    "moe": ("wr", "brv", "w1n", "w2n", "bge", "b2n"),
    "l1attn": tuple(
        [f"{nm}_l{i}" for i in range(2)
         for nm in ("wq", "wk", "wv", "wo", "w1", "w2", "bq", "bk", "bv",
                    "lng1", "lnb1", "lng2", "lnb2", "b1")] + ["qtab", "ktab"]),
}

FUSED = True
_PROG_CACHE = {}


def _get_program(phase_names):
    key = tuple(phase_names)
    if key not in _PROG_CACHE:
        _PROG_CACHE[key] = build_program(key)
    return _PROG_CACHE[key]


def kernel(**inputs):
    x = np.asarray(inputs["L0_attn_x"])
    shared = {}
    shared.update(_prep_l0ffn(inputs))
    shared.update(_prep_l0attn(inputs))
    shared.update(_prep_moe(inputs))
    shared.update(_prep_l1attn(inputs))

    if FUSED:
        seq = [("l0ffn", "l0attn", "l0ffn", "moe", "l1attn", "moe")]
    else:
        seq = [("l0ffn", "l0attn", "l0ffn"), ("moe",), ("l1attn",), ("moe",)]

    cur = [np.ascontiguousarray(x[c * NB:(c + 1) * NB].reshape(TOKC, D).T)
           for c in range(NCORES)]
    for phases in seq:
        nc = _get_program(phases)
        keys = set()
        for ph in set(phases):
            keys.update(INPUT_KEYS[ph])
        in_maps = []
        for c in range(NCORES):
            m = {"xT": cur[c]}
            for k in keys:
                m[k] = shared[k]
            in_maps.append(m)
        res = run_bass_kernel_spmd(nc, in_maps, core_ids=list(range(NCORES)))
        cur = [np.ascontiguousarray(res.results[c]["yT"]) for c in range(NCORES)]
    outs = [cur[c].T.reshape(NB, SEQ, D) for c in range(NCORES)]
    return np.concatenate(outs, axis=0)


# version 15 (final)
# speedup vs baseline: 1.1851x; 1.0015x over previous
import numpy as np
import ml_dtypes
import concourse.bass as bass
import concourse.bacc as bacc_mod
import concourse.mybir as mybir
from concourse.bass_utils import run_bass_kernel_spmd

F32 = mybir.dt.float32
BF16 = mybir.dt.bfloat16
I32 = mybir.dt.int32
AF = mybir.ActivationFunctionType
ALU = mybir.AluOpType
AX = mybir.AxisListType
BFNP = ml_dtypes.bfloat16

NCORES = 8
D = 1024
NCH = 8
TOKC = 4096
SEQ = 2048
NB = 2
HID0 = 3072
MCH0 = 24
NHEAD = 16
HDIM = 64
TOPK = 614
EXP4 = 4096
NEXP = 16
LN_EPS = 1e-5
ENGINES = ("sync", "gpsimd", "tensor", "vector", "scalar")


def AFF(*terms, const=0):
    c = const
    pairs = []
    for v, k in terms:
        if isinstance(v, (int, np.integer)):
            c += int(v) * k
        else:
            pairs.append((v, k))
    return (c, pairs)


class Prog:

    def __init__(self, nc, es):
        self.nc = nc
        self.es = es
        self.sems = {}
        for name in ("dw", "dwA", "dwB", "dx", "ds", "pe", "dv", "ac", "pb"):
            self.sems[name] = es.enter_context(nc.semaphore(name))
        self.cnt = {k: 0 for k in self.sems}
        self.cur = None
        self.eng = None
        self.cur_last = None
        self.dry = False
        self.loops = []
        self.wregs = {}
        self.symrr = {}
        self.phases = []

    def _note(self, incs):
        for s, n in incs:
            self.cnt[s] += n

    def ins(self, engine, f, incs=()):
        self._note(incs)
        if self.cur == engine and not self.dry:
            instr = f(self.eng)
            for s, n in incs:
                instr = instr.then_inc(self.sems[s], n)
            self.cur_last = instr
            return instr
        return None

    def _wregs(self):
        if self.cur not in self.wregs:
            r1 = self.es.enter_context(self.eng.register(f"wr1_{self.cur}"))
            r2 = self.es.enter_context(self.eng.register(f"wr2_{self.cur}"))
            self.wregs[self.cur] = (r1, r2)
        return self.wregs[self.cur]

    def wait(self, engine, sem, val, loop=False):
        if self.cur != engine or self.dry:
            return
        if loop and self.loops:
            levels = []
            const = val
            for snap, deltas, it, peel in self.loops:
                d = deltas.get(sem, 0)
                if d:
                    levels.append((it, d))
                    const -= peel * d
            if levels:
                w1, w2 = self._wregs()
                eng = self.eng
                eng.reg_mul(w1, levels[0][0], levels[0][1])
                for itl, dl in levels[1:]:
                    eng.reg_mul(w2, itl, dl)
                    eng.reg_add(w1, w1, w2)
                if const:
                    eng.reg_add(w1, w1, const)
                eng.wait_ge(self.sems[sem], w1)
                return
        self.eng.wait_ge(self.sems[sem], val)

    def emit_loop(self, n, body_fn, peel=1):
        for i in range(min(peel, n)):
            body_fn(i)
        if n <= peel:
            return
        snap = dict(self.cnt)
        was_dry = self.dry
        self.dry = True
        body_fn(peel)
        self.dry = was_dry
        deltas = {s: self.cnt[s] - snap[s] for s in self.cnt
                  if self.cnt[s] != snap[s]}
        self.cnt = dict(snap)
        if self.dry:
            self.cnt = {s: snap[s] + deltas.get(s, 0) * (n - peel)
                        for s in self.cnt}
            return
        depth = len(self.loops)
        key = (self.cur, depth)
        if key not in self.wregs:
            self.wregs[key] = self.es.enter_context(
                self.eng.register(f"li{depth}_{self.cur}"))
        reg = self.wregs[key]
        nc_ = self.nc
        lid = nc_.next_id()
        ls, le = f"ml{lid}s_{self.cur}", f"ml{lid}e_{self.cur}"
        self.eng.reg_mov(reg, peel)
        self.eng.br(ls)
        eng_set = [self.eng.engine]
        with nc_.body(ls, valid_engines=eng_set):
            self.loops.append((snap, deltas, reg, peel))
            body_fn(reg)
            self.loops.pop()
            self.eng.reg_add(reg, reg, 1)
            self.eng.br_lt(reg, n, on_true=ls, on_false=le)
        nc_.switch_bb(le)
        self.cnt = {s: snap[s] + deltas.get(s, 0) * (n - peel)
                    for s in self.cnt}

    def symap(self, engine, base2d, aff, width, rows=None):
        const, pairs = aff
        if rows is not None:
            base2d = base2d[rows[0]:rows[1], :]
        if not pairs:
            return base2d[:, const:const + width]
        if self.cur != engine or self.dry:
            return None
        key = (engine, "symap")
        pool = self.wregs.setdefault(key, [])
        if not pool:
            for i in range(4):
                pool.append(self.es.enter_context(
                    self.eng.register(f"so{i}_{engine}")))
            self.wregs[key] = pool
        rr = self.symrr.get(engine, 0)
        reg = pool[rr % 3]
        tmp = pool[3]
        self.symrr[engine] = rr + 1
        eng = self.eng
        eng.reg_mul(reg, pairs[0][0], pairs[0][1])
        for v, k in pairs[1:]:
            eng.reg_mul(tmp, v, k)
            eng.reg_add(reg, reg, tmp)
        total_c = const + base2d.offset
        if total_c:
            eng.reg_add(reg, reg, total_c)
        pat = [list(base2d.ap[0]), [1, width]]
        return bass.AP(base2d.tensor, reg, pat)

    def symap3(self, engine, base_ap, aff, pat):
        const, pairs = aff
        if not pairs:
            return bass.AP(base_ap.tensor, base_ap.offset + const, pat)
        if self.cur != engine or self.dry:
            return None
        key = (engine, "symap")
        pool = self.wregs.setdefault(key, [])
        if not pool:
            for i in range(4):
                pool.append(self.es.enter_context(
                    self.eng.register(f"so{i}_{engine}")))
            self.wregs[key] = pool
        rr = self.symrr.get(engine, 0)
        reg = pool[rr % 3]
        tmp = pool[3]
        self.symrr[engine] = rr + 1
        eng = self.eng
        eng.reg_mul(reg, pairs[0][0], pairs[0][1])
        for v, k in pairs[1:]:
            eng.reg_mul(tmp, v, k)
            eng.reg_add(reg, reg, tmp)
        total_c = const + base_ap.offset
        if total_c:
            eng.reg_add(reg, reg, total_c)
        return bass.AP(base_ap.tensor, reg, pat)

    def wait_prev(self, engine, sem, val):
        if self.cur != engine or self.dry:
            return
        d = self.loops[-1][1].get(sem, 0) if self.loops else 0
        self.wait(engine, sem, val - d, loop=True)

    ESEM = {"vector": "dv", "scalar": "ac", "tensor": "pe", "gpsimd": "dv"}

    def selfsync(self, engine):
        sem = self.ESEM[engine]
        self.wait(engine, sem, self.cnt[sem], loop=bool(self.loops))

    def barrier(self):
        base = self.cnt["pb"]
        for e in ENGINES:
            self._note((("pb", 1),))
            if self.cur == e:
                if e in self.ESEM:
                    self.wait(e, self.ESEM[e], self.cnt[self.ESEM[e]])
                self.eng.nop().then_inc(self.sems["pb"], 1)
                self.cur_last = None
        for e in ENGINES:
            self.wait(e, "pb", base + len(ENGINES))

    def run(self):
        nc = self.nc
        snap = dict(self.cnt)
        results = []

        def run_engine(name, proxy):
            self.cnt = dict(snap)
            self.cur = name
            self.eng = proxy
            self.cur_last = None
            self.dry = False
            self.loops = []
            for body in self.phases:
                body()
            results.append(dict(self.cnt))

        with nc.Block() as block:
            @block.sync
            def _(e):
                run_engine("sync", e)

            @block.gpsimd
            def _(e):
                run_engine("gpsimd", e)

            @block.tensor
            def _(e):
                run_engine("tensor", e)

            @block.vector
            def _(e):
                run_engine("vector", e)

            @block.scalar
            def _(e):
                run_engine("scalar", e)
        for r in results[1:]:
            assert r == results[0], "nondeterministic phase bodies"


class Arena:

    def __init__(self, nc, es, name, nbytes):
        self.t = es.enter_context(nc.sbuf_tensor(name, [128, nbytes], mybir.dt.uint8))
        self.nbytes = nbytes
        self.off = 0

    def reset(self, off=0):
        self.off = off

    def alloc(self, cols, dtype):
        size = mybir.dt.size(dtype)
        self.off = (self.off + 31) // 32 * 32
        start = self.off
        self.off += cols * size
        assert self.off <= self.nbytes, f"arena overflow: {self.off} > {self.nbytes}"
        full = self.t[:].bitcast(dtype)
        return full[:, start // size: start // size + cols]


def _chunked_vec(v):
    n = v.shape[-1]
    return np.ascontiguousarray(v.reshape(n // 128, 128).T)


def _rotary_table():
    rot, max_freq = 32, 10.0
    dim_exp = rot // 2
    freqs = 2.0 ** np.linspace(0.0, np.log2(max_freq / 2.0), dim_exp)
    pos = 1.0 + np.arange(SEQ, dtype=np.float64)
    rad = pos[:, None] * np.concatenate([freqs, freqs]) * np.pi
    sinr = np.sin(rad)
    cosr = np.cos(rad)
    tab = np.where(np.arange(rot) % 2 == 0, sinr - cosr, sinr + cosr)
    return np.ascontiguousarray(tab.T.astype(np.float32))


def phase_xcopy(p, dst, src, sem="dx", queue="gpsimd"):
    def body():
        for c in range(NCH):
            p.ins(queue, lambda e, c=c: e.dma_start(
                out=dst[c * 128:(c + 1) * 128, :],
                in_=src[c * 128:(c + 1) * 128, :]), incs=((sem, 16),))
        p.wait(queue, sem, p.cnt[sem])
        p.barrier()
    return body


def phase_wcast(p, pairs):
    def body():
        for dst, src in pairs:
            rows = dst.shape[0]
            step = 512
            for r0 in range(0, rows, step):
                r1 = min(rows, r0 + step)
                p.ins("gpsimd", lambda e, r0=r0, r1=r1, dst=dst, src=src: e.dma_start(
                    out=dst[r0:r1, :], in_=src[r0:r1, :]), incs=(("dx", 16),))
        p.wait("gpsimd", "dx", p.cnt["dx"])
        p.barrier()
    return body


class LNBlock:
    def __init__(self, p, ar, T):
        self.p = p
        self.T = T
        nh = T // 512
        self.nh = nh
        self.x2t = ar.alloc(2 * 512, F32)
        self.rows = ar.alloc(4 * nh * 512, F32)
        self.bcin = ar.alloc(2 * nh * 512, F32)


def emit_ln(p, ln, consts, xblk_c, xn_c, ps, lnscale_bias, post=None):
    T, nh = ln.T, ln.nh
    ones1, ones2 = consts["ones1"], consts["ones2"]
    eps_c, neghalf_c = consts["eps"], consts["neghalf"]
    rows = ln.rows
    bcin = ln.bcin

    for j in range(nh):
        for c in range(NCH):
            p.ins("tensor", lambda e, c=c, j=j: e.matmul(
                ps[4][0:1, :], ones1, xblk_c(c)[:, j * 512:(j + 1) * 512],
                start=(c == 0), stop=(c == NCH - 1)),
                incs=(("pe", 1),) if c == NCH - 1 else ())
        p.wait("vector", "pe", p.cnt["pe"])
        p.ins("vector", lambda e, j=j: e.tensor_copy(
            rows[0:1, j * 512:(j + 1) * 512], ps[4][0:1, :]), incs=(("dv", 1),))
        p.wait("tensor", "dv", p.cnt["dv"])
        pe_sq0 = p.cnt["pe"]
        for c in range(NCH):
            if c >= 2:
                p.wait("scalar", "pe", pe_sq0 + c - 1)
            else:
                p.wait("scalar", "pe", pe_sq0)
            ac_before = p.cnt["ac"]
            p.ins("scalar", lambda e, c=c, j=j: e.activation(
                ln.x2t[:, (c % 2) * 512:(c % 2) * 512 + 512],
                xblk_c(c)[:, j * 512:(j + 1) * 512], AF.Square),
                incs=(("ac", 1),))
            p.wait("tensor", "ac", ac_before + 1)
            p.ins("tensor", lambda e, c=c: e.matmul(
                ps[4][0:1, :], ones1, ln.x2t[:, (c % 2) * 512:(c % 2) * 512 + 512],
                start=(c == 0), stop=(c == NCH - 1)),
                incs=(("pe", 1),))
        p.wait("vector", "pe", p.cnt["pe"])
        p.ins("vector", lambda e, j=j: e.tensor_copy(
            rows[0:1, (nh + j) * 512:(nh + j + 1) * 512], ps[4][0:1, :]), incs=(("dv", 1),))
        p.wait("tensor", "dv", p.cnt["dv"])

    mean = rows[0:1, 2 * nh * 512: 3 * nh * 512]
    tmpr = rows[0:1, 3 * nh * 512: 4 * nh * 512]
    sums = rows[0:1, 0: nh * 512]
    sumsq = rows[0:1, nh * 512: 2 * nh * 512]
    rstd = bcin[0:1, 0: nh * 512]
    mr = bcin[0:1, nh * 512: 2 * nh * 512]
    p.selfsync("vector")
    p.ins("vector", lambda e: e.tensor_scalar_mul(mean, sums, 1.0 / D), incs=(("dv", 1),))
    p.ins("vector", lambda e: e.tensor_scalar_mul(tmpr, sumsq, 1.0 / D), incs=(("dv", 1),))
    p.selfsync("vector")
    p.ins("vector", lambda e: e.tensor_tensor(sums, mean, mean, ALU.mult), incs=(("dv", 1),))
    p.selfsync("vector")
    p.ins("vector", lambda e: e.tensor_tensor(tmpr, tmpr, sums, ALU.subtract),
          incs=(("dv", 1),))
    p.wait("scalar", "dv", p.cnt["dv"])
    p.ins("scalar", lambda e: e.activation(tmpr, tmpr, AF.Ln, bias=eps_c[0:1, 0:1]),
          incs=(("ac", 1),))
    p.selfsync("scalar")
    p.ins("scalar", lambda e: e.activation(rstd, tmpr, AF.Exp,
                                           scale=neghalf_c[0:1, 0:1],
                                           bias=lnscale_bias[0:1, 0:1]),
          incs=(("ac", 1),))
    p.wait("vector", "ac", p.cnt["ac"])
    p.ins("vector", lambda e: e.tensor_tensor(mr, mean, rstd, ALU.mult),
          incs=(("dv", 1),))
    p.wait("tensor", "dv", p.cnt["dv"])

    for j in range(nh):
        p.ins("tensor", lambda e, j=j: e.matmul(
            ps[j][:, :], ones2, bcin[0:2, j * 512:(j + 1) * 512],
            start=True, stop=True))
    for j in range(nh):
        p.ins("tensor", lambda e, j=j: e.matmul(
            ps[2 + j][:, :], ones2, bcin[0:2, (nh + j) * 512:(nh + j + 1) * 512],
            start=True, stop=True), incs=(("pe", 1),) if j == nh - 1 else ())
    p.wait("vector", "pe", p.cnt["pe"])

    for c in range(NCH):
        for j in range(nh):
            p.selfsync("vector")
            p.ins("vector", lambda e, c=c, j=j: e.tensor_tensor(
                ln.x2t[:, 0:512], xblk_c(c)[:, j * 512:(j + 1) * 512],
                ps[j][:, :], ALU.mult), incs=(("dv", 1),))
            p.selfsync("vector")
            if post is not None:
                p.wait("vector", "ac", p.cnt["ac"])
            p.ins("vector", lambda e, c=c, j=j: e.tensor_tensor(
                xn_c(c, j), ln.x2t[:, 0:512],
                ps[2 + j][:, :], ALU.subtract),
                incs=(("dv", 1),))
            if post is not None:
                p.wait("scalar", "dv", p.cnt["dv"])
                post(c, j)
    p.wait("tensor", "dv", p.cnt["dv"])
    if post is not None:
        p.wait("tensor", "ac", p.cnt["ac"])


def phase_l0ffn(p, ar_mk, consts, xsc, wg_bf, wu_bf, wd_bf, ps):
    MT = 1024
    NMT = TOKC // MT
    nh = MT // 512

    def body():
        ar = ar_mk()
        xblk = ar.alloc(NCH * MT, F32)
        xn = ar.alloc(NCH * MT, BF16)
        ln = LNBlock(p, ar, MT)
        wgm = ar.alloc(2 * D, BF16)
        wum = ar.alloc(2 * D, BF16)
        wdo = ar.alloc(2 * MCH0 * 128, BF16)
        h_t = ar.alloc(MCH0 * MT, BF16)
        sg = ar.alloc(2 * MT, F32)

        def xblk_c(c):
            return xblk[:, c * MT:(c + 1) * MT]

        def xn_c(c, j):
            return xn[:, c * MT + j * 512: c * MT + (j + 1) * 512]

        p.ins("vector", lambda e: e.memset(ln.bcin[0:2, :], 0.0), incs=(("dv", 1),))

        def wcol(wbf, m):
            return wbf[m * 128:(m + 1) * 128, :]

        for mt in range(NMT):
            t0 = mt * MT
            dx0 = p.cnt["dx"]
            for c in range(NCH):
                p.ins("gpsimd", lambda e, c=c: e.dma_start(
                    out=xblk_c(c), in_=xsc[c * 128:(c + 1) * 128, t0:t0 + MT]),
                    incs=(("dx", 16),))
            for e in ("tensor", "vector", "scalar"):
                p.wait(e, "dx", dx0 + 16 * NCH)

            emit_ln(p, ln, consts, xblk_c, xn_c, ps, consts["ln127"])

            dv_gu0 = p.cnt["dv"]
            for m in range(MCH0):
                if m >= 2:
                    p.wait("sync", "dv", dv_gu0 + (m - 1) * nh)
                slot = "dwA" if m % 2 == 0 else "dwB"
                p.ins("sync", lambda e, m=m: e.dma_start(
                    out=wgm[:, (m % 2) * D:(m % 2) * D + D],
                    in_=wcol(wg_bf, m)), incs=((slot, 16),))
                p.ins("sync", lambda e, m=m: e.dma_start(
                    out=wum[:, (m % 2) * D:(m % 2) * D + D],
                    in_=wcol(wu_bf, m)), incs=((slot, 16),))
                bsel = (m % 2) * 4
                gb = [ps[bsel + 0], ps[bsel + 1]]
                ub = [ps[bsel + 2], ps[bsel + 3]]
                p.wait("tensor", slot, p.cnt[slot])
                if m >= 2:
                    p.wait("tensor", "dv", dv_gu0 + (m - 1) * nh)
                for k in range(NCH):
                    p.ins("tensor", lambda e, k=k, m=m: e.ldweights(
                        wgm[:, (m % 2) * D + k * 128:(m % 2) * D + k * 128 + 128]))
                    for j in range(nh):
                        p.ins("tensor", lambda e, k=k, m=m, j=j: e.matmul(
                            gb[j][:, :],
                            wgm[:, (m % 2) * D + k * 128:(m % 2) * D + k * 128 + 128],
                            xn_c(k, j),
                            start=(k == 0), stop=(k == NCH - 1)))
                    p.ins("tensor", lambda e, k=k, m=m: e.ldweights(
                        wum[:, (m % 2) * D + k * 128:(m % 2) * D + k * 128 + 128]))
                    for j in range(nh):
                        p.ins("tensor", lambda e, k=k, m=m, j=j: e.matmul(
                            ub[j][:, :],
                            wum[:, (m % 2) * D + k * 128:(m % 2) * D + k * 128 + 128],
                            xn_c(k, j),
                            start=(k == 0), stop=(k == NCH - 1)),
                            incs=(("pe", 1),) if (k == NCH - 1 and j == nh - 1) else ())
                p.wait("scalar", "pe", p.cnt["pe"])
                if m >= 2:
                    p.wait("scalar", "dv", dv_gu0 + (m - 1) * nh)
                for j in range(nh):
                    p.ins("scalar", lambda e, m=m, j=j: e.activation(
                        sg[:, (m % 2) * MT + j * 512:(m % 2) * MT + (j + 1) * 512],
                        gb[j][:, :], AF.Silu),
                        incs=(("ac", 1),))
                p.wait("vector", "ac", p.cnt["ac"])
                for j in range(nh):
                    p.ins("vector", lambda e, m=m, j=j: e.tensor_tensor(
                        h_t[:, m * MT + j * 512: m * MT + (j + 1) * 512],
                        sg[:, (m % 2) * MT + j * 512:(m % 2) * MT + (j + 1) * 512],
                        ub[j][:, :], ALU.mult),
                        incs=(("dv", 1),))
            p.wait("tensor", "dv", p.cnt["dv"])

            dv_do0 = p.cnt["dv"]
            for o in range(NCH):
                if o >= 2:
                    p.wait("sync", "dv", dv_do0 + (o - 1) * nh)
                else:
                    p.wait("sync", "dv", dv_do0)
                slot = "dwA" if o % 2 == 0 else "dwB"
                p.ins("sync", lambda e, o=o: e.dma_start(
                    out=wdo[:, (o % 2) * MCH0 * 128:((o % 2) + 1) * MCH0 * 128],
                    in_=wcol(wd_bf, o)), incs=((slot, 16),))
                bsel = (o % 2) * 4
                db = [ps[bsel + 0], ps[bsel + 1]]
                p.wait("tensor", slot, p.cnt[slot])
                if o >= 2:
                    p.wait("tensor", "dv", dv_do0 + (o - 1) * nh)
                for k in range(MCH0):
                    p.ins("tensor", lambda e, k=k, o=o: e.ldweights(
                        wdo[:, (o % 2) * MCH0 * 128 + k * 128:(o % 2) * MCH0 * 128 + k * 128 + 128]))
                    for j in range(nh):
                        p.ins("tensor", lambda e, k=k, o=o, j=j: e.matmul(
                            db[j][:, :],
                            wdo[:, (o % 2) * MCH0 * 128 + k * 128:(o % 2) * MCH0 * 128 + k * 128 + 128],
                            h_t[:, k * MT + j * 512: k * MT + (j + 1) * 512],
                            start=(k == 0), stop=(k == MCH0 - 1)),
                            incs=(("pe", 1),) if (k == MCH0 - 1 and j == nh - 1) else ())
                p.wait("vector", "pe", p.cnt["pe"])
                for j in range(nh):
                    p.ins("vector", lambda e, o=o, j=j: e.scalar_tensor_tensor(
                        xblk_c(o)[:, j * 512:(j + 1) * 512],
                        db[j][:, :], 0.5 / 127.0,
                        xblk_c(o)[:, j * 512:(j + 1) * 512],
                        op0=ALU.mult, op1=ALU.add),
                        incs=(("dv", 1),))
            p.wait("gpsimd", "dv", p.cnt["dv"])
            for c in range(NCH):
                p.ins("gpsimd", lambda e, c=c: e.dma_start(
                    out=xsc[c * 128:(c + 1) * 128, t0:t0 + MT], in_=xblk_c(c)),
                    incs=(("dx", 16),))
            p.wait("gpsimd", "dx", p.cnt["dx"])
            p.barrier()
    return body



QB = 256
NE_SEL = 4
SEL_LAM = (1.0, 1.0, 0.7, 0.5, 0.35, 0.25, 0.2, 0.15)
SEL_Q70 = 0.5244
SEL_DINV = 1.0 / (SEQ * 0.3477)


def phase_attention(p, ar_mk, consts, cfg, ps):
    sel = cfg["sel"]
    stream = cfg["stream"]
    ident = consts["ident"]
    identf = consts["identf"]
    ones1 = consts["ones1"]

    def body():
        ar = ar_mk()
        xn = ar.alloc(NCH * SEQ, BF16)
        ksb = ar.alloc(NCH * SEQ, BF16)
        vt = ar.alloc(16 * 1024, BF16)
        wq_s = ar.alloc(NCH * D, BF16)
        wkv_s = ar.alloc(NCH * D, BF16)
        qsb = ar.alloc(NCH * QB, BF16)
        att_sb = ar.alloc(NCH * QB, BF16)
        bq_t = ar.alloc(NCH, F32)
        bk_t = ar.alloc(NCH, F32)
        bv_t = ar.alloc(NCH, F32)
        bo_t = ar.alloc(NCH, F32) if cfg.get("bo") is not None else None
        lng_t = ar.alloc(NCH, F32) if cfg.get("lng") is not None else None
        lnb_t = ar.alloc(NCH, F32) if cfg.get("lnb") is not None else None
        qtab_t = ar.alloc(SEQ, BF16) if cfg.get("qtab") is not None else None
        ktab_t = ar.alloc(SEQ, BF16) if cfg.get("ktab") is not None else None
        bq_eff = ar.alloc(NCH, F32)
        xblk = ar.alloc(NCH * 512, F32)
        ln = LNBlock(p, ar, 512)
        lntmp = ar.alloc(512, F32)
        s_f = wkv_s.bitcast(F32)[:, 0:2 * SEQ]
        s_bfT = ar.alloc(2 * SEQ, BF16) if sel else None
        s_bf2 = wkv_s.bitcast(BF16)
        e_f = xblk[:, 0:SEQ]
        trash_bf = xblk.bitcast(BF16)[:, 2 * SEQ:3 * SEQ]
        trash2_bf = xblk.bitcast(BF16)[:, 3 * SEQ:4 * SEQ]
        p_t = ar.alloc(2 * SEQ, BF16)
        p_t2 = (p_t, s_bfT if sel else xblk.bitcast(BF16)[:, 2 * SEQ:4 * SEQ])
        pt_t = ar.alloc(2 * QB, BF16)
        wo_s = ar.alloc(2 * D, BF16)
        zb_sb = ar.alloc(QB, F32)
        t4_sb = ar.alloc(128, F32)
        selst = ar.alloc(80, F32)
        zcol = ar.alloc(8, F32)
        vtmp = ar.alloc(2 * 512, BF16)
        ktmp = xblk[:, 3 * 1024:4 * 1024]
        xr = ar.alloc(NCH * QB, F32)


        dw0 = p.cnt["dw"]
        nload = 0
        for nm, t in (("bq", bq_t), ("bk", bk_t), ("bv", bv_t), ("bo", bo_t),
                      ("lng", lng_t), ("lnb", lnb_t), ("qtab", qtab_t),
                      ("ktab", ktab_t)):
            if t is not None:
                p.ins("sync", lambda e, nm=nm, t=t: e.dma_start(out=t, in_=cfg[nm]),
                      incs=(("dw", 16),))
                nload += 1
        for e in ("vector", "scalar", "tensor"):
            p.wait(e, "dw", dw0 + 16 * nload)
        if cfg.get("qscale") is not None:
            p.ins("vector", lambda e: e.memset(consts["qsc"], float(cfg["qscale"])))
            p.ins("vector", lambda e: e.tensor_scalar_mul(
                bq_eff, bq_t, float(cfg["qscale"])), incs=(("dv", 1),))
        else:
            p.ins("vector", lambda e: e.tensor_copy(bq_eff, bq_t), incs=(("dv", 1),))
        p.ins("vector", lambda e: e.memset(ln.bcin[0:2, :], 0.0), incs=(("dv", 1),))
        p.wait("scalar", "dv", p.cnt["dv"])

        for b in range(NB):
            tb = b * SEQ

            for blk in range(4):
                t0 = tb + blk * 512
                dxa = p.cnt["dx"]
                for c in range(NCH):
                    p.ins("gpsimd", lambda e, c=c, t0=t0: e.dma_start(
                        out=xblk[:, c * 512:(c + 1) * 512],
                        in_=stream[c * 128:(c + 1) * 128, t0:t0 + 512]),
                        incs=(("dx", 16),))
                for e in ("tensor", "vector", "scalar"):
                    p.wait(e, "dx", dxa + 16 * NCH)

                def xblk_c(c):
                    return xblk[:, c * 512:(c + 1) * 512]

                if lng_t is None:
                    def xn_dst(c, j, blk=blk):
                        return xn[:, c * SEQ + blk * 512: c * SEQ + blk * 512 + 512]
                    emit_ln(p, ln, consts, xblk_c, xn_dst, ps, consts["zero"])
                else:
                    def xn_dst(c, j):
                        return lntmp[:, 0:512]
                    def post(c, j, blk=blk):
                        p.ins("scalar", lambda e, c=c: e.activation(
                            xn[:, c * SEQ + blk * 512: c * SEQ + blk * 512 + 512],
                            lntmp[:, 0:512], AF.Identity,
                            scale=lng_t[:, c:c + 1], bias=lnb_t[:, c:c + 1]),
                            incs=(("ac", 1),))
                    emit_ln(p, ln, consts, xblk_c, xn_dst, ps, consts["zero"],
                            post=post)
                p.wait("gpsimd", "dv", p.cnt["dv"])

            for wname, btile, isv in (("wv", bv_t, True), ("wk", bk_t, False)):
                p.wait("sync", "dw", p.cnt["dw"])
                dwp = p.cnt["dw"]
                if not isv:
                    p.wait("sync", "pe", p.cnt["pe"])
                p.wait("tensor", "ac", p.cnt["ac"])
                p.wait("tensor", "dv", p.cnt["dv"])
                for m in range(NCH):
                    p.ins("sync", lambda e, m=m, wname=wname: e.dma_start(
                        out=wkv_s[:, m * D:(m + 1) * D],
                        in_=cfg[wname][m * 128:(m + 1) * 128, :]),
                        incs=(("dw", 16),))
                p.wait("tensor", "dw", dwp + 16 * NCH)
                evd = []
                trd = []
                for m in range(NCH):
                    for tq in range(4):
                        si = m * 4 + tq
                        bank = ps[si % 2]
                        if si >= 2:
                            p.wait("tensor", evd[si - 2][0], evd[si - 2][1],
                                   loop=False)
                        for k in range(NCH):
                            if k == 0:
                                p.ins("tensor", lambda e, m=m: e.ldweights(
                                    wkv_s[:, m * D: m * D + 128]))
                            p.ins("tensor", lambda e, m=m, k=k, tq=tq, bank=bank:
                                  e.matmul(
                                      bank[:, :],
                                      wkv_s[:, m * D + k * 128: m * D + k * 128 + 128],
                                      xn[:, k * SEQ + tq * 512: k * SEQ + (tq + 1) * 512],
                                      start=(k == 0), stop=(k == NCH - 1)),
                                  incs=(("pe", 1),) if k == NCH - 1 else ())
                        pe_mm = p.cnt["pe"]
                        p.wait("scalar", "pe", pe_mm)
                        if isv:
                            if si >= 2:
                                p.wait("scalar", "pe", trd[si - 2][0])
                            p.ins("scalar", lambda e, m=m, bank=bank, si=si:
                                  e.activation(
                                      vtmp[:, (si % 2) * 512:(si % 2) * 512 + 512],
                                      bank[:, :], AF.Identity, bias=btile[:, m:m + 1]),
                                  incs=(("ac", 1),))
                            evd.append(("ac", p.cnt["ac"]))
                            p.wait("tensor", "ac", p.cnt["ac"])
                            if si >= 1:
                                p.wait("tensor", "dv", trd[si - 1][1])
                            psT = ps[2][:, :].bitcast(BF16)
                            for i in range(4):
                                p.ins("tensor", lambda e, i=i, si=si: e.transpose(
                                    psT[:, i * 128:(i + 1) * 128],
                                    vtmp[:, (si % 2) * 512 + i * 128:(si % 2) * 512 + (i + 1) * 128],
                                    ident),
                                    incs=(("pe", 1),) if i == 3 else ())
                            trd.append((p.cnt["pe"], None))
                            p.wait("vector", "pe", p.cnt["pe"])
                            for i in range(4):
                                kt = tq * 4 + i
                                p.ins("vector", lambda e, i=i, kt=kt, m=m: e.tensor_copy(
                                    vt[:, kt * 1024 + m * 128: kt * 1024 + (m + 1) * 128],
                                    psT[:, i * 128:(i + 1) * 128]),
                                    incs=(("dv", 1),) if i == 3 else ())
                            trd[si] = (trd[si][0], p.cnt["dv"])
                        else:
                            if ktab_t is None:
                                p.ins("scalar", lambda e, m=m, tq=tq, bank=bank:
                                      e.activation(
                                          ksb[:, m * SEQ + tq * 512: m * SEQ + (tq + 1) * 512],
                                          bank[:, :], AF.Identity, bias=btile[:, m:m + 1]),
                                      incs=(("ac", 1),))
                                evd.append(("ac", p.cnt["ac"]))
                            else:
                                if si >= 2:
                                    p.wait("scalar", "dv", evd[si - 2][1])
                                p.ins("scalar", lambda e, m=m, si=si, bank=bank:
                                      e.activation(
                                          ktmp[:, (si % 2) * 512:(si % 2) * 512 + 512],
                                          bank[:, :], AF.Identity, bias=btile[:, m:m + 1]),
                                      incs=(("ac", 1),))
                                p.wait("vector", "ac", p.cnt["ac"])
                                p.ins("vector", lambda e, m=m, tq=tq, si=si: e.tensor_tensor(
                                    ksb[:, m * SEQ + tq * 512: m * SEQ + (tq + 1) * 512],
                                    ktmp[:, (si % 2) * 512:(si % 2) * 512 + 512],
                                    ktab_t[:, tq * 512:(tq + 1) * 512], ALU.mult),
                                    incs=(("dv", 1),))
                                evd.append(("dv", p.cnt["dv"]))
            p.barrier()

            p.wait("sync", "dw", p.cnt["dw"])
            dwq = p.cnt["dw"]
            for m in range(NCH):
                p.ins("sync", lambda e, m=m: e.dma_start(
                    out=wq_s[:, m * D:(m + 1) * D],
                    in_=cfg["wq"][m * 128:(m + 1) * 128, :]), incs=(("dw", 16),))
            p.wait("tensor", "dw", p.cnt["dw"])

            def qb_body(qb):
                p.selfsync("vector")
                evq = []
                for m in range(NCH):
                    bank = ps[m % 2]
                    if m >= 2:
                        p.wait("tensor", evq[m - 2][0], evq[m - 2][1], loop=True)
                    else:
                        p.wait("tensor", "dv", p.cnt["dv"], loop=True)
                    for k in range(NCH):
                        if k == 0:
                            p.ins("tensor", lambda e, m=m: e.ldweights(
                                wq_s[:, m * D: m * D + 128]))
                        rhs = p.symap("tensor", xn,
                                      AFF((qb, QB), const=k * SEQ), QB)
                        p.ins("tensor", lambda e, m=m, k=k, bank=bank, rhs=rhs:
                              e.matmul(
                                  bank[:, 0:QB],
                                  wq_s[:, m * D + k * 128: m * D + k * 128 + 128],
                                  rhs,
                                  start=(k == 0), stop=(k == NCH - 1)),
                              incs=(("pe", 1),) if k == NCH - 1 else ())
                    p.wait("scalar", "pe", p.cnt["pe"], loop=True)
                    if cfg.get("qscale") is not None:
                        p.ins("scalar", lambda e, m=m, bank=bank: e.activation(
                            qsb[:, m * QB:(m + 1) * QB], bank[:, 0:QB], AF.Identity,
                            scale=consts["qsc"][0:128, 0:1],
                            bias=bq_eff[:, m:m + 1]), incs=(("ac", 1),))
                        evq.append(("ac", p.cnt["ac"]))
                    else:
                        p.wait("scalar", "dv", p.cnt["dv"], loop=True)
                        p.ins("scalar", lambda e, m=m, bank=bank: e.activation(
                            lntmp[:, 0:QB], bank[:, 0:QB], AF.Identity,
                            bias=bq_eff[:, m:m + 1]), incs=(("ac", 1),))
                        p.wait("vector", "ac", p.cnt["ac"], loop=True)
                        qt_ap = p.symap("vector", qtab_t, AFF((qb, QB)), QB)
                        p.ins("vector", lambda e, m=m, qt_ap=qt_ap: e.tensor_tensor(
                            qsb[:, m * QB:(m + 1) * QB], lntmp[:, 0:QB],
                            qt_ap, ALU.mult),
                            incs=(("dv", 1),))
                        evq.append(("dv", p.cnt["dv"]))
                p.wait("gpsimd", "dx", p.cnt["dx"], loop=True)
                p.wait("gpsimd", "dv", p.cnt["dv"], loop=True)
                dxr0 = p.cnt["dx"]
                for c in range(NCH):
                    src_ap = p.symap("gpsimd", stream[c * 128:(c + 1) * 128, :],
                                     AFF((qb, QB), const=tb), QB)
                    p.ins("gpsimd", lambda e, c=c, src_ap=src_ap: e.dma_start(
                        out=xr[:, c * QB:(c + 1) * QB], in_=src_ap),
                        incs=(("dx", 16),))

                p.mark_sf_free = None
                if sel:
                    _attn_sel_pipe(p, consts, cfg, ps, qsb, ksb, vt, s_bf2,
                                   e_f, trash_bf, trash2_bf, p_t2, pt_t,
                                   selst, zcol, ident, identf, t4_sb, zb_sb,
                                   att_sb)
                else:
                    sb3 = (s_bf2[:, 0:2 * SEQ], s_bf2[:, 2 * SEQ:4 * SEQ],
                           xblk.bitcast(BF16)[:, 0:2 * SEQ])
                    _attn_plain_pipe(p, consts, cfg, ps, qsb, ksb, vt, sb3,
                                     p_t2, t4_sb, zb_sb, att_sb)

                evo = []
                for o in range(NCH):
                    slot = "dwA" if o % 2 == 0 else "dwB"
                    if o >= 2:
                        p.wait("sync", evo[o - 2][0], evo[o - 2][1], loop=True)
                    else:
                        p.wait("sync", "dv", p.cnt["dv"], loop=True)
                    p.ins("sync", lambda e, o=o: e.dma_start(
                        out=wo_s[:, (o % 2) * D:(o % 2) * D + D],
                        in_=cfg["wo"][o * 128:(o + 1) * 128, :]),
                        incs=((slot, 16),))
                    bank = ps[o % 2]
                    p.wait("tensor", slot, p.cnt[slot], loop=True)
                    if o < 2:
                        p.wait("tensor", "dv", p.cnt["dv"], loop=True)
                    else:
                        p.wait("tensor", evo[o - 2][0], evo[o - 2][1], loop=True)
                    for k in range(NCH):
                        if k == 0:
                            p.ins("tensor", lambda e, o=o: e.ldweights(
                                wo_s[:, (o % 2) * D: (o % 2) * D + 128]))
                        p.ins("tensor", lambda e, o=o, k=k, bank=bank: e.matmul(
                            bank[:, 0:QB],
                            wo_s[:, (o % 2) * D + k * 128:(o % 2) * D + k * 128 + 128],
                            att_sb[:, k * QB:(k + 1) * QB],
                            start=(k == 0), stop=(k == NCH - 1)),
                            incs=(("pe", 1),) if k == NCH - 1 else ())
                    p.wait("vector", "pe", p.cnt["pe"], loop=True)
                    p.wait("vector", "dx", dxr0 + 16 * NCH, loop=True)
                    if bo_t is not None:
                        p.ins("vector", lambda e, o=o, bank=bank: e.scalar_tensor_tensor(
                            xr[:, o * QB:(o + 1) * QB], bank[:, 0:QB],
                            bo_t[:, o:o + 1], xr[:, o * QB:(o + 1) * QB],
                            op0=ALU.add, op1=ALU.add), incs=(("dv", 1),))
                    else:
                        p.ins("vector", lambda e, o=o, bank=bank: e.tensor_tensor(
                            xr[:, o * QB:(o + 1) * QB], bank[:, 0:QB],
                            xr[:, o * QB:(o + 1) * QB], ALU.add),
                            incs=(("dv", 1),))
                    evo.append(("dv", p.cnt["dv"]))
                p.wait("gpsimd", "dv", p.cnt["dv"], loop=True)
                for c in range(NCH):
                    dst_ap = p.symap("gpsimd", stream[c * 128:(c + 1) * 128, :],
                                     AFF((qb, QB), const=tb), QB)
                    p.ins("gpsimd", lambda e, c=c, dst_ap=dst_ap: e.dma_start(
                        out=dst_ap, in_=xr[:, c * QB:(c + 1) * QB]),
                        incs=(("dx", 16),))

            p.emit_loop(SEQ // QB, qb_body, peel=0)
            p.wait("gpsimd", "dx", p.cnt["dx"])
            p.barrier()
    return body


def _attn_inner(p, consts, cfg, ps, sel, hc, head, qsb, ksb, vt,
                s_f, s_bfT, e_f, trash_bf, trash2_bf, p_t, pt_t, selst, zcol,
                ident):
    KSEL = float(TOPK)
    ev = []
    pe_attv_prev = p.cnt["pe"]
    p.wait("tensor", "ac", p.cnt["ac"], loop=True)
    dv_prev_inst = p.mark_sf_free if (sel and p.mark_sf_free is not None) \
        else p.cnt["dv"]
    for qt in range(2):
        for kb in range(4):
            si = qt * 4 + kb
            bank = ps[si % 2]
            if si >= 2:
                p.wait("tensor", "ac", ev[si - 2], loop=True)
            lhs = qsb[head * 64:(head + 1) * 64,
                      hc * QB + qt * 128: hc * QB + qt * 128 + 128]
            rhs = ksb[head * 64:(head + 1) * 64,
                      hc * SEQ + kb * 512: hc * SEQ + (kb + 1) * 512]
            p.ins("tensor", lambda e, lhs=lhs, head=head: e.ldweights(
                lhs, tile_position=(head * 64, 0)))
            p.ins("tensor", lambda e, lhs=lhs, rhs=rhs, head=head, bank=bank:
                  e.matmul(bank[:, :], lhs, rhs, start=True, stop=True,
                           tile_position=(head * 64, 0)),
                  incs=(("pe", 1),))
            p.wait("scalar", "pe", p.cnt["pe"], loop=True)
            p.selfsync("scalar")
            if si == 0:
                p.wait("scalar", "dv", dv_prev_inst, loop=True)
            p.ins("scalar", lambda e, qt=qt, kb=kb, si=si, bank=bank: e.activation(
                s_f[:, qt * SEQ + kb * 512: qt * SEQ + (kb + 1) * 512],
                bank[:, :], AF.Identity,
                accum_out=selst[:, 24 + si:25 + si]), incs=(("ac", 1),))
            ev.append(p.cnt["ac"])
    p.wait("vector", "ac", p.cnt["ac"], loop=True)

    if sel:
        p.selfsync("vector")
        for qt in range(2):
            p.ins("vector", lambda e, qt=qt: e.tensor_copy(
                s_bfT[:, qt * SEQ:(qt + 1) * SEQ], s_f[:, qt * SEQ:(qt + 1) * SEQ]),
                incs=(("dv", 1),))
        p.mark_sf_free = p.cnt["dv"]
        p.selfsync("vector")
        for qt in range(2):
            p.ins("vector", lambda e, qt=qt: e.reduce_sum(
                selst[:, 16 + qt:17 + qt], selst[:, 24 + qt * 4:28 + qt * 4],
                axis=AX.X), incs=(("dv", 1),))
        p.selfsync("vector")
        p.ins("vector", lambda e: e.tensor_scalar_mul(
            selst[:, 16:18], selst[:, 16:18], 1.0 / SEQ), incs=(("dv", 1),))
        p.selfsync("vector")
        p.ins("vector", lambda e: e.tensor_scalar_mul(
            selst[:, 18:20], selst[:, 16:18], -1.0), incs=(("dv", 1),))
        p.wait("scalar", "dv", p.cnt["dv"], loop=True)
        for qt in range(2):
            p.selfsync("scalar")
            p.ins("scalar", lambda e, qt=qt: e.activation(
                trash2_bf[:, 0:SEQ], s_bfT[:, qt * SEQ:(qt + 1) * SEQ], AF.Abs,
                bias=selst[:, 18 + qt:19 + qt],
                accum_out=selst[:, 32 + qt:33 + qt]), incs=(("ac", 1),))
        p.wait("vector", "ac", p.cnt["ac"], loop=True)
        p.selfsync("vector")
        p.ins("vector", lambda e: e.tensor_scalar_mul(
            selst[:, 20:22], selst[:, 32:34], 1.2533141 / SEQ), incs=(("dv", 1),))
        p.selfsync("vector")
        p.ins("vector", lambda e: e.tensor_scalar_mul(
            selst[:, 22:24], selst[:, 20:22], SEL_DINV), incs=(("dv", 1),))
        p.ins("vector", lambda e: e.scalar_tensor_tensor(
            selst[:, 0:2], selst[:, 20:22], SEL_Q70, selst[:, 16:18],
            op0=ALU.mult, op1=ALU.add), incs=(("dv", 1),))
        for r in range(NE_SEL):
            p.selfsync("vector")
            p.ins("vector", lambda e: e.tensor_scalar_mul(
                selst[:, 38:39], selst[:, 1:2], -1.0), incs=(("dv", 1),))
            p.wait("scalar", "dv", p.cnt["dv"], loop=True)
            p.selfsync("scalar")
            p.ins("scalar", lambda e: e.activation(
                trash2_bf[:, 0:SEQ], s_bfT[:, SEQ:2 * SEQ], AF.Sign,
                bias=selst[:, 38:39],
                accum_out=selst[:, 3:4]), incs=(("ac", 1),))
            p.selfsync("vector")
            p.ins("vector", lambda e: e.tensor_scalar(
                trash_bf[:, 0:SEQ], s_bfT[:, 0:SEQ],
                selst[:, 0:1], 0.0, op0=ALU.is_ge, op1=ALU.add,
                accum_out=selst[:, 2:3]), incs=(("dv", 1),))
            p.selfsync("vector")
            p.ins("vector", lambda e: e.tensor_scalar(
                selst[:, 4:5], selst[:, 2:3], KSEL, None, op0=ALU.subtract),
                incs=(("dv", 1),))
            p.wait("vector", "ac", p.cnt["ac"], loop=True)
            p.ins("vector", lambda e: e.tensor_scalar(
                selst[:, 5:6], selst[:, 3:4], 0.5, SEQ / 2.0 - KSEL,
                op0=ALU.mult, op1=ALU.add), incs=(("dv", 1),))
            p.selfsync("vector")
            if r == 0:
                p.ins("vector", lambda e: e.tensor_copy(
                    selst[:, 6:8], selst[:, 0:2]), incs=(("dv", 1),))
                p.ins("vector", lambda e: e.tensor_tensor(
                    selst[:, 8:10], selst[:, 4:6], selst[:, 4:6], ALU.mult),
                    incs=(("dv", 1),))
            else:
                p.ins("vector", lambda e: e.tensor_tensor(
                    selst[:, 10:12], selst[:, 4:6], selst[:, 4:6], ALU.mult),
                    incs=(("dv", 1),))
                p.selfsync("vector")
                maskv = selst[:, 12:14].bitcast(mybir.dt.uint32)
                p.ins("vector", lambda e, maskv=maskv: e.tensor_tensor(
                    maskv, selst[:, 10:12], selst[:, 8:10], ALU.is_lt),
                    incs=(("dv", 1),))
                p.selfsync("vector")
                p.ins("vector", lambda e, maskv=maskv: e.copy_predicated(
                    selst[:, 6:8], maskv, selst[:, 0:2]),
                    incs=(("dv", 1),))
                p.selfsync("vector")
                p.ins("vector", lambda e, maskv=maskv: e.copy_predicated(
                    selst[:, 8:10], maskv, selst[:, 10:12]),
                    incs=(("dv", 1),))
            if r < NE_SEL - 1:
                p.selfsync("vector")
                p.ins("vector", lambda e: e.tensor_tensor(
                    selst[:, 4:6], selst[:, 4:6], selst[:, 22:24], ALU.mult),
                    incs=(("dv", 1),))
                p.selfsync("vector")
                p.ins("vector", lambda e, r=r: e.scalar_tensor_tensor(
                    selst[:, 0:2], selst[:, 4:6], SEL_LAM[r], selst[:, 0:2],
                    op0=ALU.mult, op1=ALU.add), incs=(("dv", 1),))
        p.selfsync("vector")
        p.ins("vector", lambda e: e.tensor_scalar_mul(
            selst[:, 14:16], selst[:, 6:8], -1.0), incs=(("dv", 1),))
        p.wait("scalar", "dv", p.cnt["dv"], loop=True)
        for qt in range(2):
            p.selfsync("scalar")
            p.ins("scalar", lambda e, qt=qt: e.activation(
                e_f[:, 0:SEQ], s_bfT[:, qt * SEQ:(qt + 1) * SEQ], AF.Exp,
                bias=selst[:, 14 + qt:15 + qt]), incs=(("ac", 1),))
            p.wait("vector", "ac", p.cnt["ac"], loop=True)
            p.wait("vector", "pe", pe_attv_prev, loop=True)
            p.ins("vector", lambda e, qt=qt, head=head: e.scalar_tensor_tensor(
                p_t[:, qt * SEQ:(qt + 1) * SEQ], s_bfT[:, qt * SEQ:(qt + 1) * SEQ],
                selst[:, 6 + qt:7 + qt], e_f[:, 0:SEQ],
                op0=ALU.is_ge, op1=ALU.mult,
                accum_out=zcol[:, head * 2 + qt:head * 2 + qt + 1]),
                incs=(("dv", 1),))
            p.wait("scalar", "dv", p.cnt["dv"], loop=True)
    else:
        p.selfsync("vector")
        for qt in range(2):
            p.ins("vector", lambda e, qt=qt: e.reduce_sum(
                selst[:, 16 + qt:17 + qt], selst[:, 24 + qt * 4:28 + qt * 4],
                axis=AX.X), incs=(("dv", 1),))
        p.selfsync("vector")
        p.ins("vector", lambda e: e.tensor_scalar_mul(
            selst[:, 14:16], selst[:, 16:18], -1.0 / SEQ), incs=(("dv", 1),))
        p.wait("scalar", "dv", p.cnt["dv"], loop=True)
        for qt in range(2):
            p.selfsync("scalar")
            p.wait("scalar", "pe", pe_attv_prev, loop=True)
            p.ins("scalar", lambda e, qt=qt, head=head: e.activation(
                p_t[:, qt * SEQ:(qt + 1) * SEQ], s_f[:, qt * SEQ:(qt + 1) * SEQ],
                AF.Exp, bias=selst[:, 14 + qt:15 + qt],
                accum_out=zcol[:, head * 2 + qt:head * 2 + qt + 1]),
                incs=(("ac", 1),))
        p.wait("vector", "ac", p.cnt["ac"], loop=True)
        p.ins("vector", lambda e: e.nop(), incs=(("dv", 1),))

    dv_p = p.cnt["dv"]
    p.wait("tensor", "dv", dv_p, loop=True)

    def emit_tr(kt):
        ktl = kt % 2
        psT = ps[2 + ktl][:, :].bitcast(BF16)
        for qt in range(2):
            p.ins("tensor", lambda e, qt=qt, kt=kt, psT=psT: e.transpose(
                psT[:, qt * 128:(qt + 1) * 128],
                p_t[:, qt * SEQ + kt * 128: qt * SEQ + (kt + 1) * 128], ident),
                incs=(("pe", 1),) if qt == 1 else ())
        trpe.append(p.cnt["pe"])

    trpe = []
    cpq = []
    emit_tr(0)
    for kt in range(16):
        ktl = kt % 2
        psT = ps[2 + ktl][:, :].bitcast(BF16)
        p.wait("vector", "pe", trpe[kt], loop=True)
        p.selfsync("vector")
        p.ins("vector", lambda e, ktl=ktl, psT=psT: e.tensor_copy(
            pt_t[:, ktl * QB:(ktl + 1) * QB], psT[:, 0:QB]),
            incs=(("dv", 1),))
        cpq.append(p.cnt["dv"])
        if kt < 15:
            emit_tr(kt + 1)
        p.wait("tensor", "dv", cpq[kt], loop=True)
        vtl = vt[:, kt * 1024 + hc * 128 + head * 64:
                 kt * 1024 + hc * 128 + head * 64 + 64]
        p.ins("tensor", lambda e, vtl=vtl, head=head: e.ldweights(
            vtl, tile_position=(0, head * 64)))
        p.ins("tensor", lambda e, vtl=vtl, ktl=ktl, head=head, kt=kt: e.matmul(
            ps[4][head * 64:(head + 1) * 64, 0:QB],
            vtl, pt_t[:, ktl * QB:(ktl + 1) * QB],
            start=(kt == 0), stop=(kt == 15),
            tile_position=(0, head * 64)),
            incs=(("pe", 1),))


def _attn_sel_pipe(p, consts, cfg, ps, qsb, ksb, vt, s_bf2, e_f, trash_bf,
                   trash2_bf, p_t2, pt_t, selst, zcol, ident, identf,
                   t4_sb, zb_sb, att_sb):
    KSEL = float(TOPK)
    NI = 16
    evt = {}
    ev8 = {}
    mmd = {}
    muad = {}
    absd = {}
    updd = {}
    prepd = {}
    sttd0 = {}
    find = {}
    attvd = {}
    pe0 = p.cnt["pe"]
    dv0 = p.cnt["dv"]

    def sbank(i):
        b = (i % 2) * 2 * SEQ
        return s_bf2[:, b:b + 2 * SEQ]

    def mm1(i, si):
        hc, head = i // 2, i % 2
        qt, kb = si // 4, si % 4
        bank = ps[si % 2]
        if si >= 2:
            p.wait("tensor", "ac", evt[(i, si - 2)], loop=True)
        elif i >= 1:
            p.wait("tensor", "ac", evt[(i - 1, si + 6)], loop=True)
        else:
            p.wait("tensor", "ac", p.cnt["ac"], loop=True)
        lhs = qsb[head * 64:(head + 1) * 64,
                  hc * QB + qt * 128: hc * QB + qt * 128 + 128]
        rhs = ksb[head * 64:(head + 1) * 64,
                  hc * SEQ + kb * 512: hc * SEQ + (kb + 1) * 512]
        p.ins("tensor", lambda e, lhs=lhs, head=head: e.ldweights(
            lhs, tile_position=(head * 64, 0)))
        p.ins("tensor", lambda e, lhs=lhs, rhs=rhs, head=head, bank=bank:
              e.matmul(bank[:, :], lhs, rhs, start=True, stop=True,
                       tile_position=(head * 64, 0)), incs=(("pe", 1),))
        mmd[(i, si)] = p.cnt["pe"]

    def evac1(i, si):
        sb = 40 * (i % 2)
        qt, kb = si // 4, si % 4
        bank = ps[si % 2]
        p.wait("scalar", "pe", mmd[(i, si)], loop=True)
        p.selfsync("scalar")
        if si == 0:
            if i >= 2:
                p.wait("scalar", "dv", find[i - 2], loop=True)
            else:
                p.wait("scalar", "dv", dv0, loop=True)
        sbk = sbank(i)
        p.ins("scalar", lambda e, qt=qt, kb=kb, si=si, bank=bank, sb=sb,
              sbk=sbk: e.activation(
                  sbk[:, qt * SEQ + kb * 512: qt * SEQ + (kb + 1) * 512],
                  bank[:, :], AF.Identity,
                  accum_out=selst[:, sb + 24 + si: sb + 25 + si]),
              incs=(("ac", 1),))
        evt[(i, si)] = p.cnt["ac"]
        if si == 7:
            ev8[i] = p.cnt["ac"]

    def sevac_a(i):
        mm1(i, 0)
        mm1(i, 1)
        evac1(i, 0)
        mm1(i, 2)
        evac1(i, 1)
        mm1(i, 3)
        evac1(i, 2)

    def sevac_b(i):
        mm1(i, 4)
        evac1(i, 3)
        mm1(i, 5)
        evac1(i, 4)
        mm1(i, 6)
        evac1(i, 5)
        mm1(i, 7)
        evac1(i, 6)
        evac1(i, 7)

    def muA(i):
        sb = 40 * (i % 2)
        p.wait("vector", "ac", ev8[i], loop=True)
        p.selfsync("vector")
        for qt in range(2):
            p.ins("vector", lambda e, qt=qt, sb=sb: e.reduce_sum(
                selst[:, sb + 16 + qt: sb + 17 + qt],
                selst[:, sb + 24 + qt * 4: sb + 28 + qt * 4],
                axis=AX.X), incs=(("dv", 1),))
        p.selfsync("vector")
        p.ins("vector", lambda e, sb=sb: e.tensor_scalar_mul(
            selst[:, sb + 16: sb + 18], selst[:, sb + 16: sb + 18],
            1.0 / SEQ), incs=(("dv", 1),))
        p.selfsync("vector")
        p.ins("vector", lambda e, sb=sb: e.tensor_scalar_mul(
            selst[:, sb + 18: sb + 20], selst[:, sb + 16: sb + 18], -1.0),
            incs=(("dv", 1),))
        muad[i] = p.cnt["dv"]

    def absA(i):
        sb = 40 * (i % 2)
        sbk = sbank(i)
        p.wait("scalar", "dv", muad[i], loop=True)
        for qt in range(2):
            p.selfsync("scalar")
            p.ins("scalar", lambda e, qt=qt, sb=sb, sbk=sbk: e.activation(
                trash2_bf[:, 0:SEQ // 2], sbk[:, qt * SEQ: qt * SEQ + SEQ // 2],
                AF.Abs, bias=selst[:, sb + 18 + qt: sb + 19 + qt],
                accum_out=selst[:, sb + 32 + qt: sb + 33 + qt]),
                incs=(("ac", 1),))
        absd[i] = p.cnt["ac"]

    def sigt(i):
        sb = 40 * (i % 2)
        p.wait("vector", "ac", absd[i], loop=True)
        p.selfsync("vector")
        p.ins("vector", lambda e, sb=sb: e.tensor_scalar_mul(
            selst[:, sb + 20: sb + 22], selst[:, sb + 32: sb + 34],
            2.0 * 1.2533141 / SEQ), incs=(("dv", 1),))
        p.selfsync("vector")
        p.ins("vector", lambda e, sb=sb: e.tensor_scalar_mul(
            selst[:, sb + 22: sb + 24], selst[:, sb + 20: sb + 22], SEL_DINV),
            incs=(("dv", 1),))
        p.ins("vector", lambda e, sb=sb: e.scalar_tensor_tensor(
            selst[:, sb + 0: sb + 2], selst[:, sb + 20: sb + 22], SEL_Q70,
            selst[:, sb + 16: sb + 18], op0=ALU.mult, op1=ALU.add),
            incs=(("dv", 1),))
        updd[i] = p.cnt["dv"]

    def rnd(i, r):
        sb = 40 * (i % 2)
        sbk = sbank(i)
        W = SEQ // 2 if r < 2 else SEQ
        p.wait("scalar", "dv", updd[i], loop=True)
        p.selfsync("scalar")
        p.ins("scalar", lambda e, sb=sb, sbk=sbk, W=W: e.activation(
            trash2_bf[:, 0:W], sbk[:, SEQ:SEQ + W], AF.Sign,
            scale=consts["negone"][0:128, 0:1],
            bias=selst[:, sb + 1: sb + 2],
            accum_out=selst[:, sb + 3: sb + 4]), incs=(("ac", 1),))
        p.selfsync("vector")
        p.ins("vector", lambda e, sb=sb, sbk=sbk, W=W: e.tensor_scalar(
            trash_bf[:, 0:W], sbk[:, 0:W],
            selst[:, sb + 0: sb + 1], 0.0, op0=ALU.is_ge, op1=ALU.add,
            accum_out=selst[:, sb + 2: sb + 3]), incs=(("dv", 1),))
        p.selfsync("vector")
        if r < 2:
            p.ins("vector", lambda e, sb=sb: e.tensor_scalar(
                selst[:, sb + 4: sb + 5], selst[:, sb + 2: sb + 3], 2.0,
                -KSEL, op0=ALU.mult, op1=ALU.add), incs=(("dv", 1),))
        else:
            p.ins("vector", lambda e, sb=sb: e.tensor_scalar(
                selst[:, sb + 4: sb + 5], selst[:, sb + 2: sb + 3], KSEL,
                None, op0=ALU.subtract), incs=(("dv", 1),))
        p.wait("vector", "ac", p.cnt["ac"], loop=True)
        p.ins("vector", lambda e, sb=sb, W=W: e.tensor_scalar(
            selst[:, sb + 5: sb + 6], selst[:, sb + 3: sb + 4],
            -0.5 * SEQ / W, SEQ / 2.0 - KSEL, op0=ALU.mult, op1=ALU.add),
            incs=(("dv", 1),))
        p.selfsync("vector")
        if r < 2:
            pass
        elif r == 2:
            p.ins("vector", lambda e, sb=sb: e.tensor_copy(
                selst[:, sb + 6: sb + 8], selst[:, sb + 0: sb + 2]),
                incs=(("dv", 1),))
            p.ins("vector", lambda e, sb=sb: e.tensor_tensor(
                selst[:, sb + 8: sb + 10], selst[:, sb + 4: sb + 6],
                selst[:, sb + 4: sb + 6], ALU.mult), incs=(("dv", 1),))
        else:
            p.ins("vector", lambda e, sb=sb: e.tensor_tensor(
                selst[:, sb + 10: sb + 12], selst[:, sb + 4: sb + 6],
                selst[:, sb + 4: sb + 6], ALU.mult), incs=(("dv", 1),))
            p.selfsync("vector")
            maskv = selst[:, sb + 12: sb + 14].bitcast(mybir.dt.uint32)
            p.ins("vector", lambda e, maskv=maskv, sb=sb: e.tensor_tensor(
                maskv, selst[:, sb + 10: sb + 12], selst[:, sb + 8: sb + 10],
                ALU.is_lt), incs=(("dv", 1),))
            p.selfsync("vector")
            p.ins("vector", lambda e, maskv=maskv, sb=sb: e.copy_predicated(
                selst[:, sb + 6: sb + 8], maskv, selst[:, sb + 0: sb + 2]),
                incs=(("dv", 1),))
            p.selfsync("vector")
            p.ins("vector", lambda e, maskv=maskv, sb=sb: e.copy_predicated(
                selst[:, sb + 8: sb + 10], maskv,
                selst[:, sb + 10: sb + 12]), incs=(("dv", 1),))
        if r < NE_SEL - 1:
            p.selfsync("vector")
            p.ins("vector", lambda e, sb=sb: e.tensor_tensor(
                selst[:, sb + 4: sb + 6], selst[:, sb + 4: sb + 6],
                selst[:, sb + 22: sb + 24], ALU.mult), incs=(("dv", 1),))
            p.selfsync("vector")
            p.ins("vector", lambda e, r=r, sb=sb: e.scalar_tensor_tensor(
                selst[:, sb + 0: sb + 2], selst[:, sb + 4: sb + 6],
                SEL_LAM[r], selst[:, sb + 0: sb + 2],
                op0=ALU.mult, op1=ALU.add), incs=(("dv", 1),))
        updd[i] = p.cnt["dv"]

    def finals_prep(i):
        sb = 40 * (i % 2)
        p.selfsync("vector")
        p.ins("vector", lambda e, sb=sb: e.tensor_scalar_mul(
            selst[:, sb + 14: sb + 16], selst[:, sb + 6: sb + 8], -1.0),
            incs=(("dv", 1),))
        prepd[i] = p.cnt["dv"]

    def finals_qt(i, qt):
        sb = 40 * (i % 2)
        sbk = sbank(i)
        pt = p_t2[i % 2]
        head = i % 2
        p.wait("scalar", "dv", prepd[i] if qt == 0 else sttd0[i], loop=True)
        p.selfsync("scalar")
        p.ins("scalar", lambda e, qt=qt, sb=sb, sbk=sbk: e.activation(
            e_f[:, 0:SEQ], sbk[:, qt * SEQ:(qt + 1) * SEQ], AF.Exp,
            bias=selst[:, sb + 14 + qt: sb + 15 + qt]), incs=(("ac", 1),))
        p.wait("vector", "ac", p.cnt["ac"], loop=True)
        p.wait("vector", "pe", attvd.get(i - 2, pe0), loop=True)
        p.selfsync("vector")
        p.ins("vector", lambda e, qt=qt, head=head, sb=sb, sbk=sbk, pt=pt:
              e.scalar_tensor_tensor(
                  pt[:, qt * SEQ:(qt + 1) * SEQ],
                  sbk[:, qt * SEQ:(qt + 1) * SEQ],
                  selst[:, sb + 6 + qt: sb + 7 + qt], e_f[:, 0:SEQ],
                  op0=ALU.is_ge, op1=ALU.mult,
                  accum_out=zcol[:, head * 2 + qt: head * 2 + qt + 1]),
              incs=(("dv", 1),))
        if qt == 0:
            sttd0[i] = p.cnt["dv"]
        else:
            find[i] = p.cnt["dv"]

    def attv(i):
        hc, head = i // 2, i % 2
        pt = p_t2[i % 2]
        dv_p = p.cnt["dv"]
        p.wait("tensor", "dv", dv_p, loop=True)
        trpe = []
        cpq = []

        def emit_tr(k):
            psT = ps[2 + k % 2][:, :].bitcast(BF16)
            for kh in range(2):
                kt = 2 * k + kh
                for qt in range(2):
                    p.ins("tensor", lambda e, qt=qt, kt=kt, kh=kh, psT=psT,
                          pt=pt: e.transpose(
                              psT[:, kh * QB + qt * 128: kh * QB + (qt + 1) * 128],
                              pt[:, qt * SEQ + kt * 128: qt * SEQ + (kt + 1) * 128],
                              ident), incs=(("pe", 1),)
                          if (kh == 1 and qt == 1) else ())
            trpe.append(p.cnt["pe"])

        emit_tr(0)
        mmk = []
        for k in range(8):
            psT = ps[2 + k % 2][:, :].bitcast(BF16)
            p.wait("vector", "pe", mmk[k - 1] if k >= 1 else trpe[0],
                   loop=True)
            p.selfsync("vector")
            p.ins("vector", lambda e, psT=psT: e.tensor_copy(
                pt_t[:, 0:2 * QB], psT[:, 0:2 * QB]),
                incs=(("dv", 1),))
            cpq.append(p.cnt["dv"])
            if k < 7:
                emit_tr(k + 1)
            p.wait("tensor", "dv", cpq[k], loop=True)
            for kh in range(2):
                kt = 2 * k + kh
                vtl = vt[:, kt * 1024 + hc * 128 + head * 64:
                         kt * 1024 + hc * 128 + head * 64 + 64]
                p.ins("tensor", lambda e, vtl=vtl, head=head: e.ldweights(
                    vtl, tile_position=(0, head * 64)))
                p.ins("tensor", lambda e, vtl=vtl, kh=kh, head=head, kt=kt:
                      e.matmul(
                          ps[4][head * 64:(head + 1) * 64, 0:QB],
                          vtl, pt_t[:, kh * QB:(kh + 1) * QB],
                          start=(kt == 0), stop=(kt == 15),
                          tile_position=(0, head * 64)),
                      incs=(("pe", 1),))
            mmk.append(p.cnt["pe"])
        attvd[i] = p.cnt["pe"]

    sevac_a(0)
    sevac_b(0)
    muA(0)
    absA(0)
    sigt(0)
    for i in range(NI):
        rnd(i, 0)
        if i >= 1:
            finals_qt(i - 1, 0)
        rnd(i, 1)
        if i >= 1:
            finals_qt(i - 1, 1)
        rnd(i, 2)
        if i + 1 < NI:
            sevac_a(i + 1)
        if i >= 1:
            attv(i - 1)
            if (i - 1) % 2 == 1:
                _zb_and_evac(p, consts, ps, (i - 1) // 2, zcol, t4_sb,
                             zb_sb, att_sb, identf)
        rnd(i, 3)
        if i + 1 < NI:
            sevac_b(i + 1)
            muA(i + 1)
        rnd(i, 4)
        if i + 1 < NI:
            absA(i + 1)
        finals_prep(i)
        if i + 1 < NI:
            sigt(i + 1)
    finals_qt(NI - 1, 0)
    finals_qt(NI - 1, 1)
    attv(NI - 1)
    _zb_and_evac(p, consts, ps, (NI - 1) // 2, zcol, t4_sb, zb_sb, att_sb,
                 identf)


def _attn_plain_pipe(p, consts, cfg, ps, qsb, ksb, vt, sb3, p_t2, t4_sb,
                     zb_sb, att_sb):
    NHC = NCH
    evt = {}
    mmd = {}
    expi = {}
    expall = {}
    attvd = {}
    ttd = {}
    zbed = {}
    pe0 = p.cnt["pe"]
    ac0 = p.cnt["ac"]
    dv0 = p.cnt["dv"]
    zr = t4_sb.bitcast(BF16)

    p.selfsync("vector")
    p.ins("vector", lambda e: e.memset(zr[:, :], 0.0), incs=(("dv", 1),))

    def sbank(hc, h):
        return sb3[(hc * 2 + h) % 3]

    def mmS(hc, h, kt):
        bank = ps[h * 2 + (kt // 2) % 2]
        col = (kt % 2) * QB
        kk = kt // 2
        if kk >= 2:
            sem, cnt = evt[(hc, h, kk - 2)]
            p.wait("tensor", sem, cnt, loop=True)
        elif hc >= 1:
            sem, cnt = evt[(hc - 1, h, 6 + kk)]
            p.wait("tensor", sem, cnt, loop=True)
        else:
            p.wait("tensor", "ac", p.cnt["ac"], loop=True)
            p.wait("tensor", "dv", p.cnt["dv"], loop=True)
        lhs = ksb[h * 64:(h + 1) * 64,
                  hc * SEQ + kt * 128: hc * SEQ + (kt + 1) * 128]
        rhs = qsb[h * 64:(h + 1) * 64, hc * QB: hc * QB + QB]
        p.ins("tensor", lambda e, lhs=lhs, h=h: e.ldweights(
            lhs, tile_position=(h * 64, 0)))
        p.ins("tensor", lambda e, lhs=lhs, rhs=rhs, h=h, bank=bank, col=col:
              e.matmul(bank[:, col:col + QB], lhs, rhs, start=True, stop=True,
                       tile_position=(h * 64, 0)), incs=(("pe", 1),))
        mmd[(hc, h, kt)] = p.cnt["pe"]

    def evac(hc, h, k):
        bank = ps[h * 2 + k % 2]
        sbk = sbank(hc, h)
        use_act = ((h * 8 + k) % 3 == 2)
        eng = "scalar" if use_act else "vector"
        p.wait(eng, "pe", mmd[(hc, h, k * 2 + 1)], loop=True)
        p.selfsync(eng)
        if k == 0:
            j = hc * 2 + h - 3
            tgt = expi[(j // 2, j % 2)] if j >= 0 else ac0
            p.wait("scalar", "ac", tgt, loop=True)
            p.wait("vector", "ac", tgt, loop=True)
        if use_act:
            p.ins("scalar", lambda e, k=k, bank=bank, sbk=sbk: e.activation(
                sbk[:, k * 512:(k + 1) * 512], bank[:, :], AF.Identity),
                incs=(("ac", 1),))
            evt[(hc, h, k)] = ("ac", p.cnt["ac"])
        else:
            p.ins("vector", lambda e, k=k, bank=bank, sbk=sbk: e.tensor_copy(
                sbk[:, k * 512:(k + 1) * 512], bank[:, :]),
                incs=(("dv", 1),))
            evt[(hc, h, k)] = ("dv", p.cnt["dv"])

    def sev(hc):
        for k in range(8):
            mmS(hc, 0, 2 * k)
            mmS(hc, 1, 2 * k)
            mmS(hc, 0, 2 * k + 1)
            mmS(hc, 1, 2 * k + 1)
            evac(hc, 0, k)
            evac(hc, 1, k)

    def exps(hc):
        p.wait("scalar", "pe", attvd.get(hc - 1, pe0), loop=True)
        for h in range(2):
            sbk = sbank(hc, h)
            Eb = p_t2[h]
            for k in range(8):
                sem, cnt = evt[(hc, h, k)]
                if sem == "dv":
                    p.wait("scalar", "dv", cnt, loop=True)
                p.selfsync("scalar")
                p.ins("scalar", lambda e, k=k, sbk=sbk, Eb=Eb: e.activation(
                    Eb[:, k * 512:(k + 1) * 512],
                    sbk[:, k * 512:(k + 1) * 512], AF.Exp),
                    incs=(("ac", 1),))
            expi[(hc, h)] = p.cnt["ac"]
        expall[hc] = p.cnt["ac"]

    def attvz(hc):
        p.wait("tensor", "dv", ttd.get(hc - 1, dv0), loop=True)
        p.wait("tensor", "ac", zbed.get(hc - 1, ac0), loop=True)
        p.wait("tensor", "ac", expall[hc], loop=True)
        for kt in range(16):
            for h in range(2):
                vtl = vt[:, kt * 1024 + hc * 128 + h * 64:
                         kt * 1024 + hc * 128 + h * 64 + 64]
                Eb = p_t2[h]
                out = ps[4 + h][h * 64:(h + 1) * 64, 0:QB]
                p.ins("tensor", lambda e, vtl=vtl, h=h: e.ldweights(
                    vtl, tile_position=(0, h * 64)))
                p.ins("tensor", lambda e, vtl=vtl, h=h, kt=kt, Eb=Eb, out=out:
                      e.matmul(out, vtl, Eb[:, kt * QB:(kt + 1) * QB],
                               start=(kt == 0), stop=(kt == 15),
                               tile_position=(0, h * 64)),
                      incs=(("pe", 1),))
            for h in range(2):
                Eb = p_t2[h]
                zout = ps[6 + h][h * 64:h * 64 + 64, 0:QB]
                p.ins("tensor", lambda e, h=h: e.ldweights(
                    consts["onesz"], tile_position=(0, h * 64)))
                p.ins("tensor", lambda e, h=h, kt=kt, Eb=Eb, zout=zout:
                      e.matmul(zout, consts["onesz"],
                               Eb[:, kt * QB:(kt + 1) * QB],
                               start=(kt == 0), stop=(kt == 15),
                               tile_position=(0, h * 64)),
                      incs=(("pe", 1),))
        attvd[hc] = p.cnt["pe"]

    def zb(hc):
        p.wait("vector", "pe", attvd[hc], loop=True)
        p.selfsync("vector")

        def _recip(e, out, in_):
            with p.nc.allow_low_precision(reason="bf16 Zinv broadcast"):
                return e.reciprocal(out, in_)
        p.ins("vector", lambda e: _recip(
            e, zr[0:1, 0:QB], ps[6][0:1, 0:QB]), incs=(("dv", 1),))
        p.ins("vector", lambda e: _recip(
            e, zr[64:65, 0:QB], ps[7][64:65, 0:QB]), incs=(("dv", 1),))
        p.wait("tensor", "dv", p.cnt["dv"], loop=True)
        p.ins("tensor", lambda e: e.matmul(
            ps[6][0:64, QB:2 * QB], consts["oneb0"], zr[:, 0:QB],
            start=True, stop=True, tile_position=(0, 0)), incs=(("pe", 1),))
        p.ins("tensor", lambda e: e.matmul(
            ps[7][64:128, QB:2 * QB], consts["oneb64"], zr[:, 0:QB],
            start=True, stop=True, tile_position=(0, 64)), incs=(("pe", 1),))
        p.wait("scalar", "pe", p.cnt["pe"], loop=True)
        p.selfsync("scalar")
        p.ins("scalar", lambda e: e.activation(
            zb_sb[0:64, 0:QB], ps[6][0:64, QB:2 * QB], AF.Identity),
            incs=(("ac", 1),))
        p.ins("scalar", lambda e: e.activation(
            zb_sb[64:128, 0:QB], ps[7][64:128, QB:2 * QB], AF.Identity),
            incs=(("ac", 1),))
        zbed[hc] = p.cnt["ac"]
        p.wait("vector", "ac", p.cnt["ac"], loop=True)
        p.selfsync("vector")
        p.ins("vector", lambda e, hc=hc: e.tensor_tensor(
            att_sb[0:64, hc * QB:(hc + 1) * QB], ps[4][0:64, 0:QB],
            zb_sb[0:64, 0:QB], ALU.mult), incs=(("dv", 1),))
        p.ins("vector", lambda e, hc=hc: e.tensor_tensor(
            att_sb[64:128, hc * QB:(hc + 1) * QB], ps[5][64:128, 0:QB],
            zb_sb[64:128, 0:QB], ALU.mult), incs=(("dv", 1),))
        ttd[hc] = p.cnt["dv"]

    sev(0)
    for hc in range(NHC):
        exps(hc)
        if hc + 1 < NHC:
            sev(hc + 1)
        attvz(hc)
        zb(hc)


def _zb_and_evac(p, consts, ps, hc, zcol, t4_sb, zb_sb, att_sb, identf):
    p.wait("vector", "ac", p.cnt["ac"], loop=True)
    p.selfsync("vector")
    p.ins("vector", lambda e: e.reciprocal(zcol[:, 4:8], zcol[:, 0:4]),
          incs=(("dv", 1),))
    p.wait("tensor", "dv", p.cnt["dv"], loop=True)
    p.wait("tensor", "ac", p.cnt["ac"], loop=True)
    p.ins("tensor", lambda e: e.transpose(
        ps[5][0:4, 0:128], zcol[:, 4:8], identf), incs=(("pe", 1),))
    p.wait("vector", "pe", p.cnt["pe"], loop=True)
    p.selfsync("vector")
    p.ins("vector", lambda e: e.tensor_copy(
        t4_sb[0:4, 0:128], ps[5][0:4, 0:128]), incs=(("dv", 1),))
    p.wait("tensor", "dv", p.cnt["dv"], loop=True)
    for qt in range(2):
        p.ins("tensor", lambda e, qt=qt: e.matmul(
            ps[5][:, 256 + qt * 128: 256 + (qt + 1) * 128],
            consts["lq"][0:4, qt * 128:(qt + 1) * 128],
            t4_sb[0:4, 0:128], start=True, stop=True),
            incs=(("pe", 1),) if qt == 1 else ())
    p.wait("scalar", "pe", p.cnt["pe"], loop=True)
    p.selfsync("scalar")
    p.ins("scalar", lambda e: e.activation(
        zb_sb[:, 0:QB], ps[5][:, 256:256 + QB], AF.Identity), incs=(("ac", 1),))
    p.wait("vector", "ac", p.cnt["ac"], loop=True)
    p.selfsync("vector")
    p.ins("vector", lambda e, hc=hc: e.tensor_tensor(
        att_sb[:, hc * QB:(hc + 1) * QB], ps[4][:, 0:QB], zb_sb[:, 0:QB],
        ALU.mult), incs=(("dv", 1),))
    p.wait("tensor", "dv", p.cnt["dv"], loop=True)



def phase_moe(p, ar_mk, consts, xsc, dram, ps):
    HALF = 2048
    ones1 = consts["ones1"]
    ones2 = consts["ones2"]
    identf = consts["identf"]
    ident = consts["ident"]

    def body():
        ar = ar_mk()
        xn = ar.alloc(NCH * HALF, BF16)
        acc = ar.alloc(NCH * HALF, BF16)
        w1s = ar.alloc(NCH * D, BF16)
        w2s = ar.alloc(NCH * D, BF16)
        h_sb = ar.alloc(8 * D, BF16)
        hn_q = ar.alloc(4 * D, BF16)
        trash = ar.alloc(D, BF16)
        hT = ar.alloc(NCH * 512, BF16)
        xblk = ar.alloc(NCH * 512, F32)
        hn_q2 = xblk.bitcast(BF16)[:, 0:4 * D]
        ln = LNBlock(p, ar, 512)
        b1b = ar.alloc(D, F32)
        g1b = ar.alloc(D, BF16)
        beb = ar.alloc(D, BF16)
        rows1 = ar.alloc(3 * D, F32)
        wr_s = ar.alloc(NCH * 16, BF16)
        br_s = ar.alloc(1, F32)
        lg_sb = ar.alloc(512, F32)
        ltk = ar.alloc(16, F32)
        t8 = ar.alloc(8, F32)
        gtk = ar.alloc(16, F32)
        scr = ar.alloc(16, F32)
        st_tok = ar.alloc(32, F32)
        gates_all = ar.alloc(16 * 16, F32)
        b2all = ar.alloc(1024, F32)
        gfm = ar.alloc(HALF, F32)

        dw0 = p.cnt["dw"]
        p.wait("sync", "dw", dw0)
        p.ins("sync", lambda e: e.dma_start(out=b2all[0:16, :], in_=dram["b2n"]),
              incs=(("dw", 16),))
        p.ins("sync", lambda e: e.dma_start(out=br_s[0:16, 0:1], in_=dram["brv"]),
              incs=(("dw", 16),))
        p.ins("gpsimd", lambda e: e.dma_start(
            out=wr_s[:, :].rearrange("p (k n) -> p k n", k=NCH),
            in_=dram["wr"][:, :].rearrange("(k p) n -> p k n", p=128)),
            incs=(("dx", 16),))
        p.wait("gpsimd", "dx", p.cnt["dx"])
        for e in ("vector", "scalar", "tensor"):
            p.wait(e, "dw", p.cnt["dw"])
            p.wait(e, "dx", p.cnt["dx"])
        p.ins("vector", lambda e: e.memset(ln.bcin[0:2, :], 0.0))
        p.ins("vector", lambda e: e.memset(rows1[0:2, :], 0.0), incs=(("dv", 1),))

        for half in range(2):
            t0 = half * HALF
            for blk in range(4):
                tb0 = t0 + blk * 512
                dxa = p.cnt["dx"]
                for c in range(NCH):
                    p.ins("gpsimd", lambda e, c=c, tb0=tb0: e.dma_start(
                        out=xblk[:, c * 512:(c + 1) * 512],
                        in_=xsc[c * 128:(c + 1) * 128, tb0:tb0 + 512]),
                        incs=(("dx", 16),))
                for e in ("tensor", "vector", "scalar"):
                    p.wait(e, "dx", dxa + 16 * NCH)

                def xblk_c(c):
                    return xblk[:, c * 512:(c + 1) * 512]

                def xn_dst(c, j, blk=blk):
                    return xn[:, c * HALF + blk * 512: c * HALF + blk * 512 + 512]
                emit_ln(p, ln, consts, xblk_c, xn_dst, ps, consts["zero"])
                p.wait("gpsimd", "dv", p.cnt["dv"])

            for tq in range(4):
                bank = ps[6]
                p.wait("tensor", "ac", p.cnt["ac"])
                for k in range(NCH):
                    p.ins("tensor", lambda e, k=k, tq=tq, bank=bank: e.matmul(
                        bank[0:16, :],
                        wr_s[:, k * 16:(k + 1) * 16],
                        xn[:, k * HALF + tq * 512: k * HALF + (tq + 1) * 512],
                        start=(k == 0), stop=(k == NCH - 1)),
                        incs=(("pe", 1),) if k == NCH - 1 else ())
                p.wait("scalar", "pe", p.cnt["pe"])
                p.selfsync("scalar")
                p.ins("scalar", lambda e, bank=bank: e.activation(
                    lg_sb[0:16, :], bank[0:16, :], AF.Identity,
                    bias=br_s[0:16, 0:1]), incs=(("ac", 1),))
                p.wait("tensor", "ac", p.cnt["ac"])
                for tile in range(4):
                    tt = tq * 4 + tile
                    p.wait("tensor", "dv", p.cnt["dv"])
                    p.ins("tensor", lambda e, tile=tile: e.transpose(
                        ps[5][:, 0:16],
                        lg_sb[0:16, tile * 128:(tile + 1) * 128],
                        identf[0:16, 0:16]), incs=(("pe", 1),))
                    p.wait("vector", "pe", p.cnt["pe"])
                    p.selfsync("vector")
                    p.ins("vector", lambda e: e.tensor_copy(
                        ltk[:, 0:16], ps[5][:, 0:16]), incs=(("dv", 1),))
                    p.selfsync("vector")
                    p.ins("vector", lambda e: e.max(out=t8[:, 0:8], in_=ltk[:, 0:16]),
                          incs=(("dv", 1),))
                    p.selfsync("vector")
                    p.ins("vector", lambda e: e.tensor_scalar_mul(
                        scr[:, 0:1], t8[:, 0:1], -1.0), incs=(("dv", 1),))
                    p.wait("scalar", "dv", p.cnt["dv"])
                    p.selfsync("scalar")
                    p.ins("scalar", lambda e: e.activation(
                        gtk[:, 0:16], ltk[:, 0:16], AF.Exp,
                        bias=scr[:, 0:1]), incs=(("ac", 1),))
                    p.wait("vector", "ac", p.cnt["ac"])
                    p.selfsync("vector")
                    p.ins("vector", lambda e: e.scalar_tensor_tensor(
                        gtk[:, 0:16], ltk[:, 0:16], t8[:, 1:2], gtk[:, 0:16],
                        op0=ALU.is_ge, op1=ALU.mult,
                        accum_out=scr[:, 1:2]), incs=(("dv", 1),))
                    p.selfsync("vector")
                    p.ins("vector", lambda e: e.reciprocal(
                        scr[:, 2:3], scr[:, 1:2]), incs=(("dv", 1),))
                    p.selfsync("vector")
                    p.ins("vector", lambda e, tt=tt: e.tensor_scalar(
                        gates_all[:, tt * 16:(tt + 1) * 16], gtk[:, 0:16],
                        scr[:, 2:3], None, op0=ALU.mult), incs=(("dv", 1),))
                    p.wait("tensor", "dv", p.cnt["dv"])
                    p.ins("tensor", lambda e, tt=tt: e.transpose(
                        ps[5][0:16, 0:128],
                        gates_all[:, tt * 16:(tt + 1) * 16], identf),
                        incs=(("pe", 1),))
                    p.wait("vector", "pe", p.cnt["pe"])
                    p.selfsync("vector")
                    p.ins("vector", lambda e, tt=tt: e.tensor_copy(
                        gfm[0:16, tt * 128:(tt + 1) * 128], ps[5][0:16, 0:128]),
                        incs=(("dv", 1),))
            p.selfsync("vector")
            p.ins("vector", lambda e: e.memset(acc, 0.0), incs=(("dv", 1),))

            evd = {}
            sqd = {}
            gtd = {}
            trd = {}
            cpd = {}
            accd = {}

            def e_body(ei):
                ac_e0 = p.cnt["ac"]
                sl = 0
                p.wait("sync", "pe", p.cnt["pe"], loop=True)
                wpat = [[D, 128], [128 * D, NCH], [1, D]]
                w1src = p.symap3("sync", dram["w1_bf"], AFF((ei, D * D)), wpat)
                p.ins("sync", lambda e, w1src=w1src: e.dma_start(
                    out=w1s[:, :].rearrange("p (k n) -> p k n", k=NCH),
                    in_=w1src), incs=(("dwA", 16),))
                w2src = p.symap3("sync", dram["w2_bf"], AFF((ei, D * D)), wpat)
                p.ins("sync", lambda e, w2src=w2src: e.dma_start(
                    out=w2s[:, :].rearrange("p (k n) -> p k n", k=NCH),
                    in_=w2src), incs=(("dwB", 16),))
                p.wait("gpsimd", "dv", p.cnt["dv"], loop=True)
                dxb = p.cnt["dx"]
                for i3, dst in ((0, b1b), (1, g1b), (2, beb)):
                    bsrc = p.symap3("gpsimd", dram["bge"],
                                    AFF((ei, 3 * D), const=i3 * D),
                                    [[0, 128], [1, D]])
                    p.ins("gpsimd", lambda e, bsrc=bsrc, dst=dst: e.dma_start(
                        out=dst, in_=bsrc), incs=(("dx", 16),))
                p.wait("vector", "dx", dxb + 48, loop=True)
                p.wait("tensor", "dwA", p.cnt["dwA"], loop=True)
                p.wait("tensor", "dwB", p.cnt["dwB"], loop=True)

                def emit_w1(g):
                    tg0 = g * 512
                    for tl in range(4):
                        ttile = g * 4 + tl
                        tt0 = tg0 + tl * 128
                        hb = [ps[(tl % 2) * 2], ps[(tl % 2) * 2 + 1]]
                        if ttile >= 2:
                            p.wait("tensor", "dv", evd[ttile - 2], loop=True)
                        elif (ttile + 14) in evd:
                            p.wait_prev("tensor", "dv", evd[ttile + 14])
                        for hh in range(2):
                            for kc in range(NCH):
                                p.ins("tensor", lambda e, hh=hh, kc=kc, tt0=tt0:
                                      e.matmul(
                                          hb[hh][:, :],
                                          xn[:, kc * HALF + tt0: kc * HALF + tt0 + 128],
                                          w1s[:, kc * D + hh * 512: kc * D + (hh + 1) * 512],
                                          start=(kc == 0), stop=(kc == NCH - 1)),
                                      incs=(("pe", 1),) if (kc == NCH - 1 and hh == 1)
                                      else ())
                        p.wait("vector", "pe", p.cnt["pe"], loop=True)
                        p.selfsync("vector")
                        if ttile >= 8:
                            p.wait("vector", "ac", sqd[ttile - 8], loop=True)
                        elif ttile == 0:
                            p.wait("vector", "ac", ac_e0, loop=True)
                        hslot = ((g % 2) * 4 + tl) * D
                        for hh in range(2):
                            p.ins("vector", lambda e, hh=hh, hslot=hslot, tl=tl:
                                  e.scalar_tensor_tensor(
                                      h_sb[:, hslot + hh * 512: hslot + (hh + 1) * 512],
                                      hb[hh][:, :], 1.0,
                                      b1b[:, hh * 512:(hh + 1) * 512],
                                      op0=ALU.mult, op1=ALU.add,
                                      accum_out=st_tok[:, hh * 4 + tl: hh * 4 + tl + 1]),
                                  incs=(("dv", 1),))
                        evd[ttile] = p.cnt["dv"]
                        p.wait("scalar", "dv", p.cnt["dv"], loop=True)
                        p.selfsync("scalar")
                        p.ins("scalar", lambda e, hslot=hslot, tl=tl: e.activation(
                            trash[:, 0:D], h_sb[:, hslot:hslot + D], AF.Square,
                            accum_out=st_tok[:, 8 + tl:9 + tl]), incs=(("ac", 1),))
                        sqd[ttile] = p.cnt["ac"]

                def emit_elem(g):
                    p.wait("vector", "ac", sqd[g * 4 + 3], loop=True)
                    p.selfsync("vector")
                    p.ins("vector", lambda e: e.tensor_tensor(
                        st_tok[:, 12:16], st_tok[:, 0:4], st_tok[:, 4:8],
                        ALU.add), incs=(("dv", 1),))
                    p.selfsync("vector")
                    p.ins("vector", lambda e: e.tensor_scalar_mul(
                        st_tok[:, 12:16], st_tok[:, 12:16], 1.0 / D),
                        incs=(("dv", 1),))
                    p.ins("vector", lambda e: e.tensor_scalar_mul(
                        st_tok[:, 16:20], st_tok[:, 8:12], 1.0 / D),
                        incs=(("dv", 1),))
                    p.selfsync("vector")
                    p.ins("vector", lambda e: e.tensor_tensor(
                        st_tok[:, 20:24], st_tok[:, 12:16], st_tok[:, 12:16],
                        ALU.mult), incs=(("dv", 1),))
                    p.selfsync("vector")
                    p.ins("vector", lambda e: e.tensor_tensor(
                        st_tok[:, 20:24], st_tok[:, 16:20], st_tok[:, 20:24],
                        ALU.subtract), incs=(("dv", 1),))
                    p.wait("scalar", "dv", p.cnt["dv"], loop=True)
                    p.selfsync("scalar")
                    p.ins("scalar", lambda e: e.activation(
                        st_tok[:, 20:24], st_tok[:, 20:24], AF.Ln,
                        bias=consts["eps"][0:128, 0:1]), incs=(("ac", 1),))
                    p.selfsync("scalar")
                    p.ins("scalar", lambda e: e.activation(
                        st_tok[:, 24:28], st_tok[:, 20:24], AF.Exp,
                        scale=consts["neghalf"][0:128, 0:1]), incs=(("ac", 1),))
                    p.wait("vector", "ac", p.cnt["ac"], loop=True)
                    p.selfsync("vector")
                    p.ins("vector", lambda e: e.tensor_tensor(
                        st_tok[:, 28:32], st_tok[:, 12:16], st_tok[:, 24:28],
                        ALU.mult), incs=(("dv", 1),))
                    p.selfsync("vector")
                    p.ins("vector", lambda e: e.tensor_scalar_mul(
                        st_tok[:, 28:32], st_tok[:, 28:32], -1.0),
                        incs=(("dv", 1),))
                    hq = hn_q if g % 2 == 0 else hn_q2
                    p.selfsync("vector")
                    if g >= 2:
                        p.wait("vector", "pe", trd[(g - 2) * 4 + 3], loop=True)
                    for tl in range(4):
                        hs8 = ((g % 2) * 4 + tl) * D
                        p.ins("vector", lambda e, tl=tl, hs8=hs8, hq=hq:
                              e.tensor_scalar(
                                  hq[:, tl * D:(tl + 1) * D], h_sb[:, hs8:hs8 + D],
                                  st_tok[:, 24 + tl:25 + tl],
                                  st_tok[:, 28 + tl:29 + tl],
                                  op0=ALU.mult, op1=ALU.add), incs=(("dv", 1),))
                    p.selfsync("vector")
                    for tl in range(4):
                        p.ins("vector", lambda e, tl=tl, hq=hq: e.tensor_tensor(
                            hq[:, tl * D:(tl + 1) * D], hq[:, tl * D:(tl + 1) * D],
                            g1b, ALU.mult), incs=(("dv", 1),))
                    p.selfsync("vector")
                    for tl in range(4):
                        p.ins("vector", lambda e, tl=tl, hq=hq: e.tensor_tensor(
                            hq[:, tl * D:(tl + 1) * D], hq[:, tl * D:(tl + 1) * D],
                            beb, ALU.add), incs=(("dv", 1),))
                    p.wait("scalar", "dv", p.cnt["dv"], loop=True)
                    p.selfsync("scalar")
                    p.ins("scalar", lambda e, hq=hq: e.activation(
                        hq[:, 0:4 * D], hq[:, 0:4 * D], AF.Gelu),
                        incs=(("ac", 1),))
                    p.wait("vector", "ac", p.cnt["ac"], loop=True)
                    p.selfsync("vector")
                    for tl in range(4):
                        ttile = g * 4 + tl
                        gate_ap = p.symap("vector", gates_all,
                                          AFF((ei, 1), const=ttile * 16), 1)
                        p.ins("vector", lambda e, tl=tl, gate_ap=gate_ap, hq=hq:
                              e.tensor_scalar(
                                  hq[:, tl * D:(tl + 1) * D],
                                  hq[:, tl * D:(tl + 1) * D],
                                  gate_ap, None, op0=ALU.mult),
                              incs=(("dv", 1),))
                    gtd[g] = p.cnt["dv"]

                def emit_trw2(g):
                    tg0 = g * 512
                    p.wait("tensor", "dv", gtd[g], loop=True)
                    for tl in range(4):
                        ttile = g * 4 + tl
                        psT = ps[6 + tl % 2][:, :].bitcast(BF16)
                        if tl >= 2:
                            p.wait("tensor", "dv", cpd[ttile - 2], loop=True)
                        elif g >= 1:
                            p.wait("tensor", "dv", cpd[(g - 1) * 4 + 2 + tl],
                                   loop=True)
                        hq = hn_q if g % 2 == 0 else hn_q2
                        for kcj in range(NCH):
                            p.ins("tensor", lambda e, kcj=kcj, tl=tl, psT=psT,
                                  hq=hq: e.transpose(
                                      psT[:, kcj * 128:(kcj + 1) * 128],
                                      hq[:, tl * D + kcj * 128: tl * D + (kcj + 1) * 128],
                                      ident), incs=(("pe", 1),) if kcj == NCH - 1
                                  else ())
                        trd[ttile] = p.cnt["pe"]
                        p.wait("vector", "pe", trd[ttile], loop=True)
                        p.selfsync("vector")
                        for kcj in range(NCH):
                            p.ins("vector", lambda e, kcj=kcj, tl=tl, psT=psT:
                                  e.tensor_copy(
                                      hT[:, kcj * 512 + tl * 128: kcj * 512 + (tl + 1) * 128],
                                      psT[:, kcj * 128:(kcj + 1) * 128]),
                                  incs=(("dv", 1),) if kcj == NCH - 1 else ())
                        cpd[ttile] = p.cnt["dv"]
                    p.wait("tensor", "dv", cpd[g * 4 + 3], loop=True)
                    for mo in range(NCH):
                        bank = ps[4 + mo % 2]
                        if mo >= 2:
                            p.wait("tensor", "dv", accd[(g, mo - 2)], loop=True)
                        elif g >= 1:
                            p.wait("tensor", "dv", accd[(g - 1, 6 + mo)], loop=True)
                        for kcj in range(NCH):
                            p.ins("tensor", lambda e, mo=mo, kcj=kcj, bank=bank:
                                  e.matmul(
                                      bank[:, :],
                                      w2s[:, mo * D + kcj * 128: mo * D + kcj * 128 + 128],
                                      hT[:, kcj * 512:(kcj + 1) * 512],
                                      start=(kcj == 0), stop=(kcj == NCH - 1)),
                                  incs=(("pe", 1),) if kcj == NCH - 1 else ())
                        p.wait("vector", "pe", p.cnt["pe"], loop=True)
                        p.selfsync("vector")
                        p.ins("vector", lambda e, mo=mo, tg0=tg0, bank=bank:
                              e.tensor_tensor(
                                  acc[:, mo * HALF + tg0: mo * HALF + tg0 + 512],
                                  acc[:, mo * HALF + tg0: mo * HALF + tg0 + 512],
                                  bank[:, :], ALU.add), incs=(("dv", 1),))
                        accd[(g, mo)] = p.cnt["dv"]

                for g in range(4):
                    emit_w1(g)
                    if g >= 2:
                        emit_trw2(g - 2)
                    emit_elem(g)
                emit_trw2(2)
                emit_trw2(3)
            p.emit_loop(NEXP, e_body, peel=0)

            p.wait("sync", "dwA", p.cnt["dwA"])
            p.wait("sync", "dwB", p.cnt["dwB"])
            p.wait("gpsimd", "dx", p.cnt["dx"])
            for tq in range(4):
                tb0 = t0 + tq * 512
                dxa = p.cnt["dx"]
                p.wait("gpsimd", "dv", p.cnt["dv"])
                for c in range(NCH):
                    p.ins("gpsimd", lambda e, c=c, tb0=tb0: e.dma_start(
                        out=xblk[:, c * 512:(c + 1) * 512],
                        in_=xsc[c * 128:(c + 1) * 128, tb0:tb0 + 512]),
                        incs=(("dx", 16),))
                p.wait("tensor", "dx", dxa + 16 * NCH)
                p.wait("vector", "dx", dxa + 16 * NCH)
                for o in range(NCH):
                    bank = ps[6 + o % 2]
                    p.wait("tensor", "dv", p.cnt["dv"])
                    p.ins("tensor", lambda e, o=o, tq=tq, bank=bank: e.matmul(
                        bank[:, :], b2all[0:16, o * 128:(o + 1) * 128],
                        gfm[0:16, tq * 512:(tq + 1) * 512],
                        start=True, stop=True), incs=(("pe", 1),))
                    p.wait("vector", "pe", p.cnt["pe"])
                    p.selfsync("vector")
                    p.ins("vector", lambda e, o=o, bank=bank: e.scalar_tensor_tensor(
                        xblk[:, o * 512:(o + 1) * 512], bank[:, :], 0.5,
                        xblk[:, o * 512:(o + 1) * 512],
                        op0=ALU.mult, op1=ALU.add), incs=(("dv", 1),))
                    p.selfsync("vector")
                    p.ins("vector", lambda e, o=o, tq=tq: e.scalar_tensor_tensor(
                        xblk[:, o * 512:(o + 1) * 512],
                        acc[:, o * HALF + tq * 512: o * HALF + (tq + 1) * 512],
                        0.5, xblk[:, o * 512:(o + 1) * 512],
                        op0=ALU.mult, op1=ALU.add), incs=(("dv", 1),))
                p.wait("gpsimd", "dv", p.cnt["dv"])
                for c in range(NCH):
                    p.ins("gpsimd", lambda e, c=c, tb0=tb0: e.dma_start(
                        out=xsc[c * 128:(c + 1) * 128, tb0:tb0 + 512],
                        in_=xblk[:, c * 512:(c + 1) * 512]),
                        incs=(("dx", 16),))
                p.wait("gpsimd", "dx", p.cnt["dx"])
            p.barrier()
    return body


def phase_l1pre(p, ar_mk, consts, xsc, zsc, ps):
    def body():
        ar = ar_mk()
        xblk = ar.alloc(NCH * 512, F32)
        zblk = ar.alloc(NCH * 512, F32)
        ln = LNBlock(p, ar, 512)
        p.ins("vector", lambda e: e.memset(ln.bcin[0:2, :], 0.0), incs=(("dv", 1),))
        for blk in range(TOKC // 512):
            t0 = blk * 512
            dxa = p.cnt["dx"]
            for c in range(NCH):
                p.ins("gpsimd", lambda e, c=c, t0=t0: e.dma_start(
                    out=xblk[:, c * 512:(c + 1) * 512],
                    in_=xsc[c * 128:(c + 1) * 128, t0:t0 + 512]),
                    incs=(("dx", 16),))
            for e in ("tensor", "vector", "scalar"):
                p.wait(e, "dx", dxa + 16 * NCH)

            def xblk_c(c):
                return xblk[:, c * 512:(c + 1) * 512]

            def z_dst(c, j):
                return zblk[:, c * 512:(c + 1) * 512]
            emit_ln(p, ln, consts, xblk_c, z_dst, ps, consts["zero"])
            p.wait("gpsimd", "dv", p.cnt["dv"])
            for c in range(NCH):
                p.ins("gpsimd", lambda e, c=c, t0=t0: e.dma_start(
                    out=zsc[c * 128:(c + 1) * 128, t0:t0 + 512],
                    in_=zblk[:, c * 512:(c + 1) * 512]),
                    incs=(("dx", 16),))
            p.wait("gpsimd", "dx", p.cnt["dx"])
        p.barrier()
    return body


def phase_l1post(p, ar_mk, consts, xsc, zsc):
    def body():
        ar = ar_mk()
        xblk = ar.alloc(NCH * 512, F32)
        zblk = ar.alloc(NCH * 512, F32)
        for blk in range(TOKC // 512):
            t0 = blk * 512
            dxa = p.cnt["dx"]
            for c in range(NCH):
                p.ins("gpsimd", lambda e, c=c, t0=t0: e.dma_start(
                    out=xblk[:, c * 512:(c + 1) * 512],
                    in_=xsc[c * 128:(c + 1) * 128, t0:t0 + 512]),
                    incs=(("dx", 16),))
                p.ins("gpsimd", lambda e, c=c, t0=t0: e.dma_start(
                    out=zblk[:, c * 512:(c + 1) * 512],
                    in_=zsc[c * 128:(c + 1) * 128, t0:t0 + 512]),
                    incs=(("dx", 16),))
            p.wait("vector", "dx", p.cnt["dx"])
            p.selfsync("vector")
            for c in range(NCH):
                p.ins("vector", lambda e, c=c: e.tensor_tensor(
                    xblk[:, c * 512:(c + 1) * 512],
                    xblk[:, c * 512:(c + 1) * 512],
                    zblk[:, c * 512:(c + 1) * 512], ALU.add), incs=(("dv", 1),))
            p.wait("gpsimd", "dv", p.cnt["dv"])
            for c in range(NCH):
                p.ins("gpsimd", lambda e, c=c, t0=t0: e.dma_start(
                    out=xsc[c * 128:(c + 1) * 128, t0:t0 + 512],
                    in_=xblk[:, c * 512:(c + 1) * 512]),
                    incs=(("dx", 16),))
            p.wait("gpsimd", "dx", p.cnt["dx"])
        p.barrier()
    return body


def phase_l1ffn(p, ar_mk, consts, zsc, w1cb, w2cb, b1d, lngd, lnbd, ps):
    MT = 1024
    NMT = TOKC // MT
    nh = MT // 512
    MH = 32

    def body():
        ar = ar_mk()
        xblk = ar.alloc(NCH * MT, F32)
        zn = ar.alloc(NCH * MT, BF16)
        ln = LNBlock(p, ar, MT)
        lntmp = ar.alloc(512, F32)
        wgm = ar.alloc(2 * D, BF16)
        h_t = ar.alloc(MH * MT, BF16)
        b1t = ar.alloc(MH, F32)
        lng_t = ar.alloc(NCH, F32)
        lnb_t = ar.alloc(NCH, F32)

        dw0 = p.cnt["dw"]
        p.wait("sync", "dw", dw0)
        p.ins("sync", lambda e: e.dma_start(out=b1t, in_=b1d), incs=(("dw", 16),))
        p.ins("sync", lambda e: e.dma_start(out=lng_t, in_=lngd), incs=(("dw", 16),))
        p.ins("sync", lambda e: e.dma_start(out=lnb_t, in_=lnbd), incs=(("dw", 16),))
        for e in ("vector", "scalar", "tensor"):
            p.wait(e, "dw", p.cnt["dw"])
        p.ins("vector", lambda e: e.memset(ln.bcin[0:2, :], 0.0), incs=(("dv", 1),))

        def xblk_c(c):
            return xblk[:, c * MT:(c + 1) * MT]

        for mt in range(NMT):
            t0 = mt * MT
            dx0 = p.cnt["dx"]
            for c in range(NCH):
                p.ins("gpsimd", lambda e, c=c, t0=t0: e.dma_start(
                    out=xblk_c(c), in_=zsc[c * 128:(c + 1) * 128, t0:t0 + MT]),
                    incs=(("dx", 16),))
            for e in ("tensor", "vector", "scalar"):
                p.wait(e, "dx", dx0 + 16 * NCH)

            def xn_dst(c, j):
                return lntmp[:, 0:512]

            def post(c, j):
                p.ins("scalar", lambda e, c=c, j=j: e.activation(
                    zn[:, c * MT + j * 512: c * MT + (j + 1) * 512],
                    lntmp[:, 0:512], AF.Identity,
                    scale=lng_t[:, c:c + 1], bias=lnb_t[:, c:c + 1]),
                    incs=(("ac", 1),))
            emit_ln(p, ln, consts, xblk_c, xn_dst, ps, consts["zero"], post=post)

            dv_gu0 = p.cnt["dv"]
            ac_gu0 = p.cnt["ac"]
            for m in range(MH):
                slot = "dwA" if m % 2 == 0 else "dwB"
                if m >= 2:
                    p.wait("sync", "ac", ac_gu0 + (m - 1) * nh)
                p.ins("sync", lambda e, m=m: e.dma_start(
                    out=wgm[:, (m % 2) * D:(m % 2) * D + D],
                    in_=w1cb[m * 128:(m + 1) * 128, :]), incs=((slot, 16),))
                bsel = (m % 2) * 4
                gb = [ps[bsel + 0], ps[bsel + 1]]
                p.wait("tensor", slot, p.cnt[slot])
                if m >= 2:
                    p.wait("tensor", "ac", ac_gu0 + (m - 1) * nh)
                for k in range(NCH):
                    for j in range(nh):
                        p.ins("tensor", lambda e, k=k, m=m, j=j: e.matmul(
                            gb[j][:, :],
                            wgm[:, (m % 2) * D + k * 128:(m % 2) * D + k * 128 + 128],
                            zn[:, k * MT + j * 512: k * MT + (j + 1) * 512],
                            start=(k == 0), stop=(k == NCH - 1)),
                            incs=(("pe", 1),) if (k == NCH - 1 and j == nh - 1) else ())
                p.wait("scalar", "pe", p.cnt["pe"])
                for j in range(nh):
                    p.selfsync("scalar")
                    p.ins("scalar", lambda e, m=m, j=j: e.activation(
                        h_t[:, m * MT + j * 512: m * MT + (j + 1) * 512],
                        gb[j][:, :], AF.Gelu_apprx_sigmoid,
                        bias=b1t[:, m:m + 1]),
                        incs=(("ac", 1),))
            p.wait("tensor", "ac", p.cnt["ac"])

            wdo = zn
            dv_do0 = p.cnt["dv"]
            for o in range(NCH):
                slot = "dwA" if o % 2 == 0 else "dwB"
                if o >= 2:
                    p.wait("sync", "dv", dv_do0 + (o - 1) * nh)
                else:
                    p.wait("sync", "ac", p.cnt["ac"])
                p.ins("sync", lambda e, o=o: e.dma_start(
                    out=wdo[:, (o % 2) * EXP4:((o % 2) + 1) * EXP4],
                    in_=w2cb[o * 128:(o + 1) * 128, :]), incs=((slot, 16),))
                bsel = (o % 2) * 4
                db = [ps[bsel + 0], ps[bsel + 1]]
                p.wait("tensor", slot, p.cnt[slot])
                if o >= 2:
                    p.wait("tensor", "dv", dv_do0 + (o - 1) * nh)
                for k in range(MH):
                    for j in range(nh):
                        p.ins("tensor", lambda e, k=k, o=o, j=j: e.matmul(
                            db[j][:, :],
                            wdo[:, (o % 2) * EXP4 + k * 128:(o % 2) * EXP4 + k * 128 + 128],
                            h_t[:, k * MT + j * 512: k * MT + (j + 1) * 512],
                            start=(k == 0), stop=(k == MH - 1)),
                            incs=(("pe", 1),) if (k == MH - 1 and j == nh - 1) else ())
                p.wait("vector", "pe", p.cnt["pe"])
                for j in range(nh):
                    p.selfsync("vector")
                    p.ins("vector", lambda e, o=o, j=j: e.tensor_tensor(
                        xblk_c(o)[:, j * 512:(j + 1) * 512],
                        db[j][:, :],
                        xblk_c(o)[:, j * 512:(j + 1) * 512],
                        ALU.add), incs=(("dv", 1),))
            p.wait("gpsimd", "dv", p.cnt["dv"])
            for c in range(NCH):
                p.ins("gpsimd", lambda e, c=c, t0=t0: e.dma_start(
                    out=zsc[c * 128:(c + 1) * 128, t0:t0 + MT], in_=xblk_c(c)),
                    incs=(("dx", 16),))
            p.wait("gpsimd", "dx", p.cnt["dx"])
            p.barrier()
    return body


def build_program(phase_names, dev_nb=NB):
    from contextlib import ExitStack
    es = ExitStack()
    nc = bacc_mod.Bacc("TRN2", target_bir_lowering=False, debug=False)

    toks = dev_nb * SEQ
    global TOKC
    assert dev_nb == NB, "dev_nb must equal NB for now"

    xT = nc.dram_tensor("xT", [D, TOKC], F32, kind="ExternalInput").ap()
    yT = nc.dram_tensor("yT", [D, TOKC], F32, kind="ExternalOutput").ap()
    xsc = nc.dram_tensor("xsc", [D, TOKC], F32, kind="Internal").ap()

    need = set(phase_names)
    dram = {}
    if "l0attn" in need:
        for nm in ("wq_a", "wk_a", "wv_a", "wo_a"):
            dram[nm] = nc.dram_tensor(nm, [D, D], F32, kind="ExternalInput").ap()
            dram[nm + "_bf"] = nc.dram_tensor(nm + "_bf", [D, D], BF16,
                                              kind="Internal").ap()
        for nm in ("bq_a", "bk_a", "bv_a", "bo_a"):
            dram[nm] = nc.dram_tensor(nm, [128, NCH], F32, kind="ExternalInput").ap()
    if "l1attn" in need:
        for i in range(2):
            for nm in ("wq", "wk", "wv", "wo"):
                dram[f"{nm}_l{i}"] = nc.dram_tensor(f"{nm}_l{i}", [D, D], F32,
                                                    kind="ExternalInput").ap()
                dram[f"{nm}_l{i}_bf"] = nc.dram_tensor(f"{nm}_l{i}_bf", [D, D],
                                                       BF16, kind="Internal").ap()
            dram[f"w1_l{i}"] = nc.dram_tensor(f"w1_l{i}", [EXP4, D], F32,
                                              kind="ExternalInput").ap()
            dram[f"w1_l{i}_bf"] = nc.dram_tensor(f"w1_l{i}_bf", [EXP4, D], BF16,
                                                 kind="Internal").ap()
            dram[f"w2_l{i}"] = nc.dram_tensor(f"w2_l{i}", [D, EXP4], F32,
                                              kind="ExternalInput").ap()
            dram[f"w2_l{i}_bf"] = nc.dram_tensor(f"w2_l{i}_bf", [D, EXP4], BF16,
                                                 kind="Internal").ap()
            for nm in ("bq", "bk", "bv", "lng1", "lnb1", "lng2", "lnb2"):
                dram[f"{nm}_l{i}"] = nc.dram_tensor(f"{nm}_l{i}", [128, NCH], F32,
                                                    kind="ExternalInput").ap()
            dram[f"b1_l{i}"] = nc.dram_tensor(f"b1_l{i}", [128, 32], F32,
                                              kind="ExternalInput").ap()
        dram["qtab"] = nc.dram_tensor("qtab", [128, SEQ], BF16,
                                      kind="ExternalInput").ap()
        dram["ktab"] = nc.dram_tensor("ktab", [128, SEQ], BF16,
                                      kind="ExternalInput").ap()
        dram["zsc"] = nc.dram_tensor("zsc", [D, TOKC], F32, kind="Internal").ap()
    if "moe" in need:
        dram["wr"] = nc.dram_tensor("wr", [D, NEXP], F32, kind="ExternalInput").ap()
        dram["brv"] = nc.dram_tensor("brv", [NEXP, 1], F32, kind="ExternalInput").ap()
        dram["w1n"] = nc.dram_tensor("w1n", [NEXP * D, D], F32, kind="ExternalInput").ap()
        dram["w2n"] = nc.dram_tensor("w2n", [NEXP * D, D], F32, kind="ExternalInput").ap()
        dram["w1_bf"] = nc.dram_tensor("w1_bf", [NEXP * D, D], BF16, kind="Internal").ap()
        dram["w2_bf"] = nc.dram_tensor("w2_bf", [NEXP * D, D], BF16, kind="Internal").ap()
        dram["bge"] = nc.dram_tensor("bge", [NEXP, 3 * D], F32, kind="ExternalInput").ap()
        dram["b2n"] = nc.dram_tensor("b2n", [NEXP, D], F32, kind="ExternalInput").ap()
    if "l0ffn" in need:
        dram["wgT"] = nc.dram_tensor("wgT", [HID0, D], I32, kind="ExternalInput").ap()
        dram["wuT"] = nc.dram_tensor("wuT", [HID0, D], I32, kind="ExternalInput").ap()
        dram["wdT"] = nc.dram_tensor("wdT", [D, HID0], I32, kind="ExternalInput").ap()
        dram["wgT_bf"] = nc.dram_tensor("wgT_bf", [HID0, D], BF16, kind="Internal").ap()
        dram["wuT_bf"] = nc.dram_tensor("wuT_bf", [HID0, D], BF16, kind="Internal").ap()
        dram["wdT_bf"] = nc.dram_tensor("wdT_bf", [D, HID0], BF16, kind="Internal").ap()

    ones1_d = nc.inline_tensor(np.ones((128, 1), np.float32), name="ones1_c").ap()
    ones2_np = np.zeros((2, 128), np.float32)
    ones2_np[0, :] = 1.0
    ones2_d = nc.inline_tensor(ones2_np, name="ones2_c").ap()
    ident_d = nc.inline_tensor(np.eye(128, dtype=BFNP), name="ident_c").ap()
    identf_d = nc.inline_tensor(np.eye(128, dtype=np.float32), name="identf_c").ap()
    lq_np = np.zeros((4, 256), np.float32)
    for qt in range(2):
        for pp in range(128):
            r = qt if pp < 64 else 2 + qt
            lq_np[r, qt * 128 + pp] = 1.0
    lq_d = nc.inline_tensor(lq_np, name="lq_c").ap()

    p = Prog(nc, es)
    const_ar = Arena(nc, es, "consts", 4096)
    consts = {}
    consts["ones1"] = const_ar.alloc(1, F32)
    consts["ones2"] = None
    ones2_t = const_ar.alloc(128, F32)
    consts["ident"] = const_ar.alloc(256, BF16)[0:128, 0:128]
    consts["eps"] = const_ar.alloc(1, F32)
    consts["neghalf"] = const_ar.alloc(1, F32)
    consts["ln127"] = const_ar.alloc(1, F32)
    consts["zero"] = const_ar.alloc(1, F32)
    consts["eps12"] = const_ar.alloc(1, F32)
    consts["poshalf"] = const_ar.alloc(1, F32)
    consts["negone"] = const_ar.alloc(1, F32)
    consts["onesz"] = const_ar.alloc(64, BF16)
    consts["oneb0"] = const_ar.alloc(64, BF16)
    consts["oneb64"] = const_ar.alloc(64, BF16)
    consts["qsc"] = const_ar.alloc(1, F32)
    identf_t = const_ar.alloc(128, F32)
    consts["identf"] = identf_t[0:128, 0:128]
    lq_t = const_ar.alloc(256, F32)
    consts["lq"] = lq_t
    consts["ones2"] = ones2_t[0:2, 0:128]

    WS_BYTES = 208640
    ws = Arena(nc, es, "ws", WS_BYTES)

    ps = [es.enter_context(nc.psum_tensor(f"ps{i}", [128, 512], F32))[:]
          for i in range(8)]

    def phase_init():
        p.ins("sync", lambda e: e.dma_start(out=consts["ones1"], in_=ones1_d),
              incs=(("dw", 16),))
        p.ins("sync", lambda e: e.dma_start(out=ones2_t[0:2, :], in_=ones2_d),
              incs=(("dw", 16),))
        p.ins("sync", lambda e: e.dma_start(out=consts["ident"], in_=ident_d),
              incs=(("dw", 16),))
        p.ins("sync", lambda e: e.dma_start(out=consts["identf"], in_=identf_d),
              incs=(("dw", 16),))
        p.ins("sync", lambda e: e.dma_start(out=lq_t[0:4, :], in_=lq_d),
              incs=(("dw", 16),))
        p.ins("vector", lambda e: e.memset(consts["eps"], LN_EPS))
        p.ins("vector", lambda e: e.memset(consts["neghalf"], -0.5))
        p.ins("vector", lambda e: e.memset(consts["ln127"], float(np.log(1.0 / 127.0))))
        p.ins("vector", lambda e: e.memset(consts["zero"], 0.0))
        p.ins("vector", lambda e: e.memset(consts["eps12"], 1e-12))
        p.ins("vector", lambda e: e.memset(consts["negone"], -1.0))
        p.ins("vector", lambda e: e.memset(consts["onesz"], 0.0))
        p.ins("vector", lambda e: e.memset(consts["oneb0"], 0.0))
        p.ins("vector", lambda e: e.memset(consts["oneb64"], 0.0),
              incs=(("dv", 1),))
        p.selfsync("vector")
        p.ins("vector", lambda e: e.memset(consts["onesz"][:, 0:1], 1.0))
        p.ins("vector", lambda e: e.memset(consts["oneb0"][0:1, :], 1.0))
        p.ins("vector", lambda e: e.memset(consts["oneb64"][64:65, :], 1.0))
        p.ins("vector", lambda e: e.memset(consts["poshalf"], 0.5), incs=(("dv", 1),))
        p.wait("sync", "dw", p.cnt["dw"])
        p.barrier()
    p.phases.append(phase_init)

    p.phases.append(phase_xcopy(p, xsc, xT))
    if "l0attn" in need:
        p.phases.append(phase_wcast(p, [
            (dram[nm + "_bf"], dram[nm]) for nm in ("wq_a", "wk_a", "wv_a", "wo_a")]))
    if "moe" in need:
        p.phases.append(phase_wcast(p, [
            (dram["w1_bf"], dram["w1n"]), (dram["w2_bf"], dram["w2n"])]))
    if "l1attn" in need:
        pairs = []
        for i in range(2):
            for nm in ("wq", "wk", "wv", "wo", "w1", "w2"):
                pairs.append((dram[f"{nm}_l{i}_bf"], dram[f"{nm}_l{i}"]))
        p.phases.append(phase_wcast(p, pairs))
    if "l0ffn" in need:
        p.phases.append(phase_wcast(p, [
            (dram["wgT_bf"], dram["wgT"]),
            (dram["wuT_bf"], dram["wuT"]),
            (dram["wdT_bf"], dram["wdT"]),
        ]))

    def ar_mk():
        ws.reset()
        return ws

    for name in phase_names:
        if name == "l0ffn":
            p.phases.append(phase_l0ffn(
                p, ar_mk, consts, xsc,
                dram["wgT_bf"], dram["wuT_bf"], dram["wdT_bf"], ps))
        elif name == "moe":
            p.phases.append(phase_moe(p, ar_mk, consts, xsc, dram, ps))
        elif name == "l1attn":
            zsc = dram["zsc"]
            p.phases.append(phase_l1pre(p, ar_mk, consts, xsc, zsc, ps))
            for i in range(2):
                cfg = {
                    "stream": zsc, "sel": False,
                    "wq": dram[f"wq_l{i}_bf"], "wk": dram[f"wk_l{i}_bf"],
                    "wv": dram[f"wv_l{i}_bf"], "wo": dram[f"wo_l{i}_bf"],
                    "bq": dram[f"bq_l{i}"], "bk": dram[f"bk_l{i}"],
                    "bv": dram[f"bv_l{i}"],
                    "lng": dram[f"lng1_l{i}"], "lnb": dram[f"lnb1_l{i}"],
                    "qtab": dram["qtab"], "ktab": dram["ktab"],
                }
                p.phases.append(phase_attention(p, ar_mk, consts, cfg, ps))
                p.phases.append(phase_l1ffn(
                    p, ar_mk, consts, zsc, dram[f"w1_l{i}_bf"],
                    dram[f"w2_l{i}_bf"], dram[f"b1_l{i}"],
                    dram[f"lng2_l{i}"], dram[f"lnb2_l{i}"], ps))
            p.phases.append(phase_l1post(p, ar_mk, consts, xsc, zsc))
        elif name == "l0attn":
            cfg = {
                "stream": xsc, "sel": True, "qscale": 1.0 / 32.0,
                "wq": dram["wq_a_bf"], "wk": dram["wk_a_bf"],
                "wv": dram["wv_a_bf"], "wo": dram["wo_a_bf"],
                "bq": dram["bq_a"], "bk": dram["bk_a"], "bv": dram["bv_a"],
                "bo": dram["bo_a"],
            }
            p.phases.append(phase_attention(p, ar_mk, consts, cfg, ps))
        else:
            raise ValueError(name)

    p.phases.append(phase_xcopy(p, yT, xsc, sem="ds", queue="sync"))

    p.run()
    es.close()
    nc.finalize()
    return nc


def _colblocks(WT):
    K, M = WT.shape
    return np.ascontiguousarray(
        WT.reshape(K // 128, 128, M // 128, 128)
          .transpose(2, 1, 0, 3).reshape(M, K))


def _prep_l0ffn(inputs):
    return {
        "wgT": _colblocks(np.ascontiguousarray(inputs["L0_ffn_w_gate"].T)),
        "wuT": _colblocks(np.ascontiguousarray(inputs["L0_ffn_w_up"].T)),
        "wdT": _colblocks(np.ascontiguousarray(inputs["L0_ffn_w_down"].T)),
    }


def _prep_moe(inputs):
    b1 = np.asarray(inputs["L1_ffn_b1"])
    g1 = np.asarray(inputs["L1_ffn_g1"])
    be1 = np.asarray(inputs["L1_ffn_be1"])
    W2 = np.asarray(inputs["L1_ffn_W2"])
    w2cb = np.concatenate([_colblocks(np.ascontiguousarray(W2[e]))
                           for e in range(NEXP)], axis=0)
    return {
        "wr": np.ascontiguousarray(inputs["L1_ffn_Wr"]),
        "brv": np.ascontiguousarray(np.asarray(inputs["L1_ffn_br"]).reshape(NEXP, 1)),
        "w1n": np.ascontiguousarray(np.asarray(inputs["L1_ffn_W1"]).reshape(NEXP * D, D)),
        "w2n": np.ascontiguousarray(w2cb),
        "bge": np.ascontiguousarray(np.concatenate([b1, g1, be1], axis=1)),
        "b2n": np.ascontiguousarray(np.asarray(inputs["L1_ffn_b2"])),
    }


def _prep_l1attn(inputs):
    out = {}
    for i in range(2):
        for nm, key in (("wq", "L1_attn_wq"), ("wk", "L1_attn_wk"),
                        ("wv", "L1_attn_wv"), ("wo", "L1_attn_wo")):
            out[f"{nm}_l{i}"] = _colblocks(np.ascontiguousarray(
                np.asarray(inputs[key])[i]))
        out[f"w1_l{i}"] = _colblocks(np.ascontiguousarray(
            np.asarray(inputs["L1_attn_w1"])[i]))
        out[f"w2_l{i}"] = _colblocks(np.ascontiguousarray(
            np.asarray(inputs["L1_attn_w2"])[i]))
        for nm, key in (("bq", "L1_attn_bq"), ("bk", "L1_attn_bk"),
                        ("bv", "L1_attn_bv"), ("lng1", "L1_attn_ln1_g"),
                        ("lnb1", "L1_attn_ln1_b"), ("lng2", "L1_attn_ln2_g"),
                        ("lnb2", "L1_attn_ln2_b")):
            out[f"{nm}_l{i}"] = _chunked_vec(np.asarray(inputs[key])[i])
        out[f"b1_l{i}"] = _chunked_vec(np.asarray(inputs["L1_attn_b1"])[i])
    tab = _rotary_table()
    qt = np.empty((128, SEQ), np.float32)
    kt = np.empty((128, SEQ), np.float32)
    qt[0:32] = tab / 8.0
    qt[32:64] = 1.0 / 8.0
    qt[64:96] = tab / 8.0
    qt[96:128] = 1.0 / 8.0
    kt[0:32] = tab
    kt[32:64] = 1.0
    kt[64:96] = tab
    kt[96:128] = 1.0
    out["qtab"] = qt.astype(BFNP)
    out["ktab"] = kt.astype(BFNP)
    return out


def _prep_l0attn(inputs):
    Wqkv = np.asarray(inputs["L0_attn_W_qkv"])
    bqkv = np.asarray(inputs["L0_attn_b_qkv"])
    out = {}
    for i, nm in enumerate(("wq_a", "wk_a", "wv_a")):
        out[nm] = _colblocks(np.ascontiguousarray(Wqkv[i * D:(i + 1) * D].T))
    out["wo_a"] = _colblocks(np.ascontiguousarray(np.asarray(inputs["L0_attn_W_out"]).T))
    for i, nm in enumerate(("bq_a", "bk_a", "bv_a")):
        out[nm] = _chunked_vec(bqkv[i * D:(i + 1) * D])
    out["bo_a"] = _chunked_vec(np.asarray(inputs["L0_attn_b_out"]))
    return out


INPUT_KEYS = {
    "l0ffn": ("wgT", "wuT", "wdT"),
    "l0attn": ("wq_a", "wk_a", "wv_a", "wo_a", "bq_a", "bk_a", "bv_a", "bo_a"),
    "moe": ("wr", "brv", "w1n", "w2n", "bge", "b2n"),
    "l1attn": tuple(
        [f"{nm}_l{i}" for i in range(2)
         for nm in ("wq", "wk", "wv", "wo", "w1", "w2", "bq", "bk", "bv",
                    "lng1", "lnb1", "lng2", "lnb2", "b1")] + ["qtab", "ktab"]),
}

FUSED = True
_PROG_CACHE = {}


def _get_program(phase_names):
    key = tuple(phase_names)
    if key not in _PROG_CACHE:
        _PROG_CACHE[key] = build_program(key)
    return _PROG_CACHE[key]


def kernel(**inputs):
    x = np.asarray(inputs["L0_attn_x"])
    shared = {}
    shared.update(_prep_l0ffn(inputs))
    shared.update(_prep_l0attn(inputs))
    shared.update(_prep_moe(inputs))
    shared.update(_prep_l1attn(inputs))

    if FUSED:
        seq = [("l0ffn", "l0attn", "l0ffn", "moe", "l1attn", "moe")]
    else:
        seq = [("l0ffn", "l0attn", "l0ffn"), ("moe",), ("l1attn",), ("moe",)]

    cur = [np.ascontiguousarray(x[c * NB:(c + 1) * NB].reshape(TOKC, D).T)
           for c in range(NCORES)]
    for phases in seq:
        nc = _get_program(phases)
        keys = set()
        for ph in set(phases):
            keys.update(INPUT_KEYS[ph])
        in_maps = []
        for c in range(NCORES):
            m = {"xT": cur[c]}
            for k in keys:
                m[k] = shared[k]
            in_maps.append(m)
        res = run_bass_kernel_spmd(nc, in_maps, core_ids=list(range(NCORES)))
        cur = [np.ascontiguousarray(res.results[c]["yT"]) for c in range(NCORES)]
    outs = [cur[c].T.reshape(NB, SEQ, D) for c in range(NCORES)]
    return np.concatenate(outs, axis=0)
